# version 16
# speedup vs baseline: 1.0638x; 1.0295x over previous
import numpy as np
import ml_dtypes
import concourse.bass as bass
import concourse.mybir as mybir
from concourse.bass_utils import run_bass_kernel_spmd
from contextlib import ExitStack

F32 = mybir.dt.float32
BF16 = mybir.dt.bfloat16
AF = mybir.ActivationFunctionType
ALU = mybir.AluOpType
AX = mybir.AxisListType

DEBUG = False
NSUB = 4
S = 2048
D = 1024
NB = 8
NH = 16
HD = 64
NT = S // 128
NPT = D // 128
NCH = S // 512
EPS_LN = 1e-5
EPS_RMS = 1e-6
LS = float(np.log(np.float64(np.float32(S) + np.float32(1.0))))


def _host_consts():
    c = {}
    c["ident_f"] = np.eye(128, dtype=np.float32)
    c["ident_b"] = np.eye(128, dtype=ml_dtypes.bfloat16)
    c["odiv"] = np.full((128, 128), 1.0 / 1024.0, dtype=ml_dtypes.bfloat16)
    c["ones_col_f"] = np.ones((128, 1), dtype=np.float32)
    c["ones_row_f"] = np.ones((1, 128), dtype=np.float32)
    c["ones_b"] = np.ones((128, 128), dtype=ml_dtypes.bfloat16)
    rs = np.zeros((64, 256), dtype=np.float32)
    for p in range(128):
        j = (p % 64) // 2
        rs[2 * j, p] = 1.0
        rs[2 * j + 1, 128 + p] = -1.0 if p % 2 == 0 else 1.0
    c["rsel"] = rs
    ps = np.zeros((128, 128), dtype=ml_dtypes.bfloat16)
    for p in range(128):
        ps[p ^ 1, p] = 1.0
    c["pswap"] = ps
    es = np.zeros((8, 1024), dtype=np.float32)
    for e in range(8):
        es[e, 128 * e:128 * (e + 1)] = 1.0
    c["esel"] = es
    tri = (np.arange(128)[:, None] <= np.arange(128)[None, :]).astype(np.float32)
    tm = np.zeros((128, 4, 512), dtype=ml_dtypes.bfloat16)
    for v in range(4):
        blk = np.zeros((128, 512), dtype=np.float32)
        blk[:, 128 * v:128 * (v + 1)] = tri
        blk[:, 128 * (v + 1):] = 1.0
        tm[:, v, :] = blk.astype(ml_dtypes.bfloat16)
    c["trimask"] = tm
    hh = np.arange(NH, dtype=np.float64)
    gam = 1.0 - 2.0 ** (-5.0 - hh * (8.0 / NH))
    t = np.arange(S, dtype=np.float64)
    gneg = np.zeros((NPT, 128, S), dtype=np.float32)
    for pt in range(NPT):
        for half in range(2):
            h = 2 * pt + half
            gneg[pt, 64 * half:64 * (half + 1), :] = (
                0.125 * gam[h] ** (-t)).astype(np.float32)
    c["gneg"] = gneg.reshape(NPT * 128, S)
    gp = np.zeros((128, NH * NT), dtype=np.float32)
    for h in range(NH):
        for qt in range(NT):
            gp[:, h * NT + qt] = (gam[h] ** (128.0 * qt + np.arange(128))
                                  ).astype(np.float32)
    c["gpos"] = gp
    gs = np.zeros((128, 2), dtype=np.float32)
    gs[0:64, 0] = 1.0
    gs[64:128, 1] = 1.0
    c["gsel"] = gs
    gr = np.zeros((2, 128), dtype=np.float32)
    gr[0, 0:64] = 1.0
    gr[1, 64:128] = 1.0
    c["grepl"] = gr
    return c


class Ctr:
    def __init__(self, sem):
        self.sem = sem
        self.n = 0

    def inc(self, inst, k=1):
        inst.then_inc(self.sem, k)
        self.n += k


class MK:
    def __init__(self):
        self.nc = bass.Bass("TRN2", target_bir_lowering=False)
        self.hc = _host_consts()
        self.es = ExitStack()
        self._dram()
        self._sbuf()

    def _dram(self):
        nc = self.nc
        EI = dict(kind="ExternalInput")
        self.xall = nc.dram_tensor("xall", [8, S, D], F32, **EI).ap()
        self.xown = nc.dram_tensor("xown", [S, D], F32, **EI).ap()
        self.bselv = nc.dram_tensor("bselv", [1, 8], F32, **EI).ap()
        for nm, shp in (("aWq", [8, 128, 128]), ("aWk", [8, 128, 128]),
                        ("aWv", [8, 128, 128]), ("aWo", [D, D]),
                        ("aA", [D, D]), ("aB", [D, D]), ("aC", [D, D]),
                        ("aLg", [1, D]), ("aLb", [1, D]),
                        ("bWq", [D, D]), ("bWk", [D, D]), ("bWv", [D, D]),
                        ("bWo", [D, D]), ("bGw", [1, D]), ("bGb", [1, D]),
                        ("fAnw", [1, D]), ("fFnw", [1, D]),
                        ("fwq", [D, D]), ("fwk", [D, D]), ("fwv", [D, D]),
                        ("fwo", [D, D]), ("frw", [D, 8]),
                        ("fw1", [8, D, 2048]), ("fw3", [8, D, 2048]),
                        ("fw2", [8, 2048, D]), ("ffc", [S, 64])):
            setattr(self, nm, nc.dram_tensor(nm, shp, F32, **EI).ap())
        self.cst = {}
        for k, v in self.hc.items():
            dt = BF16 if v.dtype == ml_dtypes.bfloat16 else F32
            self.cst[k] = nc.dram_tensor("c_" + k, list(v.shape), dt, **EI).ap()
        self.out = nc.dram_tensor("out", [S, D], F32, kind="ExternalOutput").ap()
        SCR = dict(kind="ExternalOutput" if DEBUG else "Internal")
        self.hx = [nc.dram_tensor(f"hx{i}", [128, NPT, S], F32, **SCR).ap()
                   for i in range(4)]
        self.hgn = nc.dram_tensor("hgn", [128, NPT, S], F32, kind="Internal").ap()
        self.hxp = nc.dram_tensor("hxp", [128, NPT, S], F32, kind="Internal").ap()

    def _sbuf(self):
        nc, es = self.nc, self.es
        sbt = lambda *a: es.enter_context(nc.sbuf_tensor(*a))
        self.a1 = sbt("a1", [128, NPT, S], BF16)
        self.a2 = sbt("a2", [128, NPT, S], BF16)
        self.a3 = sbt("a3", [128, NPT, S], BF16)
        self.a4 = sbt("a4", [128, NT, 1040], BF16)
        self.a5 = sbt("a5", [128, 4, S], BF16)
        self.af = sbt("af", [128, 2, S], F32)
        self.ag = sbt("ag", [128, 2, S], F32)
        self.aw = sbt("aw", [128, NPT, 1024], BF16)
        self.asml = sbt("asml", [128, 192], F32)
        self.atin = sbt("atin", [128, 48], F32)

        def fview(t):
            return t[:, :, :].rearrange("p a b -> p (a b)").bitcast(F32) \
                .rearrange("p (a b) -> p a b", b=S)
        self.a1f = fview(self.a1)
        self.a2f = fview(self.a2)
        self.a3f = fview(self.a3)
        self.a5f = self.a5[:, :, :].rearrange("p a b -> p (a b)").bitcast(F32) \
            .rearrange("p (a b) -> p a b", b=S)
        self.ass = self.a5f[:, 1, :].rearrange("p (a b) -> p a b", b=512)
        self.a4flat = self.a4[:, :, :].rearrange("p a b -> p (a b)")
        self.a4f = self.a4flat[:, 0:16384].bitcast(F32).rearrange(
            "p (a b) -> p a b", b=S)
        self.c = {}
        for k, v in self.hc.items():
            if k == "gneg":
                continue
            dt = BF16 if v.dtype == ml_dtypes.bfloat16 else F32
            self.c[k] = sbt("k_" + k, list(v.shape), dt)
        ps = lambda *a: es.enter_context(nc.psum_tensor(*a))
        self.pA = ps("pA", [128, 512], F32)
        self.pB = ps("pB", [128, 512], F32)
        self.pC = ps("pC", [128, 512], F32)
        self.pD = ps("pD", [128, 512], F32)
        self.pE = ps("pE", [128, 512], F32)
        self.pF = ps("pF", [128, 512], F32)
        self.gram = ps("gram", [128, 1024], F32)
        self.gslot = lambda h: (
            (self.pA, self.pB, self.pC, self.pD, self.pE, self.pF)[h][:, 0:128]
            if h < 6 else self.gram[:, 512 * (h - 6):512 * (h - 6) + 128])
        self.am = self.pF[0:1, 384:392]
        sem = lambda n: Ctr(es.enter_context(nc.semaphore(n)))
        self.ceps = sbt("ceps", [128, 2], F32)
        self.dL = sem("dL")
        self.dX = sem("dX")
        self.dY = sem("dY")
        self.dO = sem("dO")
        self.t0 = sem("t0")
        self.v0 = sem("v0")
        self.s0 = sem("s0")
        self.t1 = sem("t1")
        self.v1 = sem("v1")
        self.s1 = sem("s1")
        self.g1 = sem("g1")
        self.t2 = sem("t2")
        self.v2 = sem("v2")
        self.dp = [sem(f"dp{i}") for i in range(8)]
        self.sw = [sem(f"sw{i}") for i in range(13)]
        self.swbase = 0

    def _load_consts(self):
        nc = self.nc
        with nc.Block() as blk:
            @blk.vector
            def _(v):
                v.memset(self.ceps[:, 0:1], EPS_LN)
                v.memset(self.ceps[:, 1:2], EPS_RMS)
            @blk.sync
            def _(sy):
                for k, sb in self.c.items():
                    idx = tuple([slice(None)] * len(self.hc[k].shape))
                    sy.dma_start(sb[idx], self.cst[k]).then_inc(self.dX.sem, 16)
                    self.dX.n += 16
                sy.wait_ge(self.dX.sem, self.dX.n)

    def prologue(self):
        nc, c, es = self.nc, self.c, self.es
        self.wv_w = self.a4flat[:, 8192:11264].rearrange(
            "p (m h d) -> p m h d", m=3, h=8)
        self.wv_d = self.a4flat[:, 11264:14336].rearrange(
            "p (m h d) -> p m h d", m=3, h=8)
        with nc.Block() as blk:
            @blk.gpsimd
            def _(g):
                for m, ap_w in enumerate((self.aWq, self.aWk, self.aWv)):
                    g.dma_start(self.wv_w[:, m, :, :],
                                ap_w.rearrange("h v d -> v h d")).then_inc(self.dL.sem, 16)
                    self.dL.n += 16
                for m, ap_m in enumerate((self.aA, self.aB, self.aC)):
                    for h in range(NB):
                        sl = slice(128 * h, 128 * (h + 1))
                        g.dma_start(self.wv_d[:, m, h, :], ap_m[sl, sl]) \
                            .then_inc(self.dL.sem, 16)
                        self.dL.n += 16
                g.dma_start(self.atin[0:1, 0:8], self.bselv).then_inc(self.dL.sem, 16)
                self.dL.n += 16
                g.wait_ge(self.dL.sem, self.dL.n)

        msb = self.a4flat[:, 14336:16384].bitcast(F32).rearrange(
            "p (h d) -> p h d", h=8)
        bt = self.t0.n
        with nc.Block() as blk:
            @blk.tensor
            def _(t):
                for h in range(NB):
                    p = (self.pA, self.pB, self.pC, self.pD)[h // 2]
                    sl = slice(128 * (h % 2), 128 * (h % 2) + 128)
                    self.t0.inc(t.matmul(p[:, sl], self.wv_w[:, 0, h, :],
                                         self.wv_w[:, 1, h, :], start=True, stop=True))
            @blk.vector
            def _(v):
                for h in range(NB):
                    v.wait_ge(self.t0.sem, bt + h + 1)
                    p = (self.pA, self.pB, self.pC, self.pD)[h // 2]
                    sl = slice(128 * (h % 2), 128 * (h % 2) + 128)
                    v.tensor_scalar(msb[:, h, :], p[:, sl], 1.0 / float(S),
                                    None, ALU.mult)

        xbat1 = self.a2f[:, :, :].rearrange("p a (c b) -> p (a c) b", b=1024)
        xbat2 = self.a3f[:, :, :].rearrange("p a (c b) -> p (a c) b", b=1024)
        xb = lambda tt: xbat1[:, tt, :] if tt < 8 else xbat2[:, tt - 8, :]
        st1 = self.asml[:, 0:16]
        st2 = self.asml[:, 16:32]
        sta = self.asml[:, 32:48]
        stb = self.asml[:, 48:64]
        prcol = self.asml[:, 64:72]
        sqscr = self.a5f[:, 0, 0:1024]
        ytile = self.a5[:, 2, :].rearrange("p (a b) -> p a b", b=1024)

        def batch_stats_block():
            bv1, bs1 = self.v1.n, self.s1.n
            with nc.Block() as blk:
                @blk.vector
                def _(v):
                    v.wait_ge(self.v1.sem, self.v1.n)
                    self.v1.inc(v.tensor_scalar(st1[:, :], st1[:, :],
                                                1.0 / 1024.0, None, ALU.mult))
                    v.wait_ge(self.v1.sem, self.v1.n)
                    self.v1.inc(v.tensor_scalar(st2[:, :], st2[:, :],
                                                1.0 / 1024.0, None, ALU.mult))
                    v.wait_ge(self.v1.sem, self.v1.n)
                    self.v1.inc(v.tensor_mul(sta[:, :], st1[:, :], st1[:, :]))
                    v.wait_ge(self.v1.sem, self.v1.n)
                    self.v1.inc(v.tensor_sub(sta[:, :], st2[:, :], sta[:, :]))
                    v.wait_ge(self.s1.sem, bs1 + 2)
                    self.v1.inc(v.scalar_tensor_tensor(
                        stb[:, :], st1[:, :], -1.0, sta[:, :],
                        ALU.mult, ALU.mult))
                @blk.scalar
                def _(s):
                    s.wait_ge(self.v1.sem, bv1 + 4)
                    self.s1.inc(s.activation(sta[:, :], sta[:, :], AF.Ln,
                                             bias=self.ceps[:, 0:1]))
                    s.wait_ge(self.s1.sem, self.s1.n)
                    self.s1.inc(s.activation(sta[:, :], sta[:, :], AF.Exp,
                                             scale=-0.5))

        ycons = [None]

        def load_and_rowstats(src_ap):
            ldv = []
            bprev = ycons[0]
            with nc.Block() as blk:
                @blk.sync
                def _(sy):
                    for tt in range(NT):
                        dp = self.dp[2 + tt % 4]
                        if tt >= 4:
                            sy.wait_ge(dp.sem, ldv[tt - 4])
                        if bprev is not None:
                            sy.wait_ge(self.s0.sem, bprev + tt + 1)
                        sy.dma_start(xb(tt), src_ap(tt)).then_inc(dp.sem, 16)
                        dp.n += 16
                        ldv.append(dp.n)
                @blk.vector
                def _(v):
                    for tt in range(NT):
                        v.wait_ge(self.dp[2 + tt % 4].sem, ldv[tt])
                        v.reduce_sum(st1[:, tt:tt + 1], xb(tt), axis=AX.X)
                @blk.scalar
                def _(s):
                    for tt in range(NT):
                        s.wait_ge(self.dp[2 + tt % 4].sem, ldv[tt])
                        s.wait_ge(self.s1.sem, self.s1.n)
                        self.s1.inc(s.activation(sqscr, xb(tt), AF.Square,
                                                 accum_out=st2[:, tt:tt + 1]))


        V0B, S1B, V1B, S0B, T0B = (self.v0.n, self.s1.n, self.v1.n,
                                   self.s0.n, self.t0.n)
        ldv = []
        with nc.Block() as blk:
            @blk.sync
            def _(sy):
                for j in range(8):
                    for tt in range(NT):
                        idx = 16 * j + tt
                        dp = self.dp[tt % 8]
                        if idx >= 8:
                            sy.wait_ge(dp.sem, ldv[idx - 8])
                        if j >= 1:
                            sy.wait_ge(self.s0.sem, S0B + 16 * (j - 1) + tt + 1)
                            sy.wait_ge(self.v0.sem, V0B + 16 * (j - 1) + tt + 1)
                        sy.dma_start(xb(tt),
                                     self.xall[j, 128 * tt:128 * (tt + 1), :]) \
                            .then_inc(dp.sem, 16)
                        dp.n += 16
                        ldv.append(dp.n)
            @blk.vector
            def _(v):
                for j in range(8):
                    for tt in range(NT):
                        v.wait_ge(self.dp[tt % 8].sem, ldv[16 * j + tt])
                        self.v0.inc(v.reduce_sum(st1[:, tt:tt + 1], xb(tt),
                                                 axis=AX.X))
                    def vch(inst):
                        self.v1.inc(inst)
                        v.wait_ge(self.v1.sem, self.v1.n)
                    v.wait_ge(self.v0.sem, V0B + 16 * j + 16)
                    vch(v.tensor_scalar(st1[:, :], st1[:, :], 1.0 / 1024.0,
                                        None, ALU.mult))
                    v.wait_ge(self.s1.sem, S1B + 18 * j + 16)
                    vch(v.tensor_scalar(st2[:, :], st2[:, :], 1.0 / 1024.0,
                                        None, ALU.mult))
                    vch(v.tensor_mul(sta[:, :], st1[:, :], st1[:, :]))
                    vch(v.tensor_sub(sta[:, :], st2[:, :], sta[:, :]))
                    v.wait_ge(self.s1.sem, S1B + 18 * j + 18)
                    vch(v.scalar_tensor_tensor(stb[:, :], st1[:, :], -1.0,
                                               sta[:, :], ALU.mult, ALU.mult))
                    v.wait_ge(self.t0.sem, T0B + 16 * j + 16)
                    for h in range(NB):
                        vch(v.tensor_mul(self.ass[:, h % 2, 0:128],
                                         self.gslot(h), msb[:, h, :]))
                        vch(v.reduce_sum(prcol[:, h:h + 1],
                                         self.ass[:, h % 2, 0:128], axis=AX.X))
            @blk.scalar
            def _(s):
                for j in range(8):
                    for tt in range(NT):
                        s.wait_ge(self.dp[tt % 8].sem, ldv[16 * j + tt])
                        s.wait_ge(self.s1.sem, self.s1.n)
                        self.s1.inc(s.activation(sqscr, xb(tt), AF.Square,
                                                 accum_out=st2[:, tt:tt + 1]))
                    s.wait_ge(self.v1.sem, V1B + 21 * j + 4)
                    s.wait_ge(self.s1.sem, self.s1.n)
                    self.s1.inc(s.activation(sta[:, :], sta[:, :], AF.Ln,
                                             bias=self.ceps[:, 0:1]))
                    s.wait_ge(self.s1.sem, self.s1.n)
                    self.s1.inc(s.activation(sta[:, :], sta[:, :], AF.Exp,
                                             scale=-0.5))
                    s.wait_ge(self.v1.sem, V1B + 21 * j + 5)
                    for tt in range(NT):
                        if tt >= 2:
                            s.wait_ge(self.t0.sem, T0B + 16 * j + tt - 1)
                        elif j >= 1:
                            s.wait_ge(self.t0.sem, T0B + 16 * j)
                        self.s0.inc(s.activation(ytile[:, tt % 2, :], xb(tt),
                                                 AF.Identity,
                                                 bias=stb[:, tt:tt + 1],
                                                 scale=sta[:, tt:tt + 1]))
            @blk.tensor
            def _(t):
                for j in range(8):
                    if j >= 1:
                        t.wait_ge(self.v1.sem, V1B + 21 * j)
                    for tt in range(NT):
                        t.wait_ge(self.s0.sem, S0B + 16 * j + tt + 1)
                        last = None
                        for h in range(NB):
                            last = t.matmul(
                                self.gslot(h),
                                ytile[:, tt % 2, 128 * h:128 * (h + 1)],
                                ytile[:, tt % 2, 128 * h:128 * (h + 1)],
                                start=(tt == 0), stop=(tt == NT - 1))
                        self.t0.inc(last)
                    t.wait_ge(self.v1.sem, V1B + 21 * (j + 1))
                    for h in range(NB):
                        t.matmul(self.am[0:1, j:j + 1], c["ones_col_f"][:, 0:1],
                                 prcol[:, h:h + 1], start=(h == 0),
                                 stop=(h == NB - 1))
        ycons[0] = None

        A = self.atin

        def vstep(fn):
            with nc.Block() as blk:
                @blk.vector
                def _(v):
                    fn(v)
        with nc.Block() as blk:
            @blk.scalar
            def _(s):
                s.activation(A[0:1, 8:16], self.am[0:1, 0:8], AF.Exp, scale=-10.0 / LS)
        vstep(lambda v: v.tensor_scalar(A[0:1, 8:16], A[0:1, 8:16], 1.0, None, ALU.add))
        vstep(lambda v: v.reciprocal(A[0:1, 16:24], A[0:1, 8:16]))
        vstep(lambda v: (v.reduce_sum(A[0:1, 24:25], A[0:1, 16:24], axis=AX.X),
                         v.tensor_mul(A[0:1, 32:40], A[0:1, 16:24], A[0:1, 0:8])))
        vstep(lambda v: (v.tensor_scalar(A[0:1, 25:26], A[0:1, 24:25], 0.125,
                                         None, ALU.mult),
                         v.reduce_sum(A[0:1, 26:27], A[0:1, 32:40], axis=AX.X)))
        vstep(lambda v: (v.tensor_scalar(A[0:1, 27:28], A[0:1, 25:26], 0.99,
                                         None, ALU.is_gt),
                         v.tensor_scalar(A[0:1, 28:29], A[0:1, 25:26], 0.01,
                                         None, ALU.is_lt)))
        vstep(lambda v: (v.tensor_scalar(A[0:1, 29:30], A[0:1, 28:29], -1.0, 1.0,
                                         ALU.mult, ALU.add),
                         v.tensor_scalar(A[0:1, 31:32], A[0:1, 27:28], -1.0, 1.0,
                                         ALU.mult, ALU.add)))
        vstep(lambda v: v.tensor_mul(A[0:1, 30:31], A[0:1, 29:30], A[0:1, 26:27]))
        vstep(lambda v: v.tensor_mul(A[0:1, 30:31], A[0:1, 31:32], A[0:1, 30:31]))
        vstep(lambda v: v.tensor_add(A[0:1, 40:41], A[0:1, 30:31], A[0:1, 27:28]))
        vstep(lambda v: (v.tensor_scalar(A[0:1, 41:42], A[0:1, 40:41], -1.0, 1.0,
                                         ALU.mult, ALU.add),
                         v.tensor_scalar(A[0:1, 42:43], A[0:1, 40:41], 0.1,
                                         None, ALU.mult)))
        with nc.Block() as blk:
            @blk.tensor
            def _(t):
                t.matmul(self.pA[:, 0:1], c["ones_row_f"][0:1, :], A[0:1, 41:42],
                         start=True, stop=True)
                t.matmul(self.pA[:, 1:2], c["ones_row_f"][0:1, :], A[0:1, 42:43],
                         start=True, stop=True)
        vstep(lambda v: v.tensor_copy(self.asml[:, 128:130], self.pA[:, 0:2]))
        self.w1m = self.asml[:, 128:129]
        self.w01 = self.asml[:, 129:130]

        load_and_rowstats(lambda tt: self.xown[128 * tt:128 * (tt + 1), :])
        batch_stats_block()
        yown = self.a1[:, :, :].rearrange("p a (c b) -> p (a c) b", b=1024)
        with nc.Block() as blk:
            @blk.scalar
            def _(s):
                for tt in range(NT):
                    s.activation(yown[:, tt, :], xb(tt), AF.Identity,
                                 bias=stb[:, tt:tt + 1], scale=sta[:, tt:tt + 1])
        xrow = self.ag[:, :, :]
        bt, bv = self.t0.n, self.v0.n
        stv = []
        with nc.Block() as blk:
            @blk.sync
            def _(sy):
                for pt in range(NPT):
                    dp = self.dp[pt % 2]
                    if pt >= 2:
                        sy.wait_ge(dp.sem, stv[pt - 2])
                    sy.wait_ge(self.v0.sem, bv + (pt + 1) * NT)
                    sy.dma_start(self.hx[0][:, pt, :], xrow[:, pt % 2, :]) \
                        .then_inc(dp.sem, 16)
                    dp.n += 16
                    stv.append(dp.n)
            @blk.tensor
            def _(t):
                for pt in range(NPT):
                    for tt in range(NT):
                        k = pt * NT + tt
                        p = (self.pA, self.pB, self.pC, self.pD)[k % 4]
                        if k >= 4:
                            t.wait_ge(self.v0.sem, bv + k - 3)
                        self.t0.inc(t.transpose(p[:, 0:128],
                                                xb(tt)[:, 128 * pt:128 * (pt + 1)],
                                                c["ident_f"][:, :]))
            @blk.vector
            def _(v):
                for pt in range(NPT):
                    for tt in range(NT):
                        k = pt * NT + tt
                        v.wait_ge(self.t0.sem, bt + k + 1)
                        if tt == 0 and pt >= 2:
                            v.wait_ge(self.dp[pt % 2].sem, stv[pt - 2])
                        p = (self.pA, self.pB, self.pC, self.pD)[k % 4]
                        self.v0.inc(v.tensor_copy(
                            xrow[:, pt % 2, 128 * tt:128 * (tt + 1)], p[:, 0:128]))
        self.hx0_done = (self.dp[0].n, self.dp[1].n)
        self.yhat = self.a2
        pAb = self.pA[:, :].bitcast(BF16)
        pBb = self.pB[:, :].bitcast(BF16)
        pCb = self.pC[:, :].bitcast(BF16)
        pDb = self.pD[:, :].bitcast(BF16)
        bt, bv = self.t0.n, self.v0.n
        with nc.Block() as blk:
            @blk.tensor
            def _(t):
                for tt in range(NT):
                    for pt in range(NPT):
                        k = tt * NPT + pt
                        pb = (pAb, pBb, pCb, pDb)[k % 4]
                        if k >= 4:
                            t.wait_ge(self.v0.sem, bv + k - 3)
                        self.t0.inc(t.transpose(pb[:, 0:128],
                                                yown[:, tt, 128 * pt:128 * (pt + 1)],
                                                c["ident_b"][:, :]))
            @blk.vector
            def _(v):
                for tt in range(NT):
                    for pt in range(NPT):
                        k = tt * NPT + pt
                        v.wait_ge(self.t0.sem, bt + k + 1)
                        pb = (pAb, pBb, pCb, pDb)[k % 4]
                        self.v0.inc(v.tensor_copy(
                            self.yhat[:, pt, 128 * tt:128 * (tt + 1)], pb[:, 0:128]))

    def attn_a(self):
        nc, c = self.nc, self.c
        awv = self.aw[:, :, :].rearrange("p a b -> p (a b)")
        Gm = lambda m, h: awv[:, (m * 8 + h) * 128:(m * 8 + h) * 128 + 128]
        Gp = lambda h: awv[:, (24 + h) * 128:(24 + h) * 128 + 128]
        Rm = lambda m, h: awv[:, (32 + m * 8 + h) * 128:(32 + m * 8 + h) * 128 + 128]
        bt, bv = self.t0.n, self.v0.n
        with nc.Block() as blk:
            @blk.tensor
            def _(t):
                for h in range(NB):
                    if h >= 1:
                        t.wait_ge(self.v0.sem, bv + 7 * h)
                    last = None
                    for m in range(3):
                        p = (self.pA, self.pB, self.pC)[m]
                        t.matmul(p[:, 0:128], self.wv_w[:, m, h, :],
                                 self.wv_d[:, m, h, :], start=True, stop=True)
                        t.matmul(self.pD[:, 0:128], self.wv_w[:, m, h, :],
                                 self.wv_d[:, m, h, :], start=(m == 0),
                                 stop=(m == 2))
                        pe = self.pE if m < 2 else self.pF
                        last = t.matmul(
                            pe[:, 128 * (m % 2):128 * (m % 2) + 128],
                            self.wv_w[:, m, h, :], c["ones_b"][:, :],
                            start=True, stop=True)
                    self.t0.inc(last)
            @blk.vector
            def _(v):
                for h in range(NB):
                    v.wait_ge(self.t0.sem, bt + h + 1)
                    for m in range(3):
                        p = (self.pA, self.pB, self.pC)[m]
                        self.v0.inc(v.tensor_copy(Gm(m, h), p[:, 0:128]))
                        pe = self.pE if m < 2 else self.pF
                        self.v0.inc(v.tensor_scalar(
                            Rm(m, h), pe[:, 128 * (m % 2):128 * (m % 2) + 128],
                            self.w01[:, 0:1], None, ALU.mult))
                    self.v0.inc(v.scalar_tensor_tensor(
                        Gp(h), self.pD[:, 0:128], self.w1m[:, 0:1],
                        c["ident_f"][:, :], ALU.mult, ALU.add))

        ssm = lambda pt: (self.a3f[:, pt, :] if pt < 4 else self.a4f[:, pt - 4, :])
        yh = self.yhat
        ymt = self.a5[:, 0:3, :].rearrange("p a (cc b) -> p (a cc) b", b=512)
        NK = NB * NCH
        bt0, bt1, bv1, bs0 = self.t0.n, self.t1.n, self.v1.n, self.s0.n
        with nc.Block() as blk:
            @blk.tensor
            def _(t):
                for h in range(NB):
                    for ch in range(NCH):
                        k = h * NCH + ch
                        sl = slice(512 * ch, 512 * (ch + 1))
                        if k >= 1:
                            t.wait_ge(self.v1.sem, bv1 + k)
                        last = None
                        for m in range(3):
                            p = (self.pA, self.pB, self.pC)[m]
                            last = t.matmul(p[:, :], Rm(m, h), yh[:, h, sl],
                                            start=True, stop=True)
                        self.t0.inc(last)
                        t.wait_ge(self.v1.sem, bv1 + (k + 1))
                        if k >= 2:
                            t.wait_ge(self.s0.sem, bs0 + k - 1)
                        po = self.pD if k % 2 == 0 else self.pE
                        t.matmul(po[:, :], Gp(h), yh[:, h, sl], start=True, stop=False)
                        for m in range(3):
                            mm = t.matmul(po[:, :], Gm(m, h),
                                          ymt[:, m * 4 + (k % 4), :],
                                          start=False, stop=(m == 2))
                            if m == 2:
                                self.t1.inc(mm)
            @blk.vector
            def _(v):
                for h in range(NB):
                    for ch in range(NCH):
                        k = h * NCH + ch
                        sl = slice(512 * ch, 512 * (ch + 1))
                        v.wait_ge(self.t0.sem, bt0 + k + 1)
                        if k >= 4:
                            v.wait_ge(self.t1.sem, bt1 + k - 3)
                        last = None
                        for m in range(3):
                            p = (self.pA, self.pB, self.pC)[m]
                            last = v.tensor_mul(ymt[:, m * 4 + (k % 4), :],
                                                p[:, :], yh[:, h, sl])
                        self.v1.inc(last)
            @blk.scalar
            def _(s):
                for h in range(NB):
                    for ch in range(NCH):
                        k = h * NCH + ch
                        sl = slice(512 * ch, 512 * (ch + 1))
                        s.wait_ge(self.t1.sem, bt1 + k + 1)
                        po = self.pD if k % 2 == 0 else self.pE
                        self.s0.inc(s.copy(ssm(h)[:, sl], po[:, :]))

        wo = self.aw
        with nc.Block() as blk:
            @blk.gpsimd
            def _(g):
                g.dma_start(wo[:, :, :],
                            self.aWo.rearrange("(a p) o -> p a o", p=128)) \
                    .then_inc(self.dL.sem, 16)
                self.dL.n += 16

        self._ln_gamma_beta(ssm)
        wotot = self.dL.n

        normed = self.a1
        xch = self.ag[:, 0, :].rearrange("p (a b) -> p a b", b=512)
        och = self.ag[:, 1, :].rearrange("p (a b) -> p a b", b=512)
        bt, bv = self.t0.n, self.v0.n
        ldv = []
        stv = []
        with nc.Block() as blk:
            @blk.sync
            def _(sy):
                sy.wait_ge(self.dp[0].sem, self.hx0_done[0])
                sy.wait_ge(self.dp[1].sem, self.hx0_done[1])
                for k in range(32):
                    ot, ch = divmod(k, NCH)
                    dp = self.dp[2 + k % 4]
                    if k >= 4:
                        sy.wait_ge(dp.sem, ldv[k - 4])
                        sy.wait_ge(self.v0.sem, bv + k - 3)
                    sy.dma_start(xch[:, k % 4, :],
                                 self.hx[0][:, ot, 512 * ch:512 * (ch + 1)]) \
                        .then_inc(dp.sem, 16)
                    dp.n += 16
                    ldv.append(dp.n)
            @blk.scalar
            def _(sy):
                for k in range(32):
                    ot, ch = divmod(k, NCH)
                    dp = self.dp[6 + k % 2]
                    sy.wait_ge(self.v0.sem, bv + k + 1)
                    sy.dma_start(self.hx[1][:, ot, 512 * ch:512 * (ch + 1)],
                                 och[:, k % 2, :]).then_inc(dp.sem, 16)
                    dp.n += 16
                    stv.append(dp.n)
            @blk.tensor
            def _(t):
                t.wait_ge(self.dL.sem, wotot)
                for k in range(32):
                    ot, ch = divmod(k, NCH)
                    po = self.pA if k % 2 == 0 else self.pB
                    if k >= 2:
                        t.wait_ge(self.v0.sem, bv + k - 1)
                    last = None
                    for kt in range(NPT):
                        last = t.matmul(po[:, :], wo[:, kt, 128 * ot:128 * (ot + 1)],
                                        normed[:, kt, 512 * ch:512 * (ch + 1)],
                                        start=(kt == 0), stop=(kt == NPT - 1))
                    self.t0.inc(last)
            @blk.vector
            def _(v):
                for k in range(32):
                    po = self.pA if k % 2 == 0 else self.pB
                    v.wait_ge(self.t0.sem, bt + k + 1)
                    v.wait_ge(self.dp[2 + k % 4].sem, ldv[k])
                    if k >= 2:
                        v.wait_ge(self.dp[6 + k % 2].sem, stv[k - 2])
                    self.v0.inc(v.tensor_add(och[:, k % 2, :], xch[:, k % 4, :],
                                             po[:, :]))
        self.hx1_done = [(self.dp[6], self.dp[6].n), (self.dp[7], self.dp[7].n)]

    def _ln_gamma_beta(self, ssm):
        nc, c = self.nc, self.c
        gb = self.asml[:, 130:146]
        with nc.Block() as blk:
            @blk.gpsimd
            def _(g):
                g.dma_start(gb[:, 0:8],
                            self.aLg.rearrange("o (a p) -> p (o a)", p=128),
                            allow_slow_non_contiguous=True).then_inc(self.dL.sem, 16)
                g.dma_start(gb[:, 8:16],
                            self.aLb.rearrange("o (a p) -> p (o a)", p=128),
                            allow_slow_non_contiguous=True).then_inc(self.dL.sem, 16)
                self.dL.n += 32
                g.wait_ge(self.dL.sem, self.dL.n)
        xbfc = self.a5[:, 0:2, :].rearrange("p a (cc b) -> p (a cc) b", b=512)
        sqc = self.a5[:, 2:4, :].rearrange("p a (cc b) -> p (a cc) b", b=512)
        mus = self.af[:, 0, :]
        rst = self.af[:, 1, :]
        for ch in range(NCH):
            sl = slice(512 * ch, 512 * (ch + 1))
            with nc.Block() as blk:
                @blk.vector
                def _(v):
                    for pt in range(NPT):
                        v.tensor_copy(xbfc[:, pt, :], ssm(pt)[:, sl])
                @blk.scalar
                def _(s):
                    for pt in range(NPT):
                        s.activation(sqc[:, pt, :], ssm(pt)[:, sl], AF.Square)
            with nc.Block() as blk:
                @blk.tensor
                def _(t):
                    for pt in range(NPT):
                        t.matmul(self.pA[:, :], c["odiv"][:, :], xbfc[:, pt, :],
                                 start=(pt == 0), stop=(pt == NPT - 1))
                    for pt in range(NPT):
                        t.matmul(self.pB[:, :], c["odiv"][:, :], sqc[:, pt, :],
                                 start=(pt == 0), stop=(pt == NPT - 1))
            with nc.Block() as blk:
                @blk.vector
                def _(v):
                    v.tensor_copy(mus[:, sl], self.pA[:, :])
            with nc.Block() as blk:
                @blk.vector
                def _(v):
                    v.tensor_mul(self.ass[:, 0, :], mus[:, sl], mus[:, sl])
            with nc.Block() as blk:
                @blk.vector
                def _(v):
                    v.tensor_sub(rst[:, sl], self.pB[:, :], self.ass[:, 0, :])
        with nc.Block() as blk:
            @blk.scalar
            def _(s):
                s.activation(rst[:, :], rst[:, :], AF.Ln, bias=self.ceps[:, 0:1])
        with nc.Block() as blk:
            @blk.scalar
            def _(s):
                s.activation(rst[:, :], rst[:, :], AF.Exp, scale=-0.5)
        with nc.Block() as blk:
            @blk.vector
            def _(v):
                for pt in range(NPT):
                    for ch in range(NCH):
                        sl = slice(512 * ch, 512 * (ch + 1))
                        v.wait_ge(self.v1.sem, self.v1.n)
                        self.v1.inc(v.tensor_sub(self.ass[:, ch % 2, :],
                                                 ssm(pt)[:, sl], mus[:, sl]))
                        v.wait_ge(self.v1.sem, self.v1.n)
                        self.v1.inc(v.tensor_mul(self.ass[:, ch % 2, :],
                                                 self.ass[:, ch % 2, :], rst[:, sl]))
                        v.wait_ge(self.v1.sem, self.v1.n)
                        self.v1.inc(v.tensor_scalar(
                            self.a1[:, pt, sl], self.ass[:, ch % 2, :],
                            gb[:, pt:pt + 1], gb[:, 8 + pt:9 + pt],
                            ALU.mult, ALU.add))

    def _ln_stats_stream(self, src, done):
        nc, c = self.nc, self.c
        xc = self.ag[:, :, :].rearrange("p a (cc b) -> p (a cc) b", b=512)
        xbfc = self.a5[:, 0:2, :].rearrange("p a (cc b) -> p (a cc) b", b=512)
        mus = self.af[:, 0, :]
        rst = self.af[:, 1, :]
        for ch in range(NCH):
            sl = slice(512 * ch, 512 * (ch + 1))
            with nc.Block() as blk:
                @blk.sync
                def _(sy):
                    if ch == 0 and done:
                        for ctr, val in done:
                            sy.wait_ge(ctr.sem, val)
                    dp = self.dp[0]
                    sy.dma_start(xc[:, 0:8, :], src[:, :, sl]) \
                        .then_inc(dp.sem, 16)
                    dp.n += 16
                tot = self.dp[0].n
                @blk.vector
                def _(v):
                    v.wait_ge(self.dp[0].sem, tot)
                    for pt in range(NPT):
                        v.tensor_copy(xbfc[:, pt, :], xc[:, pt, :])
            with nc.Block() as blk:
                @blk.tensor
                def _(t):
                    for pt in range(NPT):
                        t.matmul(self.pA[:, :], c["odiv"][:, :], xbfc[:, pt, :],
                                 start=(pt == 0), stop=(pt == NPT - 1))
            with nc.Block() as blk:
                @blk.scalar
                def _(s):
                    for pt in range(NPT):
                        s.wait_ge(self.s1.sem, self.s1.n)
                        self.s1.inc(s.activation(xbfc[:, pt, :], xc[:, pt, :],
                                                 AF.Square))
            with nc.Block() as blk:
                @blk.tensor
                def _(t):
                    for pt in range(NPT):
                        t.matmul(self.pB[:, :], c["odiv"][:, :], xbfc[:, pt, :],
                                 start=(pt == 0), stop=(pt == NPT - 1))
            with nc.Block() as blk:
                @blk.vector
                def _(v):
                    v.tensor_copy(mus[:, sl], self.pA[:, :])
            with nc.Block() as blk:
                @blk.vector
                def _(v):
                    v.tensor_mul(self.ass[:, 0, :], mus[:, sl], mus[:, sl])
            with nc.Block() as blk:
                @blk.vector
                def _(v):
                    v.tensor_sub(rst[:, sl], self.pB[:, :], self.ass[:, 0, :])
        with nc.Block() as blk:
            @blk.scalar
            def _(s):
                s.activation(rst[:, :], rst[:, :], AF.Ln, bias=self.ceps[:, 0:1])
        with nc.Block() as blk:
            @blk.scalar
            def _(s):
                s.activation(rst[:, :], rst[:, :], AF.Exp, scale=-0.5)

    def _make_z_rms(self, src, w_dram, dst):
        nc, c = self.nc, self.c
        wcol = self.asml[:, 146:154]
        with nc.Block() as blk:
            @blk.gpsimd
            def _(g):
                g.dma_start(wcol[:, 0:8],
                            w_dram.rearrange("o (a p) -> p (o a)", p=128),
                            allow_slow_non_contiguous=True).then_inc(self.dL.sem, 16)
                self.dL.n += 16
                g.wait_ge(self.dL.sem, self.dL.n)
        xc = self.ag[:, :, :].rearrange("p a (cc b) -> p (a cc) b", b=512)
        a5s = self.a5[:, :, :].rearrange("p a (cc b) -> p (a cc) b", b=512)
        y2bf = lambda pt: a5s[:, pt, :]
        sqs = lambda r: a5s[:, 14 + r, :]
        rstd2 = self.a5f[:, 1, 1024:1536]
        mus = self.af[:, 0, :]
        rst = self.af[:, 1, :]
        for ch in range(NCH):
            sl = slice(512 * ch, 512 * (ch + 1))
            with nc.Block() as blk:
                @blk.sync
                def _(sy):
                    dp = self.dp[0]
                    sy.dma_start(xc[:, 0:8, :], src[:, :, sl]) \
                        .then_inc(dp.sem, 16)
                    dp.n += 16
                tot = self.dp[0].n
                @blk.vector
                def _(v):
                    v.wait_ge(self.dp[0].sem, tot)
                    for pt in range(NPT):
                        v.wait_ge(self.v1.sem, self.v1.n)
                        self.v1.inc(v.tensor_sub(y2bf(pt), xc[:, pt, :],
                                                 mus[:, sl]))
                        v.wait_ge(self.v1.sem, self.v1.n)
                        self.v1.inc(v.tensor_mul(y2bf(pt), y2bf(pt), rst[:, sl]))
            bs1, bt0 = self.s1.n, self.t0.n
            with nc.Block() as blk:
                @blk.scalar
                def _(s):
                    for pt in range(NPT):
                        if pt >= 2:
                            s.wait_ge(self.t0.sem, bt0 + pt - 1)
                        self.s1.inc(s.activation(sqs(pt % 2), y2bf(pt), AF.Square))
                @blk.tensor
                def _(t):
                    for pt in range(NPT):
                        t.wait_ge(self.s1.sem, bs1 + pt + 1)
                        self.t0.inc(t.matmul(self.pA[:, :], c["odiv"][:, :],
                                             sqs(pt % 2), start=(pt == 0),
                                             stop=(pt == NPT - 1)))
            with nc.Block() as blk:
                @blk.scalar
                def _(s):
                    s.activation(rstd2, self.pA[:, :], AF.Ln,
                                 bias=self.ceps[:, 1:2])
            with nc.Block() as blk:
                @blk.scalar
                def _(s):
                    s.activation(rstd2, rstd2, AF.Exp, scale=-0.5)
            with nc.Block() as blk:
                @blk.vector
                def _(v):
                    for pt in range(NPT):
                        v.wait_ge(self.v1.sem, self.v1.n)
                        self.v1.inc(v.tensor_mul(y2bf(pt), y2bf(pt), rstd2))
                        v.wait_ge(self.v1.sem, self.v1.n)
                        self.v1.inc(v.tensor_scalar(dst[:, pt, sl], y2bf(pt),
                                                    wcol[:, pt:pt + 1], None,
                                                    ALU.mult))

    def _front_fused(self, src, done, w_dram, dst, plain=False):
        nc, c = self.nc, self.c
        xcA = self.ag[:, :, :].rearrange("p a (cc b) -> p (a cc) b", b=512)
        a2f = self.a2[:, :, :].rearrange("p a b -> p (a b)").bitcast(F32) \
            .rearrange("p (a b) -> p a b", b=512)
        xcB = a2f[:, 0:8, :]
        xbuf = lambda ch: xcA if ch % 2 == 0 else xcB
        a5s = self.a5[:, :, :].rearrange("p a (cc b) -> p (a cc) b", b=512)
        xb = lambda pt: a5s[:, pt, :]
        y2s = lambda pt: a5s[:, 8 + pt, :]
        mucf = lambda ch: self.af[:, 0, 512 * ch:512 * (ch + 1)]
        rstcf = lambda ch: self.af[:, 1, 512 * ch:512 * (ch + 1)]
        mu2 = a2f[:, 8, :]
        rstd2 = a2f[:, 9, :]
        wcol = self.asml[:, 146:154]
        VS, SS, TS = (27, 10, 2) if plain else (43, 20, 3)
        V1B, S1B, T0B = self.v1.n, self.s1.n, self.t0.n
        v1 = lambda ch, off: V1B + VS * ch + off
        s1 = lambda ch, off: S1B + SS * ch + off
        t0 = lambda ch, off: T0B + TS * ch + off
        ldv = []
        if not plain:
            self.dL.n += 16
        wctot = self.dL.n

        def vop(v, inst):
            self.v1.inc(inst)
            v.wait_ge(self.v1.sem, self.v1.n)

        def sop(s, inst):
            self.s1.inc(inst)
            s.wait_ge(self.s1.sem, self.s1.n)
        with nc.Block() as blk:
            if not plain:
                @blk.gpsimd
                def _(g):
                    g.dma_start(wcol[:, 0:8],
                                w_dram.rearrange("o (a p) -> p (o a)", p=128),
                                allow_slow_non_contiguous=True) \
                        .then_inc(self.dL.sem, 16)
            @blk.sync
            def _(sy):
                for ch in range(NCH):
                    sl = slice(512 * ch, 512 * (ch + 1))
                    dp = self.dp[ch % 2]
                    if ch == 0 and done:
                        for ctr, val in done:
                            sy.wait_ge(ctr.sem, val)
                    if ch >= 2:
                        sy.wait_ge(dp.sem, ldv[ch - 2])
                        sy.wait_ge(self.v1.sem, v1(ch - 2, 26))
                    sy.dma_start(xbuf(ch)[:, 0:8, :], src[:, :, sl]) \
                        .then_inc(dp.sem, 16)
                    dp.n += 16
                    ldv.append(dp.n)
            @blk.vector
            def _(v):
                for ch in range(NCH):
                    sl = slice(512 * ch, 512 * (ch + 1))
                    xc = xbuf(ch)
                    v.wait_ge(self.dp[ch % 2].sem, ldv[ch])
                    if ch >= 1:
                        v.wait_ge(self.t0.sem, t0(ch - 1, TS))
                    for pt in range(NPT):
                        self.v1.inc(v.tensor_copy(xb(pt), xc[:, pt, :]))
                    v.wait_ge(self.t0.sem, t0(ch, 1))
                    vop(v, v.tensor_copy(mucf(ch), self.pA[:, :]))
                    vop(v, v.tensor_mul(mu2, mucf(ch), mucf(ch)))
                    v.wait_ge(self.t0.sem, t0(ch, 2))
                    vop(v, v.tensor_sub(rstcf(ch), self.pB[:, :], mu2))
                    v.wait_ge(self.s1.sem, s1(ch, 10))
                    if plain:
                        for pt in range(NPT):
                            vop(v, v.tensor_sub(y2s(pt), xc[:, pt, :],
                                                mucf(ch)))
                            vop(v, v.tensor_mul(dst[:, pt, sl], y2s(pt),
                                                rstcf(ch)))
                        continue
                    for pt in range(NPT):
                        vop(v, v.tensor_sub(y2s(pt), xc[:, pt, :], mucf(ch)))
                        vop(v, v.tensor_mul(y2s(pt), y2s(pt), rstcf(ch)))
                    v.wait_ge(self.s1.sem, s1(ch, 20))
                    if ch == 0:
                        v.wait_ge(self.dL.sem, wctot)
                    for pt in range(NPT):
                        vop(v, v.tensor_mul(y2s(pt), y2s(pt), rstd2))
                        vop(v, v.tensor_scalar(dst[:, pt, sl], y2s(pt),
                                               wcol[:, pt:pt + 1], None,
                                               ALU.mult))
            @blk.tensor
            def _(t):
                for ch in range(NCH):
                    t.wait_ge(self.v1.sem, v1(ch, 8))
                    for pt in range(NPT):
                        last = t.matmul(self.pA[:, :], c["odiv"][:, :], xb(pt),
                                        start=(pt == 0), stop=(pt == NPT - 1))
                    self.t0.inc(last)
                    t.wait_ge(self.s1.sem, s1(ch, 8))
                    for pt in range(NPT):
                        last = t.matmul(self.pB[:, :], c["odiv"][:, :], xb(pt),
                                        start=(pt == 0), stop=(pt == NPT - 1))
                    self.t0.inc(last)
                    if plain:
                        continue
                    for pt in range(NPT):
                        t.wait_ge(self.s1.sem, s1(ch, 11 + pt))
                        last = t.matmul(self.pD[:, :], c["odiv"][:, :], xb(pt),
                                        start=(pt == 0), stop=(pt == NPT - 1))
                    self.t0.inc(last)
            @blk.scalar
            def _(s):
                for ch in range(NCH):
                    s.wait_ge(self.t0.sem, t0(ch, 1))
                    for pt in range(NPT):
                        s.wait_ge(self.v1.sem, v1(ch, pt + 1))
                        sop(s, s.activation(xb(pt), xb(pt), AF.Square))
                    s.wait_ge(self.v1.sem, v1(ch, 11))
                    sop(s, s.activation(rstcf(ch), rstcf(ch), AF.Ln,
                                        bias=self.ceps[:, 0:1]))
                    sop(s, s.activation(rstcf(ch), rstcf(ch), AF.Exp,
                                        scale=-0.5))
                    if plain:
                        continue
                    s.wait_ge(self.t0.sem, t0(ch, 2))
                    for pt in range(NPT):
                        s.wait_ge(self.v1.sem, v1(ch, 13 + 2 * pt))
                        sop(s, s.activation(xb(pt), y2s(pt), AF.Square))
                    s.wait_ge(self.t0.sem, t0(ch, 3))
                    if ch >= 1:
                        s.wait_ge(self.v1.sem, v1(ch - 1, 42))
                    sop(s, s.activation(rstd2, self.pD[:, :], AF.Ln,
                                        bias=self.ceps[:, 1:2]))
                    sop(s, s.activation(rstd2, rstd2, AF.Exp, scale=-0.5))

    def ffn(self, xin, xin_done, xout):
        nc = self.nc
        wkbuf = self.a4flat[:, 0:8192].rearrange("p (a b) -> p a b", b=1024)
        with nc.Block() as blk:
            @blk.gpsimd
            def _(g):
                g.dma_start(self.aw[:, :, :],
                            self.fwq.rearrange("(a p) n -> p a n", p=128)) \
                    .then_inc(self.dL.sem, 16)
                self.dL.n += 16
                ck = self.sw[12]
                g.dma_start(wkbuf[:, :, :],
                            self.fwk.rearrange("(a p) n -> p a n", p=128)) \
                    .then_inc(ck.sem, 16)
                ck.n += 16
        self._front_fused(xin, xin_done, self.fAnw, self.a1)
        self._rope_tables()
        self._qkv_rope()
        self._attention()
        self._wo_h_xp(xin, xout)
        self._make_z2()
        return self._moe(xout)

    def _rope_tables(self):
        nc, c = self.nc, self.c
        fct = self.a5f[:, 0, :][0:64, :]
        ft = self.a5f[:, 1, 0:128].rearrange("p (a b) -> p a b", b=64)
        ldv = []
        bt, bv = self.t0.n, self.v0.n
        with nc.Block() as blk:
            @blk.sync
            def _(sy):
                for tt in range(NT):
                    dp = self.dp[1 + tt % 2]
                    if tt >= 2:
                        sy.wait_ge(dp.sem, ldv[tt - 2])
                        sy.wait_ge(self.t0.sem, bt + tt - 1)
                    sy.dma_start(ft[:, tt % 2, :],
                                 self.ffc[128 * tt:128 * (tt + 1), :]) \
                        .then_inc(dp.sem, 16)
                    dp.n += 16
                    ldv.append(dp.n)
            @blk.tensor
            def _(t):
                for tt in range(NT):
                    t.wait_ge(self.dp[1 + tt % 2].sem, ldv[tt])
                    if tt >= 2:
                        t.wait_ge(self.v0.sem, bv + tt - 1)
                    self.t0.inc(t.transpose(self.pA[0:64, 0:128] if tt % 2 == 0
                                            else self.pB[0:64, 0:128],
                                            ft[:, tt % 2, :], c["ident_f"][:, :]))
            @blk.vector
            def _(v):
                for tt in range(NT):
                    v.wait_ge(self.t0.sem, bt + tt + 1)
                    p = self.pA if tt % 2 == 0 else self.pB
                    self.v0.inc(v.tensor_copy(fct[:, 128 * tt:128 * (tt + 1)],
                                              p[0:64, 0:128]))
        bt, bv = self.t0.n, self.v0.n
        with nc.Block() as blk:
            @blk.tensor
            def _(t):
                for ch in range(NCH):
                    sl = slice(512 * ch, 512 * (ch + 1))
                    if ch >= 1:
                        t.wait_ge(self.v0.sem, bv + 2 * ch)
                    t.matmul(self.pC[:, :], c["rsel"][:, 0:128], fct[:, sl],
                             start=True, stop=True)
                    self.t0.inc(t.matmul(self.pD[:, :], c["rsel"][:, 128:256],
                                         fct[:, sl], start=True, stop=True))
            @blk.vector
            def _(v):
                for ch in range(NCH):
                    sl = slice(512 * ch, 512 * (ch + 1))
                    v.wait_ge(self.t0.sem, bt + ch + 1)
                    self.v0.inc(v.tensor_copy(self.ag[:, 0, sl], self.pC[:, :]))
                    self.v0.inc(v.tensor_copy(self.ag[:, 1, sl], self.pD[:, :]))

    def _qkv_rope(self):
        nc, c = self.nc, self.c
        a5s = self.a5[:, :, :].rearrange("p a (cc b) -> p (a cc) b", b=512)
        z1 = self.a1
        wkbuf = self.a4flat[:, 0:8192].rearrange("p (a b) -> p a b", b=1024)
        T0B, T1B, S1B, V1B = self.t0.n, self.t1.n, self.s1.n, self.v1.n
        T0 = {"q": T0B, "k": T0B + 32, "v": T0B + 64}
        T1 = {"q": T1B, "k": T1B + 32, "v": T1B + 64}
        S1 = {"q": S1B, "k": S1B + 32, "v": S1B + 64}
        V1 = {"q": V1B, "k": V1B + 96, "v": V1B + 192}
        vtot = self.dY.n + 16
        pAb = self.pA[:, :].bitcast(BF16)
        pBb = self.pB[:, :].bitcast(BF16)
        wq_tot, wk_tot = self.dL.n, self.sw[12].n
        self.dY.n += 16
        with nc.Block() as blk:
            @blk.gpsimd
            def _(g):
                g.wait_ge(self.t0.sem, T0B + 32)
                g.dma_start(self.aw[:, :, :],
                            self.fwv.rearrange("(a p) n -> p a n", p=128)) \
                    .then_inc(self.dY.sem, 16)

            @blk.tensor
            def _(t):
                for ph, wbuf in (("q", self.aw), ("k", wkbuf)):
                    if ph == "q":
                        t.wait_ge(self.dL.sem, wq_tot)
                    else:
                        t.wait_ge(self.sw[12].sem, wk_tot)
                    for k in range(34):
                        if k < 32:
                            nt, ch = divmod(k, NCH)
                            po = self.pA if k % 2 == 0 else self.pB
                            if k >= 2:
                                t.wait_ge(self.s1.sem, S1[ph] + k - 1)
                            elif ph == "k":
                                t.wait_ge(self.s1.sem, S1["q"] + 31 + k)
                            for kt in range(NPT):
                                last = t.matmul(
                                    po[:, :],
                                    wbuf[:, kt, 128 * nt:128 * (nt + 1)],
                                    z1[:, kt, 512 * ch:512 * (ch + 1)],
                                    start=(kt == 0), stop=(kt == NPT - 1))
                            self.t0.inc(last)
                        if k >= 2:
                            j = k - 2
                            t.wait_ge(self.s1.sem, S1[ph] + j + 1)
                            ps = self.pC if j % 2 == 0 else self.pD
                            if j >= 2:
                                t.wait_ge(self.v1.sem, V1[ph] + 3 * (j - 1))
                            elif ph == "k":
                                t.wait_ge(self.v1.sem, V1["q"] + 92 + 3 * j)
                            self.t1.inc(t.matmul(ps[:, :], c["pswap"][:, :],
                                                 a5s[:, j % 2, :],
                                                 start=True, stop=True))
                t.wait_ge(self.dY.sem, vtot)
                for nt in range(NPT):
                    for ch in range(NCH):
                        k = nt * NCH + ch
                        po = self.pC if k % 2 == 0 else self.pD
                        if k >= 2:
                            t.wait_ge(self.s1.sem, S1["v"] + k - 1)
                        else:
                            t.wait_ge(self.v1.sem, V1["k"] + 92 + 3 * k)
                        for kt in range(NPT):
                            last = t.matmul(
                                po[:, :],
                                self.aw[:, kt, 128 * nt:128 * (nt + 1)],
                                z1[:, kt, 512 * ch:512 * (ch + 1)],
                                start=(kt == 0), stop=(kt == NPT - 1))
                        self.t0.inc(last)
                    t.wait_ge(self.s1.sem, S1["v"] + (nt + 1) * NCH)
                    for tb in range(NT):
                        kk = nt * NT + tb
                        pb = pAb if kk % 2 == 0 else pBb
                        if kk >= 2:
                            t.wait_ge(self.v1.sem, V1["v"] + 2 * (kk - 1))
                        else:
                            t.wait_ge(self.s1.sem, S1["k"] + 31 + kk)
                        self.t1.inc(t.transpose(
                            pb[:, 0:128],
                            a5s[:, 8 + tb // 4,
                                128 * (tb % 4):128 * (tb % 4) + 128],
                            c["ident_b"][:, :]))

            @blk.scalar
            def _(s):
                for ph in ("q", "k"):
                    for k in range(32):
                        po = self.pA if k % 2 == 0 else self.pB
                        s.wait_ge(self.t0.sem, T0[ph] + k + 1)
                        if k >= 2:
                            s.wait_ge(self.v1.sem, V1[ph] + 3 * (k - 2) + 1)
                        elif ph == "k":
                            s.wait_ge(self.v1.sem, V1["q"] + 91 + 3 * k)
                        self.s1.inc(s.copy(a5s[:, k % 2, :], po[:, :]))
                for nt in range(NPT):
                    for ch in range(NCH):
                        k = nt * NCH + ch
                        po = self.pC if k % 2 == 0 else self.pD
                        s.wait_ge(self.t0.sem, T0["v"] + k + 1)
                        if nt >= 1:
                            s.wait_ge(self.t1.sem, T1["v"] + nt * NT)
                        elif k < 2:
                            s.wait_ge(self.v1.sem, V1["k"] + 91 + 3 * k)
                        self.s1.inc(s.copy(a5s[:, 8 + ch, :], po[:, :]))

            @blk.vector
            def _(v):
                for ph, dst in (("q", self.a2), ("k", self.a3)):
                    for k in range(32):
                        nt, ch = divmod(k, NCH)
                        sl = slice(512 * ch, 512 * (ch + 1))
                        ps = self.pC if k % 2 == 0 else self.pD
                        v.wait_ge(self.t1.sem, T1[ph] + k + 1)
                        v.wait_ge(self.v1.sem, self.v1.n)
                        self.v1.inc(v.tensor_mul(a5s[:, 2 + k % 2, :],
                                                 a5s[:, k % 2, :],
                                                 self.ag[:, 0, sl]))
                        v.wait_ge(self.v1.sem, self.v1.n)
                        self.v1.inc(v.tensor_mul(a5s[:, 4 + k % 2, :],
                                                 ps[:, :], self.ag[:, 1, sl]))
                        v.wait_ge(self.v1.sem, self.v1.n)
                        self.v1.inc(v.tensor_add(dst[:, nt, sl],
                                                 a5s[:, 2 + k % 2, :],
                                                 a5s[:, 4 + k % 2, :]))
                v.wait_ge(self.t0.sem, T0B + 64)
                for tt in range(NT):
                    v.memset(self.a4[:, tt, 64::65], 1.0)
                for nt in range(NPT):
                    for tb in range(NT):
                        kk = nt * NT + tb
                        v.wait_ge(self.t1.sem, T1["v"] + kk + 1)
                        pb = pAb if kk % 2 == 0 else pBb
                        hA, hB = 2 * nt, 2 * nt + 1
                        self.v1.inc(v.tensor_copy(
                            self.a4[:, tb, 65 * hA:65 * hA + 64], pb[:, 0:64]))
                        self.v1.inc(v.tensor_copy(
                            self.a4[:, tb, 65 * hB:65 * hB + 64],
                            pb[:, 64:128]))

    def _attention(self, retention=False):
        nc, c = self.nc, self.c
        a1v = self.a1[:, :, :].rearrange("p a (cc b) -> p (a cc) b", b=1024)
        a5s = self.a5[:, :, :].rearrange("p a (cc b) -> p (a cc) b", b=512)
        agb = self.ag[:, :, :].rearrange("p a b -> p (a b)").bitcast(BF16) \
            .rearrange("p (a b) -> p a b", b=512)
        if retention:
            bufB = self.af[:, :, :].rearrange("p a b -> p (a b)").bitcast(BF16) \
                .rearrange("p (a b) -> p a b", b=512)
        else:
            bufB = self.aw[:, :, :].rearrange("p a (c b) -> p (a c) b", b=512)

        def ebuf(u, hh, kt):
            if u % 2 == 0:
                return (a5s if hh == 0 else agb)[:, kt, :]
            return bufB[:, 8 * hh + kt, :]
        qT, kT = self.a2, self.a3
        bt0, bs1, bv1, bt1, bv0, bs0 = (self.t0.n, self.s1.n, self.v1.n,
                                        self.t1.n, self.v0.n, self.s0.n)
        avslot = lambda i: (self.pC, self.pD, self.pE, self.pF)[i % 4][:, 0:65]
        denr = self.asml[:, 72:76]
        NU = (NH // 2) * NCH
        uhst = []
        for stA, stB in ((3, 0), (2, 1)):
            for hp in range(NH // 2):
                uhst += [(hp, stA), (hp, stB)]
        on_S = lambda kt, hh: (not retention) or ((2 * kt + hh) % 2 == 0)
        s1v = {}
        v1e = {}
        v1m = {}
        t0v = {}
        s1c, v1c, t0c = bs1, bv1, bt0
        eidx = {}
        gi = 0
        for u, (hp, st) in enumerate(uhst):
            for kt in range(4 * st + 4):
                t0c += 1
                for hh in range(2):
                    t0v[(u, hh, kt)] = t0c
                    eidx[gi] = (u, hh, kt)
                    gi += 1
                    if on_S(kt, hh):
                        s1c += 1
                        s1v[(u, hh, kt)] = s1c
                    else:
                        v1c += 1
                        v1e[(u, hh, kt)] = v1c
            for hh in range(2):
                for j in range(4):
                    v1c += 1
                    v1m[(u, hh, j)] = v1c

        def emit_av(t, u):
            hp, st = uhst[u]
            smax = max([s1v[k] for k in s1v if k[0] == u], default=bs1)
            t.wait_ge(self.s1.sem, smax)
            t.wait_ge(self.v1.sem, v1m[(u, 1, 3)])
            nw = 64 if retention else 65
            for hh in range(2):
                h = 2 * hp + hh
                for j in range(4):
                    qt = 4 * st + j
                    gidx = 8 * u + 4 * hh + j + 1
                    if gidx >= 5:
                        if retention:
                            t.wait_ge(self.v0.sem, bv0 + gidx - 4)
                        else:
                            t.wait_ge(self.v0.sem, bv0 + 2 * (gidx - 4))
                    av = avslot(gidx)
                    for kt in range(qt + 1):
                        last = t.matmul(av[:, 0:nw],
                                        ebuf(u, hh, kt)[:, 128 * j:128 * (j + 1)],
                                        self.a4[:, kt, 65 * h:65 * h + nw],
                                        start=(kt == 0), stop=(kt == qt))
                    self.t1.inc(last)

        with nc.Block() as blk:
            @blk.tensor
            def _(t):
                n = 0
                for u in range(NU):
                    hp, st = uhst[u]
                    pt_h = hp
                    if u >= 2:
                        emit_av(t, u - 2)
                    for kt in range(4 * st + 4):
                        if n >= 2:
                            for hh in range(2):
                                up, hhp, kp = eidx[2 * (n - 2) + hh]
                                if (up, hhp, kp) in s1v:
                                    t.wait_ge(self.s1.sem, s1v[(up, hhp, kp)])
                                else:
                                    t.wait_ge(self.v1.sem, v1e[(up, hhp, kp)])
                        pe0 = self.pA if n % 2 == 0 else self.pB
                        pe1 = self.gram[:, 0:512] if n % 2 == 0 \
                            else self.gram[:, 512:1024]
                        t.matmul(pe0[:, :],
                                 kT[0:64, pt_h, 128 * kt:128 * (kt + 1)],
                                 qT[0:64, pt_h, 512 * st:512 * (st + 1)],
                                 start=True, stop=True)
                        self.t0.inc(t.matmul(
                            pe1[:, :],
                            kT[64:128, pt_h, 128 * kt:128 * (kt + 1)],
                            qT[64:128, pt_h, 512 * st:512 * (st + 1)],
                            start=True, stop=True))
                        n += 1
                emit_av(t, NU - 2)
                emit_av(t, NU - 1)

            @blk.scalar
            def _(s):
                n = 0
                for u in range(NU):
                    hp, st = uhst[u]
                    if u >= 2:
                        s.wait_ge(self.t1.sem, bt1 + 8 * (u - 1))
                    for kt in range(4 * st + 4):
                        for hh in range(2):
                            if on_S(kt, hh):
                                s.wait_ge(self.t0.sem, t0v[(u, hh, kt)])
                                psc = (self.pA if n % 2 == 0 else self.pB) \
                                    if hh == 0 else (
                                    self.gram[:, 0:512] if n % 2 == 0
                                    else self.gram[:, 512:1024])
                                if retention:
                                    self.s1.inc(s.copy(ebuf(u, hh, kt), psc[:, :]))
                                else:
                                    self.s1.inc(s.activation(
                                        ebuf(u, hh, kt), psc[:, :],
                                        AF.Exp, scale=0.125))
                        n += 1

            @blk.vector
            def _(v):
                def post(u):
                    hp, ss = uhst[u]
                    for hh in range(2):
                        h = 2 * hp + hh
                        for j in range(4):
                            gidx = 8 * u + 4 * hh + j + 1
                            qt = 4 * ss + j
                            v.wait_ge(self.t1.sem, bt1 + gidx)
                            av = avslot(gidx)
                            if retention:
                                self.v0.inc(v.tensor_scalar(
                                    a1v[:, qt, 64 * h:64 * (h + 1)], av[:, 0:64],
                                    c["gpos"][:, h * NT + qt:h * NT + qt + 1],
                                    None, ALU.mult))
                            else:
                                self.v0.inc(v.reciprocal(
                                    denr[:, gidx % 4:gidx % 4 + 1], av[:, 64:65]))
                                v.wait_ge(self.v0.sem, self.v0.n)
                                self.v0.inc(v.tensor_scalar(
                                    a1v[:, qt, 64 * h:64 * (h + 1)], av[:, 0:64],
                                    denr[:, gidx % 4:gidx % 4 + 1],
                                    None, ALU.mult))
                n = 0
                for u in range(NU):
                    hp, st = uhst[u]
                    if retention and u >= 2:
                        v.wait_ge(self.t1.sem, bt1 + 8 * (u - 1))
                    for kt in range(4 * st + 4):
                        for hh in range(2):
                            if not on_S(kt, hh):
                                v.wait_ge(self.t0.sem, t0v[(u, hh, kt)])
                                psc = (self.pA if n % 2 == 0 else self.pB) \
                                    if hh == 0 else (
                                    self.gram[:, 0:512] if n % 2 == 0
                                    else self.gram[:, 512:1024])
                                self.v1.inc(v.tensor_copy(ebuf(u, hh, kt),
                                                          psc[:, :]))
                        n += 1
                    for hh in range(2):
                        for j in range(4):
                            kt = 4 * st + j
                            if (u, hh, kt) in s1v:
                                v.wait_ge(self.s1.sem, s1v[(u, hh, kt)])
                            else:
                                v.wait_ge(self.v1.sem, self.v1.n)
                            self.v1.inc(v.tensor_mul(ebuf(u, hh, kt),
                                                     ebuf(u, hh, kt),
                                                     c["trimask"][:, j, :]))
                    if u >= 1:
                        post(u - 1)
                for u in (NU - 1,):
                    post(u)

    def _wo_h_xp(self, xin, xout):
        nc, c = self.nc, self.c
        a1v = self.a1[:, :, :].rearrange("p a (cc b) -> p (a cc) b", b=1024)
        pAb = self.pA[:, :].bitcast(BF16)
        pBb = self.pB[:, :].bitcast(BF16)
        pCb = self.pC[:, :].bitcast(BF16)
        pDb = self.pD[:, :].bitcast(BF16)
        bt, bv = self.t0.n, self.v0.n
        with nc.Block() as blk:
            @blk.tensor
            def _(t):
                for tt in range(NT):
                    for nt in range(NPT):
                        k = tt * NPT + nt
                        pb = (pAb, pBb, pCb, pDb)[k % 4]
                        if k >= 4:
                            t.wait_ge(self.v0.sem, bv + k - 3)
                        self.t0.inc(t.transpose(
                            pb[:, 0:128], a1v[:, tt, 128 * nt:128 * (nt + 1)],
                            c["ident_b"][:, :]))
            @blk.vector
            def _(v):
                for tt in range(NT):
                    for nt in range(NPT):
                        k = tt * NPT + nt
                        v.wait_ge(self.t0.sem, bt + k + 1)
                        pb = (pAb, pBb, pCb, pDb)[k % 4]
                        self.v0.inc(v.tensor_copy(
                            self.a2[:, nt, 128 * tt:128 * (tt + 1)], pb[:, 0:128]))
        with nc.Block() as blk:
            @blk.gpsimd
            def _(g):
                g.dma_start(self.aw[:, :, :],
                            self.fwo.rearrange("(a p) n -> p a n", p=128)) \
                    .then_inc(self.dL.sem, 16)
                self.dL.n += 16
                g.wait_ge(self.dL.sem, self.dL.n)
        xc = self.ag[:, 0, :].rearrange("p (a b) -> p a b", b=512)
        hsc = self.ag[:, 1, 0:512]
        xps = self.ag[:, 1, :].rearrange("p (a b) -> p a b", b=512)
        mus = self.af[:, 0, :]
        rst = self.af[:, 1, :]
        bt, bv = self.t0.n, self.v0.n
        bv1 = self.v1.n
        bs0w = self.s0.n
        ldv = []
        stv = []
        with nc.Block() as blk:
            @blk.sync
            def _(sy):
                for k in range(32):
                    ot, ch = divmod(k, NCH)
                    dp = self.dp[2 + k % 4]
                    if k >= 4:
                        sy.wait_ge(dp.sem, ldv[k - 4])
                        sy.wait_ge(self.v0.sem, bv + 4 * (k - 3))
                    sy.dma_start(xc[:, k % 4, :],
                                 xin[:, ot, 512 * ch:512 * (ch + 1)]) \
                        .then_inc(dp.sem, 16)
                    dp.n += 16
                    ldv.append(dp.n)
            @blk.scalar
            def _(sy):
                bs0 = self.s0.n
                for k in range(32):
                    ot, ch = divmod(k, NCH)
                    dp = self.dp[6 + k % 2]
                    sy.wait_ge(self.v0.sem, bv + 4 * (k + 1))
                    sy.dma_start(xout[:, ot, 512 * ch:512 * (ch + 1)],
                                 xps[:, 1 + k % 2, :]).then_inc(dp.sem, 16)
                    dp.n += 16
                    stv.append(dp.n)
                    sy.wait_ge(self.v1.sem, bv1 + 3 * k + 3)
                    self.s0.inc(sy.copy(self.a3[:, ot,
                                        512 * ch:512 * (ch + 1)], hsc))
            @blk.tensor
            def _(t):
                for k in range(32):
                    ot, ch = divmod(k, NCH)
                    po = self.pE if k % 2 == 0 else self.pF
                    if k >= 2:
                        t.wait_ge(self.v1.sem, bv1 + 3 * (k - 2) + 3)
                    for kt in range(NPT):
                        last = t.matmul(po[:, :],
                                        self.aw[:, kt, 128 * ot:128 * (ot + 1)],
                                        self.a2[:, kt, 512 * ch:512 * (ch + 1)],
                                        start=(kt == 0), stop=(kt == NPT - 1))
                    self.t0.inc(last)
            @blk.vector
            def _(v):
                for k in range(32):
                    ot, ch = divmod(k, NCH)
                    sl = slice(512 * ch, 512 * (ch + 1))
                    po = self.pE if k % 2 == 0 else self.pF
                    v.wait_ge(self.t0.sem, bt + k + 1)
                    v.wait_ge(self.dp[2 + k % 4].sem, ldv[k])
                    if k >= 2:
                        v.wait_ge(self.dp[6 + k % 2].sem, stv[k - 2])
                    if k >= 1:
                        v.wait_ge(self.v0.sem, bv + 4 * k)
                        v.wait_ge(self.s0.sem, bs0w + k)
                    self.v1.inc(v.tensor_sub(hsc, xc[:, k % 4, :], mus[:, sl]))
                    v.wait_ge(self.v1.sem, self.v1.n)
                    self.v1.inc(v.tensor_mul(hsc, hsc, rst[:, sl]))
                    v.wait_ge(self.v1.sem, self.v1.n)
                    self.v1.inc(v.tensor_add(hsc, hsc, po[:, :]))
                    v.wait_ge(self.v1.sem, self.v1.n)
                    last = v.tensor_add(xps[:, 1 + k % 2, :], xc[:, k % 4, :], hsc)
                    self.v0.inc(last, k=4)
        self.xp_done = [(self.dp[6], self.dp[6].n), (self.dp[7], self.dp[7].n)]

    def _make_z2(self):
        nc, c = self.nc, self.c
        wcol = self.asml[:, 146:154]
        with nc.Block() as blk:
            @blk.gpsimd
            def _(g):
                g.dma_start(wcol[:, 0:8],
                            self.fFnw.rearrange("o (a p) -> p (o a)", p=128),
                            allow_slow_non_contiguous=True).then_inc(self.dL.sem, 16)
                self.dL.n += 16
                g.wait_ge(self.dL.sem, self.dL.n)
        a5s = self.a5[:, :, :].rearrange("p a (cc b) -> p (a cc) b", b=512)
        rstd2 = self.af[:, 0, :]
        bs1, bt0, bv1 = self.s1.n, self.t0.n, self.v1.n
        with nc.Block() as blk:
            @blk.scalar
            def _(s):
                for ch in range(NCH):
                    sl = slice(512 * ch, 512 * (ch + 1))
                    for pt in range(NPT):
                        ki = ch * NPT + pt
                        if ki >= 2:
                            s.wait_ge(self.t0.sem, bt0 + ki - 1)
                        self.s1.inc(s.activation(a5s[:, 14 + pt % 2, :],
                                                 self.a3[:, pt, sl], AF.Square))
                    s.wait_ge(self.t0.sem, bt0 + 8 * (ch + 1))
                    self.s1.inc(s.activation(rstd2[:, sl],
                                             (self.pA, self.pB, self.pC,
                                              self.pD)[ch][:, :], AF.Ln,
                                             bias=self.ceps[:, 1:2]))
                    s.wait_ge(self.s1.sem, self.s1.n)
                    self.s1.inc(s.activation(rstd2[:, sl], rstd2[:, sl],
                                             AF.Exp, scale=-0.5))
            @blk.tensor
            def _(t):
                for ch in range(NCH):
                    for pt in range(NPT):
                        t.wait_ge(self.s1.sem, bs1 + ch * 10 + pt + 1)
                        self.t0.inc(t.matmul(
                            (self.pA, self.pB, self.pC, self.pD)[ch][:, :],
                            c["odiv"][:, :], a5s[:, 14 + pt % 2, :],
                            start=(pt == 0), stop=(pt == NPT - 1)))
            @blk.vector
            def _(v):
                for ch in range(NCH):
                    sl = slice(512 * ch, 512 * (ch + 1))
                    v.wait_ge(self.s1.sem, bs1 + ch * 10 + 10)
                    for pt in range(NPT):
                        v.wait_ge(self.v1.sem, self.v1.n)
                        self.v1.inc(v.tensor_mul(a5s[:, pt % 2, :],
                                                 self.a3[:, pt, sl],
                                                 rstd2[:, sl]))
                        v.wait_ge(self.v1.sem, self.v1.n)
                        self.v1.inc(v.tensor_scalar(self.a1[:, pt, sl],
                                                    a5s[:, pt % 2, :],
                                                    wcol[:, pt:pt + 1], None,
                                                    ALU.mult))

    def _moe(self, xout):
        nc, c = self.nc, self.c
        awf = self.aw[:, :, :].rearrange("p a b -> p (a b)").bitcast(F32)
        combT = awf[0:8, 0:2048]
        logit = awf[0:8, 2048:4096]
        probs = self.asml[:, 64:192].rearrange("p (a b) -> p a b", b=8)
        pmax = self.asml[:, 32:48]
        psum_c = self.asml[:, 48:64]
        prec = self.asml[:, 0:16]
        max1 = self.asml[:, 16:32]
        agf = self.ag[:, :, :].rearrange("p a b -> p (a b)")
        eq1 = agf[:, 0:128].rearrange("p (a b) -> p a b", b=8)
        pr2 = agf[:, 128:256].rearrange("p (a b) -> p a b", b=8)
        eq2 = agf[:, 256:384].rearrange("p (a b) -> p a b", b=8)
        tmpq = agf[:, 384:512].rearrange("p (a b) -> p a b", b=8)
        comb = agf[:, 512:640].rearrange("p (a b) -> p a b", b=8)
        max2 = self.asml[:, 154:170]
        ssm_ = self.asml[:, 170:186]
        with nc.Block() as blk:
            @blk.gpsimd
            def _(g):
                g.dma_start(self.aw[:, :, 0:8],
                            self.frw.rearrange("(a p) n -> p a n", p=128)) \
                    .then_inc(self.dL.sem, 16)
                self.dL.n += 16
                g.wait_ge(self.dL.sem, self.dL.n)
        bt = self.t0.n
        with nc.Block() as blk:
            @blk.tensor
            def _(t):
                for ch in range(NCH):
                    p = (self.pA, self.pB, self.pC, self.pD)[ch]
                    for kt in range(NPT):
                        last = t.matmul(p[0:8, :], self.aw[:, kt, 0:8],
                                        self.a1[:, kt, 512 * ch:512 * (ch + 1)],
                                        start=(kt == 0), stop=(kt == NPT - 1))
                    self.t0.inc(last)
            @blk.scalar
            def _(s):
                for ch in range(NCH):
                    s.wait_ge(self.t0.sem, bt + NCH)
                    p = (self.pA, self.pB, self.pC, self.pD)[ch]
                    s.copy(logit[:, 512 * ch:512 * (ch + 1)], p[0:8, :])
        bt, bv = self.t0.n, self.v0.n
        with nc.Block() as blk:
            @blk.tensor
            def _(t):
                for tt in range(NT):
                    p = self.pE if tt % 2 == 0 else self.pF
                    if tt >= 2:
                        t.wait_ge(self.v0.sem, bv + tt - 1)
                    self.t0.inc(t.transpose(p[:, 0:8],
                                            logit[:, 128 * tt:128 * (tt + 1)],
                                            c["ident_f"][0:8, 0:8]))
            @blk.vector
            def _(v):
                for tt in range(NT):
                    v.wait_ge(self.t0.sem, bt + tt + 1)
                    p = self.pE if tt % 2 == 0 else self.pF
                    self.v0.inc(v.tensor_copy(probs[:, tt, :], p[:, 0:8]))
        prb = probs[:, :, :]
        bP = lambda t16: t16.rearrange("p (a b) -> p a b", b=1).broadcast_to(
            (128, 16, 8))
        bv1, bs1 = self.v1.n, self.s1.n

        def vch(v, inst):
            self.v1.inc(inst)
            v.wait_ge(self.v1.sem, self.v1.n)
        with nc.Block() as blk:
            @blk.vector
            def _(v):
                vch(v, v.reduce_max(pmax[:, 0:16].rearrange("p (a b) -> p a b", b=1),
                                    prb, axis=AX.X))
                vch(v, v.tensor_sub(eq1[:, :, :], prb, bP(pmax[:, 0:16])))
                v.wait_ge(self.s1.sem, bs1 + 1)
                vch(v, v.reduce_sum(psum_c[:, 0:16].rearrange("p (a b) -> p a b", b=1),
                                    prb, axis=AX.X))
                vch(v, v.reciprocal(prec[:, 0:16], psum_c[:, 0:16]))
                vch(v, v.tensor_mul(prb, prb, bP(prec[:, 0:16])))
                vch(v, v.reduce_max(max1[:, 0:16].rearrange("p (a b) -> p a b", b=1),
                                    prb, axis=AX.X))
                vch(v, v.tensor_tensor(eq1[:, :, :], prb, bP(max1[:, 0:16]),
                                       ALU.is_equal))
                vch(v, v.scalar_tensor_tensor(pr2[:, :, :], eq1[:, :, :], -1.0,
                                              prb, ALU.mult, ALU.mult))
                vch(v, v.tensor_add(pr2[:, :, :], pr2[:, :, :], prb))
                vch(v, v.reduce_max(max2[:, 0:16].rearrange("p (a b) -> p a b", b=1),
                                    pr2, axis=AX.X))
                vch(v, v.tensor_tensor(eq2[:, :, :], pr2, bP(max2[:, 0:16]),
                                       ALU.is_equal))
                vch(v, v.tensor_add(ssm_[:, 0:16], max1[:, 0:16], max2[:, 0:16]))
                vch(v, v.reciprocal(ssm_[:, 0:16], ssm_[:, 0:16]))
                vch(v, v.tensor_mul(max1[:, 0:16], max1[:, 0:16], ssm_[:, 0:16]))
                vch(v, v.tensor_mul(max2[:, 0:16], max2[:, 0:16], ssm_[:, 0:16]))
                vch(v, v.tensor_mul(comb[:, :, :], eq1[:, :, :], bP(max1[:, 0:16])))
                vch(v, v.tensor_mul(eq2[:, :, :], eq2[:, :, :], bP(max2[:, 0:16])))
                vch(v, v.tensor_add(comb[:, :, :], comb[:, :, :], eq2[:, :, :]))
            @blk.scalar
            def _(s):
                s.wait_ge(self.v1.sem, bv1 + 2)
                self.s1.inc(s.activation(prb, eq1[:, :, :], AF.Exp))
        bt, bv = self.t0.n, self.v0.n
        with nc.Block() as blk:
            @blk.tensor
            def _(t):
                for tt in range(NT):
                    p = self.pE if tt % 2 == 0 else self.pF
                    if tt >= 2:
                        t.wait_ge(self.v0.sem, bv + tt - 1)
                    self.t0.inc(t.transpose(p[0:8, 0:128], comb[:, tt, :],
                                            c["ident_f"][:, :]))
            @blk.vector
            def _(v):
                for tt in range(NT):
                    v.wait_ge(self.t0.sem, bt + tt + 1)
                    p = self.pE if tt % 2 == 0 else self.pF
                    self.v0.inc(v.tensor_copy(combT[:, 128 * tt:128 * (tt + 1)],
                                              p[0:8, 0:128]))
        pG = self.gram[:, 0:512]
        pH = self.gram[:, 512:1024]
        a4flat = self.a4flat
        glo = self.a5[:, :, :].rearrange("p a (cc b) -> p (a cc) b", b=1024)
        ghi = a4flat[:, 0:8192].rearrange("p (a b) -> p a b", b=1024)
        g_of = lambda it: glo[:, it, :] if it < 8 else ghi[:, it - 8, :]
        ze = a4flat[:, 8192:16384].rearrange("p (a b) -> p a b", b=1024)
        affl = self.af[:, :, :].rearrange("p a b -> p (a b)")
        combB = lambda c2: self.af[:, 0, 1024 * c2:1024 * (c2 + 1)]
        silu_s = affl[:, 2048:3072].bitcast(BF16).rearrange(
            "p (a b) -> p a b", b=512)
        yev = self.af[:, 1, :].rearrange("p (a b) -> p a b", b=512)
        w2s = agf[:, 1024:4096].bitcast(BF16).rearrange(
            "p (a b) -> p a b", b=2048)
        w2v = lambda r: w2s[:, r, :].rearrange("p (a b) -> p a b", b=128)
        ldw = []
        with nc.Block() as blk:
            @blk.gpsimd
            def _(g):
                for dst, wsrc in ((self.a2, self.fw1), (self.a3, self.fw3)):
                    g.dma_start(dst[:, :, :],
                                wsrc[0].rearrange("(a p) i -> p a i", p=128)) \
                        .then_inc(self.dL.sem, 16)
                    self.dL.n += 16
                ldw.append(self.dL.n)
        accv = []
        w2ldv = []
        w2t0 = []
        w1tv = {}
        w3tv = {}
        NE = 8
        T0B, T1B, T2B = self.t0.n, self.t1.n, self.t2.n
        S1B, V1B, V2B = self.s1.n, self.v1.n, self.v2.n
        t2_start = lambda i: T2B + 2 + 18 * i
        t2_ye = lambda i, ka: t2_start(i) + (
            ka + 1 if (ka < 2 or i == 15) else ka + 3)
        t2_cb = lambda j: t2_start(j - 1) + 4
        sw_acc = (self.sw[self.swbase + 3], self.sw[self.swbase + 4])
        wtp = ([self.sw[5 + k] for k in range(4)] if self.swbase == 0
               else [self.sw[k] for k in range(4)])
        wtsem = lambda it, j: wtp[(2 * it + j) % 4]
        with nc.Block() as blk:
            @blk.gpsimd
            def _(g):
                for e in range(NE):
                    for c2 in range(2):
                        i = e * 2 + c2
                        w2base = len(w2ldv)
                        if i == 0:
                            for ctr, val in self.xp_done:
                                g.wait_ge(ctr.sem, val)

                        def issue_load(ot, e=e, i=i, w2base=w2base):
                            idx = w2base + ot
                            dpw = self.sw[self.swbase + idx % 3]
                            if idx >= 3:
                                g.wait_ge(dpw.sem, w2ldv[idx - 3])
                                g.wait_ge(self.t2.sem, w2t0[idx - 3])
                            g.dma_start(
                                w2s[:, idx % 3, :].rearrange(
                                    "p (a b) -> p a b", b=128),
                                self.fw2[e].rearrange("(a p) o -> p a o", p=128)
                                [:, :, 128 * ot:128 * (ot + 1)])                                 .then_inc(dpw.sem, 16)
                            dpw.n += 16
                            w2ldv.append(dpw.n)
                            w2t0.append(t2_ye(i, 2 * ot + 1))

                        for ot in range(3):
                            issue_load(ot)
                        if c2 == 1 and e < NE - 1:
                            for it in range(16):
                                g.wait_ge(self.t0.sem, T0B + 32 * i + 2 * it + 2)
                                g.wait_ge(self.t1.sem, T1B + 32 * i + 2 * it + 2)
                                for j, (dst, wsrc, rec) in enumerate((
                                        (self.a2, self.fw1, w1tv),
                                        (self.a3, self.fw3, w3tv))):
                                    ctr = wtsem(it, j)
                                    g.wait_ge(ctr.sem, ctr.n)
                                    g.dma_start(
                                        dst[:, :, 128 * it:128 * (it + 1)],
                                        wsrc[e + 1].rearrange(
                                            "(a p) i -> p a i", p=128)
                                        [:, :, 128 * it:128 * (it + 1)]) \
                                        .then_inc(ctr.sem, 16)
                                    ctr.n += 16
                                    rec[(e + 1, it)] = ctr.n
                        for ot in range(NPT):
                            for hf in range(2):
                                ka = ot * 2 + hf
                                dpa = sw_acc[ka % 2]
                                gi = 16 * i + ka
                                if gi >= 2:
                                    g.wait_ge(dpa.sem, accv[gi - 2])
                                g.wait_ge(self.s1.sem,
                                          S1B + 48 * i + 32 + ka + 1)
                                g.dma_start(
                                    xout[:, ot, 1024 * c2 + 512 * hf:
                                         1024 * c2 + 512 * (hf + 1)],
                                    yev[:, 2 + ka % 2, :],
                                    accum_op=ALU.add).then_inc(dpa.sem, 16)
                                dpa.n += 16
                                accv.append(dpa.n)
                            if ot + 3 < NPT:
                                issue_load(ot + 3)

            @blk.tensor
            def _(t):
                for e in range(NE):
                    for c2 in range(2):
                        i = e * 2 + c2
                        w2base = 8 * i
                        if i == 0:
                            t.wait_ge(self.dL.sem, ldw[0])
                            for hf in range(2):
                                self.t2.inc(t.matmul(
                                    (self.pC if hf == 0 else self.pD)[:, :],
                                    c["esel"][:, 0:128],
                                    combT[:, 512 * hf:512 * (hf + 1)],
                                    start=True, stop=True))
                        for it in range(16):
                            for hf in range(2):
                                ki = it * 2 + hf
                                p1 = (self.pA, self.pB, self.pC, self.pD)[ki % 4]
                                p3 = (self.pE, self.pF, pG, pH)[ki % 4]
                                if c2 == 0 and e > 0 and hf == 0:
                                    t.wait_ge(wtsem(it, 0).sem, w1tv[(e, it)])
                                if ki == 0 and i >= 1:
                                    t.wait_ge(self.s1.sem, S1B + 48 * i - 1)
                                if ki == 1 and i >= 1:
                                    t.wait_ge(self.s1.sem, S1B + 48 * i)
                                if ki == 2:
                                    t.wait_ge(self.v2.sem, V2B + 10 * i + 2)
                                if ki >= 4:
                                    t.wait_ge(self.s1.sem,
                                              S1B + 48 * i + ki - 3)
                                for kt in range(NPT):
                                    lastA = t.matmul(
                                        p1[:, :],
                                        self.a2[:, kt, 128 * it:128 * (it + 1)],
                                        self.a1[:, kt, 1024 * c2 + 512 * hf:
                                                1024 * c2 + 512 * (hf + 1)],
                                        start=(kt == 0), stop=(kt == NPT - 1))
                                self.t0.inc(lastA)
                                if c2 == 0 and e > 0 and hf == 0:
                                    t.wait_ge(wtsem(it, 1).sem, w3tv[(e, it)])
                                if ki == 0:
                                    t.wait_ge(self.v2.sem, V2B + 10 * (i + 1))
                                if ki >= 4 or i >= 1:
                                    t.wait_ge(self.v1.sem,
                                              max(0, V1B + 32 * i + ki - 3))
                                for kt in range(NPT):
                                    lastB = t.matmul(
                                        p3[:, :],
                                        self.a3[:, kt, 128 * it:128 * (it + 1)],
                                        ze[:, kt, 512 * hf:512 * (hf + 1)],
                                        start=(kt == 0), stop=(kt == NPT - 1))
                                self.t1.inc(lastB)
                        t.wait_ge(self.s1.sem, S1B + 48 * i + 32)
                        t.wait_ge(self.v1.sem, V1B + 32 * i + 32)
                        for ot in range(NPT):
                            t.wait_ge(self.sw[self.swbase + (w2base + ot) % 3].sem,
                                      w2ldv[w2base + ot])
                            for hf in range(2):
                                ka = ot * 2 + hf
                                po = self.pA if ka % 2 == 0 else self.pB
                                if ka >= 2:
                                    t.wait_ge(self.s1.sem,
                                              S1B + 48 * i + 32 + ka - 1)
                                for it in range(16):
                                    last = t.matmul(
                                        po[:, :],
                                        w2v((w2base + ot) % 3)[:, it, :],
                                        g_of(it)[:, 512 * hf:512 * (hf + 1)],
                                        start=(it == 0), stop=(it == 15))
                                self.t2.inc(last)
                            if ot == 0 and i < 15:
                                c2n = (i + 1) % 2
                                en = (i + 1) // 2
                                for hf in range(2):
                                    self.t2.inc(t.matmul(
                                        (self.pC if hf == 0 else self.pD)[:, :],
                                        c["esel"][:, 128 * en:128 * (en + 1)],
                                        combT[:, 1024 * c2n + 512 * hf:
                                              1024 * c2n + 512 * (hf + 1)],
                                        start=True, stop=True))

            @blk.scalar
            def _(s):
                for e in range(NE):
                    for c2 in range(2):
                        i = e * 2 + c2
                        for it in range(16):
                            for hf in range(2):
                                ki = it * 2 + hf
                                p1 = (self.pA, self.pB, self.pC, self.pD)[ki % 4]
                                s.wait_ge(self.t0.sem, T0B + 32 * i + ki + 1)
                                s.wait_ge(self.v1.sem,
                                          max(0, V1B + 32 * i + ki - 3))
                                self.s1.inc(s.activation(silu_s[:, ki % 4, :],
                                                         p1[:, :], AF.Silu))
                        for ka in range(16):
                            po = self.pA if ka % 2 == 0 else self.pB
                            s.wait_ge(self.t2.sem, t2_ye(i, ka))
                            gi = 16 * i + ka
                            if gi >= 2:
                                s.wait_ge(sw_acc[ka % 2].sem, accv[gi - 2])
                            self.s1.inc(s.copy(yev[:, 2 + ka % 2, :], po[:, :]))

            @blk.vector
            def _(v):
                def prep(j):
                    c2j = j % 2
                    v.wait_ge(self.t2.sem, T2B + 2 if j == 0 else t2_cb(j))
                    self.v2.inc(v.tensor_copy(combB(c2j)[:, 0:512],
                                              self.pC[:, :]))
                    self.v2.inc(v.tensor_copy(combB(c2j)[:, 512:1024],
                                              self.pD[:, :]))
                    v.wait_ge(self.v2.sem, V2B + 10 * j + 2)
                    for kt in range(NPT):
                        self.v2.inc(v.tensor_mul(
                            ze[:, kt, :],
                            self.a1[:, kt, 1024 * c2j:1024 * (c2j + 1)],
                            combB(c2j)))
                prep(0)
                for e in range(NE):
                    for c2 in range(2):
                        i = e * 2 + c2
                        if i >= 1:
                            v.wait_ge(self.t2.sem, t2_start(i))
                        for it in range(16):
                            for hf in range(2):
                                ki = it * 2 + hf
                                p3 = (self.pE, self.pF, pG, pH)[ki % 4]
                                v.wait_ge(self.t1.sem, T1B + 32 * i + ki + 1)
                                v.wait_ge(self.s1.sem, S1B + 48 * i + ki + 1)
                                self.v1.inc(v.tensor_mul(
                                    g_of(it)[:, 512 * hf:512 * (hf + 1)],
                                    silu_s[:, ki % 4, :], p3[:, :]))
                        if i < 15:
                            prep(i + 1)
        ret = [(self.sw[self.swbase + 3], self.sw[self.swbase + 3].n),
               (self.sw[self.swbase + 4], self.sw[self.swbase + 4].n)]
        self.swbase += 5
        return ret

    def _ln_plain_to(self, src, dst):
        nc = self.nc
        xc = self.ag[:, :, :].rearrange("p a (cc b) -> p (a cc) b", b=512)
        a5s = self.a5[:, :, :].rearrange("p a (cc b) -> p (a cc) b", b=512)
        mus = self.af[:, 0, :]
        rst = self.af[:, 1, :]
        for ch in range(NCH):
            sl = slice(512 * ch, 512 * (ch + 1))
            with nc.Block() as blk:
                @blk.sync
                def _(sy):
                    dp = self.dp[0]
                    sy.dma_start(xc[:, 0:8, :], src[:, :, sl]) \
                        .then_inc(dp.sem, 16)
                    dp.n += 16
                tot = self.dp[0].n
                @blk.vector
                def _(v):
                    v.wait_ge(self.dp[0].sem, tot)
                    for pt in range(NPT):
                        v.wait_ge(self.v1.sem, self.v1.n)
                        self.v1.inc(v.tensor_sub(a5s[:, pt, :], xc[:, pt, :],
                                                 mus[:, sl]))
                        v.wait_ge(self.v1.sem, self.v1.n)
                        self.v1.inc(v.tensor_mul(dst[:, pt, sl], a5s[:, pt, :],
                                                 rst[:, sl]))

    def attn_b(self, xin, xin_done, xout):
        nc, c = self.nc, self.c
        a4flat = self.a4flat
        wldA = a4flat[:, 0:8192].rearrange("p (a b) -> p a b", b=1024)
        wldB = a4flat[:, 8192:16384].rearrange("p (a b) -> p a b", b=1024)
        wldC = self.af[:, :, :].rearrange("p a b -> p (a b)").bitcast(BF16) \
            .rearrange("p (a b) -> p a b", b=1024)
        ldrec = {}
        wsem = {"wq": self.sw[10], "wk": self.sw[11],
                "wv": self.sw[12], "wo": self.sw[11]}
        with nc.Block() as blk:
            @blk.gpsimd
            def _(g):
                for nm, wsrc, buf in (("wq", self.bWq, wldA),
                                      ("wk", self.bWk, wldB)):
                    ctr = wsem[nm]
                    g.dma_start(buf[:, :, :],
                                wsrc.rearrange("(a p) d -> p a d", p=128)) \
                        .then_inc(ctr.sem, 16)
                    ctr.n += 16
                    ldrec[nm] = (ctr, ctr.n)
        self._front_fused(xin, xin_done, None, self.a1, plain=True)

        def transpose_w(buf, extra_load=None):
            pAb = self.pA[:, :].bitcast(BF16)
            pBb = self.pB[:, :].bitcast(BF16)
            bt, bv = self.t0.n, self.v0.n
            with nc.Block() as blk:
                if extra_load is not None:
                    nm, wsrc, dbuf = extra_load
                    @blk.gpsimd
                    def _(g):
                        ctr = wsem[nm]
                        g.dma_start(dbuf[:, :, :],
                                    wsrc.rearrange("(a p) d -> p a d", p=128)) \
                            .then_inc(ctr.sem, 16)
                        ctr.n += 16
                        ldrec[nm] = (ctr, ctr.n)
                @blk.tensor
                def _(t):
                    for k in range(64):
                        a, b = divmod(k, NPT)
                        pb = pAb if k % 2 == 0 else pBb
                        if k >= 2:
                            t.wait_ge(self.v0.sem, bv + k - 1)
                        self.t0.inc(t.transpose(
                            pb[:, 0:128], buf[:, a, 128 * b:128 * (b + 1)],
                            c["ident_b"][:, :]))
                @blk.vector
                def _(v):
                    for k in range(64):
                        a, b = divmod(k, NPT)
                        v.wait_ge(self.t0.sem, bt + k + 1)
                        pb = pAb if k % 2 == 0 else pBb
                        self.v0.inc(v.tensor_copy(
                            self.aw[:, b, 128 * a:128 * (a + 1)], pb[:, 0:128]))

        def wait_w(nm):
            with nc.Block() as blk:
                @blk.gpsimd
                def _(g):
                    ctr, val = ldrec[nm]
                    g.wait_ge(ctr.sem, val)

        wait_w("wq")
        transpose_w(wldA)
        self._proj_plain(self.a1, self.a2, scale_stream=None)
        with nc.Block() as blk:
            @blk.gpsimd
            def _(g):
                ctr = wsem["wv"]
                g.dma_start(wldA[:, :, :],
                            self.bWv.rearrange("(a p) d -> p a d", p=128)) \
                    .then_inc(ctr.sem, 16)
                ctr.n += 16
                ldrec["wv"] = (ctr, ctr.n)
        wait_w("wk")
        transpose_w(wldB, extra_load=("wo", self.bWo, wldC))
        self._proj_plain(self.a1, self.a3, scale_stream=self.cst["gneg"])
        wait_w("wv")
        transpose_w(wldA)
        self._v_proj_tm(self.a1)
        wait_w("wo")
        transpose_w(wldC)
        self._attention(retention=True)
        a1v = self.a1[:, :, :].rearrange("p a (cc b) -> p (a cc) b", b=1024)
        pAb = self.pA[:, :].bitcast(BF16)
        pBb = self.pB[:, :].bitcast(BF16)
        pCb = self.pC[:, :].bitcast(BF16)
        pDb = self.pD[:, :].bitcast(BF16)
        bt, bv = self.t0.n, self.v0.n
        with nc.Block() as blk:
            @blk.tensor
            def _(t):
                for tt in range(NT):
                    for nt in range(NPT):
                        k = tt * NPT + nt
                        pb = (pAb, pBb, pCb, pDb)[k % 4]
                        if k >= 4:
                            t.wait_ge(self.v0.sem, bv + k - 3)
                        self.t0.inc(t.transpose(
                            pb[:, 0:128], a1v[:, tt, 128 * nt:128 * (nt + 1)],
                            c["ident_b"][:, :]))
            @blk.vector
            def _(v):
                for tt in range(NT):
                    for nt in range(NPT):
                        k = tt * NPT + nt
                        v.wait_ge(self.t0.sem, bt + k + 1)
                        pb = (pAb, pBb, pCb, pDb)[k % 4]
                        self.v0.inc(v.tensor_copy(
                            self.a2[:, nt, 128 * tt:128 * (tt + 1)], pb[:, 0:128]))
        s1c = self.asml[:, 0:8]
        s2c = self.asml[:, 8:16]
        A2 = self.atin
        gwb = self.asml[:, 32:48]
        bt, bv, bs = self.t0.n, self.v1.n, self.s1.n
        self.dL.n += 32
        gdone = self.dL.n
        with nc.Block() as blk:
            @blk.gpsimd
            def _(g):
                g.dma_start(gwb[:, 0:8],
                            self.bGw.rearrange("o (a p) -> p (o a)", p=128),
                            allow_slow_non_contiguous=True).then_inc(self.dL.sem, 16)
                g.dma_start(gwb[:, 8:16],
                            self.bGb.rearrange("o (a p) -> p (o a)", p=128),
                            allow_slow_non_contiguous=True).then_inc(self.dL.sem, 16)
            @blk.vector
            def _(v):
                for pt in range(NPT):
                    self.v1.inc(v.reduce_sum(s1c[:, pt:pt + 1],
                                             self.a2[:, pt, :], axis=AX.X))
                v.wait_ge(self.t0.sem, bt + 1)
                self.v1.inc(v.tensor_scalar(A2[0:2, 0:16], self.pA[0:2, 0:16],
                                            1.0 / (64.0 * float(S)), None,
                                            ALU.mult))
                v.wait_ge(self.v1.sem, bv + 9)
                self.v1.inc(v.tensor_mul(A2[0:2, 16:24], A2[0:2, 0:8],
                                         A2[0:2, 0:8]))
                v.wait_ge(self.v1.sem, bv + 10)
                self.v1.inc(v.tensor_sub(A2[0:2, 16:24], A2[0:2, 8:16],
                                         A2[0:2, 16:24]))
                v.wait_ge(self.v1.sem, bv + 11)
                v.wait_ge(self.s1.sem, bs + 10)
                self.v1.inc(v.scalar_tensor_tensor(A2[0:2, 24:32], A2[0:2, 0:8],
                                                   -1.0, A2[0:2, 16:24],
                                                   ALU.mult, ALU.mult))
                v.wait_ge(self.t0.sem, bt + 2)
                self.v1.inc(v.tensor_copy(self.asml[:, 16:32], self.pB[:, 0:16]))
                v.wait_ge(self.v1.sem, bv + 13)
                v.wait_ge(self.dL.sem, gdone)
                self.v1.inc(v.tensor_mul(self.asml[:, 48:56],
                                         self.asml[:, 16:24], gwb[:, 0:8]))
                v.wait_ge(self.v1.sem, bv + 14)
                self.v1.inc(v.scalar_tensor_tensor(self.asml[:, 56:64],
                                                   self.asml[:, 24:32], 1.0,
                                                   gwb[:, 0:8], ALU.mult,
                                                   ALU.mult))
                v.wait_ge(self.v1.sem, bv + 15)
                self.v1.inc(v.tensor_add(self.asml[:, 56:64],
                                         self.asml[:, 56:64], gwb[:, 8:16]))
            @blk.scalar
            def _(s):
                for pt in range(NPT):
                    s.wait_ge(self.s1.sem, self.s1.n)
                    self.s1.inc(s.activation(self.a5[:, 0, 0:S],
                                             self.a2[:, pt, :], AF.Square,
                                             accum_out=s2c[:, pt:pt + 1]))
                s.wait_ge(self.v1.sem, bv + 11)
                s.wait_ge(self.s1.sem, bs + 8)
                self.s1.inc(s.activation(A2[0:2, 16:24], A2[0:2, 16:24], AF.Ln,
                                         bias=self.ceps[0:2, 0:1]))
                s.wait_ge(self.s1.sem, bs + 9)
                self.s1.inc(s.activation(A2[0:2, 16:24], A2[0:2, 16:24], AF.Exp,
                                         scale=-0.5))
                s.wait_ge(self.v1.sem, bv + 16)
                for pt in range(NPT):
                    for ch in range(NCH):
                        sl = slice(512 * ch, 512 * (ch + 1))
                        s.wait_ge(self.s1.sem, self.s1.n)
                        self.s1.inc(s.activation(
                            self.a1[:, pt, sl], self.a2[:, pt, sl], AF.Identity,
                            scale=self.asml[:, 48 + pt:49 + pt],
                            bias=self.asml[:, 56 + pt:57 + pt]))
            @blk.tensor
            def _(t):
                t.wait_ge(self.v1.sem, bv + 8)
                t.wait_ge(self.s1.sem, bs + 8)
                for pt in range(NPT):
                    t.matmul(self.pA[0:2, pt:pt + 1], c["gsel"][:, :],
                             s1c[:, pt:pt + 1], start=True, stop=True)
                    last = t.matmul(self.pA[0:2, 8 + pt:9 + pt], c["gsel"][:, :],
                                    s2c[:, pt:pt + 1], start=True, stop=True)
                self.t0.inc(last)
                t.wait_ge(self.s1.sem, bs + 10)
                t.wait_ge(self.v1.sem, bv + 12)
                for pt in range(NPT):
                    t.matmul(self.pB[:, pt:pt + 1], c["grepl"][0:2, :],
                             A2[0:2, 16 + pt:17 + pt], start=True, stop=True)
                    last = t.matmul(self.pB[:, 8 + pt:9 + pt], c["grepl"][0:2, :],
                                    A2[0:2, 24 + pt:25 + pt], start=True, stop=True)
                self.t0.inc(last)
        xc = self.ag[:, 0, :].rearrange("p (a b) -> p a b", b=512)
        och = self.ag[:, 1, :].rearrange("p (a b) -> p a b", b=512)
        bt, bv = self.t0.n, self.v0.n
        ldv = []
        stv = []
        with nc.Block() as blk:
            @blk.sync
            def _(sy):
                for k in range(32):
                    ot, ch = divmod(k, NCH)
                    dp = self.dp[2 + k % 4]
                    if k >= 4:
                        sy.wait_ge(dp.sem, ldv[k - 4])
                        sy.wait_ge(self.v0.sem, bv + k - 3)
                    sy.dma_start(xc[:, k % 4, :],
                                 xin[:, ot, 512 * ch:512 * (ch + 1)]) \
                        .then_inc(dp.sem, 16)
                    dp.n += 16
                    ldv.append(dp.n)
            @blk.scalar
            def _(sy):
                for k in range(32):
                    ot, ch = divmod(k, NCH)
                    dp = self.dp[6 + k % 2]
                    sy.wait_ge(self.v0.sem, bv + k + 1)
                    sy.dma_start(xout[:, ot, 512 * ch:512 * (ch + 1)],
                                 och[:, k % 2, :]).then_inc(dp.sem, 16)
                    dp.n += 16
                    stv.append(dp.n)
            @blk.tensor
            def _(t):
                for k in range(32):
                    ot, ch = divmod(k, NCH)
                    po = self.pE if k % 2 == 0 else self.pF
                    if k >= 2:
                        t.wait_ge(self.v0.sem, bv + k - 1)
                    for kt in range(NPT):
                        last = t.matmul(po[:, :],
                                        self.aw[:, kt, 128 * ot:128 * (ot + 1)],
                                        self.a1[:, kt, 512 * ch:512 * (ch + 1)],
                                        start=(kt == 0), stop=(kt == NPT - 1))
                    self.t0.inc(last)
            @blk.vector
            def _(v):
                for k in range(32):
                    po = self.pE if k % 2 == 0 else self.pF
                    v.wait_ge(self.t0.sem, bt + k + 1)
                    v.wait_ge(self.dp[2 + k % 4].sem, ldv[k])
                    if k >= 2:
                        v.wait_ge(self.dp[6 + k % 2].sem, stv[k - 2])
                    self.v0.inc(v.tensor_add(och[:, k % 2, :], xc[:, k % 4, :],
                                             po[:, :]))
        return [(self.dp[6], self.dp[6].n), (self.dp[7], self.dp[7].n)]

    def _proj_plain(self, zin, dst, scale_stream):
        nc, c = self.nc, self.c
        gst = self.ag[:, :, :]
        bt0, bs1, bv1 = self.t0.n, self.s1.n, self.v1.n
        ldv = []
        with nc.Block() as blk:
            if scale_stream is not None:
                @blk.sync
                def _(sy):
                    for nt in range(NPT):
                        dp = self.dp[0 if nt % 2 == 0 else 1]
                        if nt >= 2:
                            sy.wait_ge(dp.sem, ldv[nt - 2])
                            sy.wait_ge(self.v1.sem, bv1 + 4 * (nt - 1))
                        sy.dma_start(gst[:, nt % 2, :],
                                     scale_stream[128 * nt:128 * (nt + 1), :]) \
                            .then_inc(dp.sem, 16)
                        dp.n += 16
                        ldv.append(dp.n)
            @blk.tensor
            def _(t):
                for k in range(32):
                    nt, ch = divmod(k, NCH)
                    po = self.pA if k % 2 == 0 else self.pB
                    if k >= 2:
                        if scale_stream is not None:
                            t.wait_ge(self.v1.sem, bv1 + k - 1)
                        else:
                            t.wait_ge(self.s1.sem, bs1 + k - 1)
                    for kt in range(NPT):
                        last = t.matmul(po[:, :],
                                        self.aw[:, kt, 128 * nt:128 * (nt + 1)],
                                        zin[:, kt, 512 * ch:512 * (ch + 1)],
                                        start=(kt == 0), stop=(kt == NPT - 1))
                    self.t0.inc(last)
            if scale_stream is not None:
                @blk.vector
                def _(v):
                    for k in range(32):
                        nt, ch = divmod(k, NCH)
                        sl = slice(512 * ch, 512 * (ch + 1))
                        po = self.pA if k % 2 == 0 else self.pB
                        v.wait_ge(self.t0.sem, bt0 + k + 1)
                        if ch == 0:
                            v.wait_ge(self.dp[0 if nt % 2 == 0 else 1].sem,
                                      ldv[nt])
                        self.v1.inc(v.tensor_mul(dst[:, nt, sl], po[:, :],
                                                 gst[:, nt % 2, sl]))
            else:
                @blk.scalar
                def _(s):
                    for k in range(32):
                        nt, ch = divmod(k, NCH)
                        sl = slice(512 * ch, 512 * (ch + 1))
                        po = self.pA if k % 2 == 0 else self.pB
                        s.wait_ge(self.t0.sem, bt0 + k + 1)
                        self.s1.inc(s.copy(dst[:, nt, sl], po[:, :]))

    def _v_proj_tm(self, zin):
        nc, c = self.nc, self.c
        a5s = self.a5[:, :, :].rearrange("p a (cc b) -> p (a cc) b", b=512)
        bt0, bs1, bt1, bv1 = self.t0.n, self.s1.n, self.t1.n, self.v1.n
        pAb = self.pA[:, :].bitcast(BF16)
        pBb = self.pB[:, :].bitcast(BF16)
        with nc.Block() as blk:
            @blk.tensor
            def _(t):
                for nt in range(NPT):
                    for ch in range(NCH):
                        k = nt * NCH + ch
                        po = self.pC if k % 2 == 0 else self.pD
                        if k >= 2:
                            t.wait_ge(self.s1.sem, bs1 + k - 1)
                        for kt in range(NPT):
                            last = t.matmul(
                                po[:, :], self.aw[:, kt, 128 * nt:128 * (nt + 1)],
                                zin[:, kt, 512 * ch:512 * (ch + 1)],
                                start=(kt == 0), stop=(kt == NPT - 1))
                        self.t0.inc(last)
                    t.wait_ge(self.s1.sem, bs1 + (nt + 1) * NCH)
                    for tb in range(NT):
                        kk = nt * NT + tb
                        pb = pAb if kk % 2 == 0 else pBb
                        if kk >= 2:
                            t.wait_ge(self.v1.sem, bv1 + 2 * (kk - 1))
                        self.t1.inc(t.transpose(
                            pb[:, 0:128],
                            a5s[:, 8 + tb // 4, 128 * (tb % 4):128 * (tb % 4) + 128],
                            c["ident_b"][:, :]))
            @blk.scalar
            def _(s):
                for nt in range(NPT):
                    for ch in range(NCH):
                        k = nt * NCH + ch
                        po = self.pC if k % 2 == 0 else self.pD
                        s.wait_ge(self.t0.sem, bt0 + k + 1)
                        if nt >= 1:
                            s.wait_ge(self.t1.sem, bt1 + nt * NT)
                        self.s1.inc(s.copy(a5s[:, 8 + ch, :], po[:, :]))
            @blk.vector
            def _(v):
                for tt in range(NT):
                    v.memset(self.a4[:, tt, 64::65], 1.0)
                for nt in range(NPT):
                    for tb in range(NT):
                        kk = nt * NT + tb
                        v.wait_ge(self.t1.sem, bt1 + kk + 1)
                        pb = pAb if kk % 2 == 0 else pBb
                        hA, hB = 2 * nt, 2 * nt + 1
                        self.v1.inc(v.tensor_copy(
                            self.a4[:, tb, 65 * hA:65 * hA + 64], pb[:, 0:64]))
                        self.v1.inc(v.tensor_copy(
                            self.a4[:, tb, 65 * hB:65 * hB + 64], pb[:, 64:128]))

    def finalize_out_from_hx(self, idx, done=None):
        nc, c = self.nc, self.c
        hsrc = self.hxp if idx == 4 else self.hx[idx]
        xch = self.ag[:, 0, :].rearrange("p (a b) -> p a b", b=1024)
        och = self.ag[:, 1, :].rearrange("p (a b) -> p a b", b=1024)
        bt, bv = self.t0.n, self.v0.n
        ldv = []
        stv = []
        with nc.Block() as blk:
            @blk.sync
            def _(sy):
                if done is not None:
                    for ctr, val in done:
                        sy.wait_ge(ctr.sem, val)
                for k in range(16):
                    pt, hh = divmod(k, 2)
                    dp = self.dp[2 + k % 2]
                    if k >= 2:
                        sy.wait_ge(dp.sem, ldv[k - 2])
                        sy.wait_ge(self.t0.sem, bt + 8 * (k - 1))
                    sy.dma_start(xch[:, k % 2, :],
                                 hsrc[:, pt, 1024 * hh:1024 * (hh + 1)]) \
                        .then_inc(dp.sem, 16)
                    dp.n += 16
                    ldv.append(dp.n)
            @blk.scalar
            def _(sy):
                for k in range(16):
                    pt, hh = divmod(k, 2)
                    dp = self.dp[4 + k % 2]
                    sy.wait_ge(self.v0.sem, bv + (k + 1) * 8)
                    sy.dma_start(
                        self.out[1024 * hh:1024 * (hh + 1),
                                 128 * pt:128 * (pt + 1)].rearrange(
                                     "(q p) d -> p q d", p=128),
                        och[:, k % 2, :].rearrange("p (q d) -> p q d", d=128)) \
                        .then_inc(dp.sem, 16)
                    dp.n += 16
                    stv.append(dp.n)
            @blk.tensor
            def _(t):
                for k in range(16):
                    t.wait_ge(self.dp[2 + k % 2].sem, ldv[k])
                    for q in range(8):
                        kk = k * 8 + q
                        p = (self.pA, self.pB, self.pC, self.pD)[kk % 4]
                        if kk >= 4:
                            t.wait_ge(self.v0.sem, bv + kk - 3)
                        self.t0.inc(t.transpose(p[:, 0:128],
                                                xch[:, k % 2, 128 * q:128 * (q + 1)],
                                                c["ident_f"][:, :]))
            @blk.vector
            def _(v):
                for k in range(16):
                    for q in range(8):
                        kk = k * 8 + q
                        v.wait_ge(self.t0.sem, bt + kk + 1)
                        if q == 0 and k >= 2:
                            v.wait_ge(self.dp[4 + k % 2].sem, stv[k - 2])
                        p = (self.pA, self.pB, self.pC, self.pD)[kk % 4]
                        self.v0.inc(v.tensor_copy(
                            och[:, k % 2, 128 * q:128 * (q + 1)], p[:, 0:128]))
        with nc.Block() as blk:
            @blk.sync
            def _(sy):
                sy.wait_ge(self.dp[4].sem, self.dp[4].n)
                sy.wait_ge(self.dp[5].sem, self.dp[5].n)

    def build(self):
        self._load_consts()
        self.prologue()
        self.attn_a()
        if NSUB == 1:
            self.finalize_out_from_hx(1, done=self.hx1_done)
        else:
            done2 = self.ffn(self.hx[1], self.hx1_done, self.hx[2])
            if NSUB == 2:
                self.finalize_out_from_hx(2, done=done2)
            else:
                done3 = self.attn_b(self.hx[2], done2, self.hx[3])
                if NSUB == 3:
                    self.finalize_out_from_hx(3, done=done3)
                else:
                    done4 = self.ffn(self.hx[3], done3, self.hxp)
                    self.finalize_out_from_hx(4, done=done4)
        self.es.close()
        return self.nc


_CACHE = {}


def _get_nc():
    if "nc" not in _CACHE:
        _CACHE["nc"] = MK().build()
    return _CACHE["nc"]


def kernel(**inputs):
    nc = _get_nc()
    hc = _host_consts()
    x = np.ascontiguousarray(np.asarray(inputs["L0_attn_a_x"], dtype=np.float32))
    base = {
        "xall": x,
        "aWq": inputs["L0_attn_a_W_q"], "aWk": inputs["L0_attn_a_W_k"],
        "aWv": inputs["L0_attn_a_W_v"], "aWo": inputs["L0_attn_a_W_o"],
        "aA": inputs["L0_attn_a_SSM_A"], "aB": inputs["L0_attn_a_SSM_B"],
        "aC": inputs["L0_attn_a_SSM_C"],
        "aLg": np.asarray(inputs["L0_attn_a_ln_gamma"]).reshape(1, D),
        "aLb": np.asarray(inputs["L0_attn_a_ln_beta"]).reshape(1, D),
        "bWq": inputs["L0_attn_b_Wq"], "bWk": inputs["L0_attn_b_Wk"],
        "bWv": inputs["L0_attn_b_Wv"], "bWo": inputs["L0_attn_b_Wo"],
        "bGw": np.asarray(inputs["L0_attn_b_gn_w"]).reshape(1, D),
        "bGb": np.asarray(inputs["L0_attn_b_gn_b"]).reshape(1, D),
        "fAnw": np.asarray(inputs["L0_ffn_attn_norm_w"]).reshape(1, D),
        "fFnw": np.asarray(inputs["L0_ffn_ffn_norm_w"]).reshape(1, D),
        "fwq": inputs["L0_ffn_wq"], "fwk": inputs["L0_ffn_wk"],
        "fwv": inputs["L0_ffn_wv"], "fwo": inputs["L0_ffn_wo"],
        "frw": inputs["L0_ffn_router_w"],
        "fw1": inputs["L0_ffn_w1"], "fw3": inputs["L0_ffn_w3"],
        "fw2": inputs["L0_ffn_w2"],
        "ffc": inputs["L0_ffn_freqs_cis"],
    }
    base = {k: np.ascontiguousarray(np.asarray(v, dtype=np.float32))
            for k, v in base.items()}
    for k, v in hc.items():
        base["c_" + k] = v
    in_maps = []
    for b in range(8):
        m = dict(base)
        m["xown"] = base["xall"][b]
        sel = np.zeros((1, 8), dtype=np.float32)
        sel[0, b] = 1.0
        m["bselv"] = sel
        in_maps.append(m)
    res = run_bass_kernel_spmd(nc, in_maps, core_ids=list(range(8)))
    return np.stack([np.asarray(r["out"]) for r in res.results], axis=0)


# version 17 (final)
# speedup vs baseline: 1.0695x; 1.0053x over previous
import numpy as np
import ml_dtypes
import concourse.bass as bass
import concourse.mybir as mybir
from concourse.bass_utils import run_bass_kernel_spmd
from contextlib import ExitStack

F32 = mybir.dt.float32
BF16 = mybir.dt.bfloat16
AF = mybir.ActivationFunctionType
ALU = mybir.AluOpType
AX = mybir.AxisListType

DEBUG = False
NSUB = 4
S = 2048
D = 1024
NB = 8
NH = 16
HD = 64
NT = S // 128
NPT = D // 128
NCH = S // 512
EPS_LN = 1e-5
EPS_RMS = 1e-6
LS = float(np.log(np.float64(np.float32(S) + np.float32(1.0))))


def _host_consts():
    c = {}
    c["ident_f"] = np.eye(128, dtype=np.float32)
    c["ident_b"] = np.eye(128, dtype=ml_dtypes.bfloat16)
    c["odiv"] = np.full((128, 128), 1.0 / 1024.0, dtype=ml_dtypes.bfloat16)
    c["ones_col_f"] = np.ones((128, 1), dtype=np.float32)
    c["ones_row_f"] = np.ones((1, 128), dtype=np.float32)
    c["ones_b"] = np.ones((128, 128), dtype=ml_dtypes.bfloat16)
    rs = np.zeros((64, 256), dtype=np.float32)
    for p in range(128):
        j = (p % 64) // 2
        rs[2 * j, p] = 1.0
        rs[2 * j + 1, 128 + p] = -1.0 if p % 2 == 0 else 1.0
    c["rsel"] = rs
    ps = np.zeros((128, 128), dtype=ml_dtypes.bfloat16)
    for p in range(128):
        ps[p ^ 1, p] = 1.0
    c["pswap"] = ps
    es = np.zeros((8, 1024), dtype=np.float32)
    for e in range(8):
        es[e, 128 * e:128 * (e + 1)] = 1.0
    c["esel"] = es
    tri = (np.arange(128)[:, None] <= np.arange(128)[None, :]).astype(np.float32)
    tm = np.zeros((128, 4, 512), dtype=ml_dtypes.bfloat16)
    for v in range(4):
        blk = np.zeros((128, 512), dtype=np.float32)
        blk[:, 128 * v:128 * (v + 1)] = tri
        blk[:, 128 * (v + 1):] = 1.0
        tm[:, v, :] = blk.astype(ml_dtypes.bfloat16)
    c["trimask"] = tm
    hh = np.arange(NH, dtype=np.float64)
    gam = 1.0 - 2.0 ** (-5.0 - hh * (8.0 / NH))
    t = np.arange(S, dtype=np.float64)
    gneg = np.zeros((NPT, 128, S), dtype=np.float32)
    for pt in range(NPT):
        for half in range(2):
            h = 2 * pt + half
            gneg[pt, 64 * half:64 * (half + 1), :] = (
                0.125 * gam[h] ** (-t)).astype(np.float32)
    c["gneg"] = gneg.reshape(NPT * 128, S)
    gp = np.zeros((128, NH * NT), dtype=np.float32)
    for h in range(NH):
        for qt in range(NT):
            gp[:, h * NT + qt] = (gam[h] ** (128.0 * qt + np.arange(128))
                                  ).astype(np.float32)
    c["gpos"] = gp
    gs = np.zeros((128, 2), dtype=np.float32)
    gs[0:64, 0] = 1.0
    gs[64:128, 1] = 1.0
    c["gsel"] = gs
    gr = np.zeros((2, 128), dtype=np.float32)
    gr[0, 0:64] = 1.0
    gr[1, 64:128] = 1.0
    c["grepl"] = gr
    return c


class Ctr:
    def __init__(self, sem):
        self.sem = sem
        self.n = 0

    def inc(self, inst, k=1):
        inst.then_inc(self.sem, k)
        self.n += k


class MK:
    def __init__(self):
        self.nc = bass.Bass("TRN2", target_bir_lowering=False)
        self.hc = _host_consts()
        self.es = ExitStack()
        self._dram()
        self._sbuf()

    def _dram(self):
        nc = self.nc
        EI = dict(kind="ExternalInput")
        self.xall = nc.dram_tensor("xall", [8, S, D], F32, **EI).ap()
        self.xown = nc.dram_tensor("xown", [S, D], F32, **EI).ap()
        self.bselv = nc.dram_tensor("bselv", [1, 8], F32, **EI).ap()
        for nm, shp in (("aWq", [8, 128, 128]), ("aWk", [8, 128, 128]),
                        ("aWv", [8, 128, 128]), ("aWo", [D, D]),
                        ("aA", [D, D]), ("aB", [D, D]), ("aC", [D, D]),
                        ("aLg", [1, D]), ("aLb", [1, D]),
                        ("bWq", [D, D]), ("bWk", [D, D]), ("bWv", [D, D]),
                        ("bWo", [D, D]), ("bGw", [1, D]), ("bGb", [1, D]),
                        ("fAnw", [1, D]), ("fFnw", [1, D]),
                        ("fwq", [D, D]), ("fwk", [D, D]), ("fwv", [D, D]),
                        ("fwo", [D, D]), ("frw", [D, 8]),
                        ("fw1", [8, D, 2048]), ("fw3", [8, D, 2048]),
                        ("fw2", [8, 2048, D]), ("ffc", [S, 64])):
            setattr(self, nm, nc.dram_tensor(nm, shp, F32, **EI).ap())
        self.cst = {}
        for k, v in self.hc.items():
            dt = BF16 if v.dtype == ml_dtypes.bfloat16 else F32
            self.cst[k] = nc.dram_tensor("c_" + k, list(v.shape), dt, **EI).ap()
        self.out = nc.dram_tensor("out", [S, D], F32, kind="ExternalOutput").ap()
        SCR = dict(kind="ExternalOutput" if DEBUG else "Internal")
        self.hx = [nc.dram_tensor(f"hx{i}", [128, NPT, S], F32, **SCR).ap()
                   for i in range(4)]
        self.hgn = nc.dram_tensor("hgn", [128, NPT, S], F32, kind="Internal").ap()
        self.hxp = nc.dram_tensor("hxp", [128, NPT, S], F32, kind="Internal").ap()

    def _sbuf(self):
        nc, es = self.nc, self.es
        sbt = lambda *a: es.enter_context(nc.sbuf_tensor(*a))
        self.a1 = sbt("a1", [128, NPT, S], BF16)
        self.a2 = sbt("a2", [128, NPT, S], BF16)
        self.a3 = sbt("a3", [128, NPT, S], BF16)
        self.a4 = sbt("a4", [128, NT, 1040], BF16)
        self.a5 = sbt("a5", [128, 4, S], BF16)
        self.af = sbt("af", [128, 2, S], F32)
        self.ag = sbt("ag", [128, 2, S], F32)
        self.aw = sbt("aw", [128, NPT, 1024], BF16)
        self.asml = sbt("asml", [128, 192], F32)
        self.atin = sbt("atin", [128, 48], F32)

        def fview(t):
            return t[:, :, :].rearrange("p a b -> p (a b)").bitcast(F32) \
                .rearrange("p (a b) -> p a b", b=S)
        self.a1f = fview(self.a1)
        self.a2f = fview(self.a2)
        self.a3f = fview(self.a3)
        self.a5f = self.a5[:, :, :].rearrange("p a b -> p (a b)").bitcast(F32) \
            .rearrange("p (a b) -> p a b", b=S)
        self.ass = self.a5f[:, 1, :].rearrange("p (a b) -> p a b", b=512)
        self.a4flat = self.a4[:, :, :].rearrange("p a b -> p (a b)")
        self.a4f = self.a4flat[:, 0:16384].bitcast(F32).rearrange(
            "p (a b) -> p a b", b=S)
        self.c = {}
        for k, v in self.hc.items():
            if k == "gneg":
                continue
            dt = BF16 if v.dtype == ml_dtypes.bfloat16 else F32
            self.c[k] = sbt("k_" + k, list(v.shape), dt)
        ps = lambda *a: es.enter_context(nc.psum_tensor(*a))
        self.pA = ps("pA", [128, 512], F32)
        self.pB = ps("pB", [128, 512], F32)
        self.pC = ps("pC", [128, 512], F32)
        self.pD = ps("pD", [128, 512], F32)
        self.pE = ps("pE", [128, 512], F32)
        self.pF = ps("pF", [128, 512], F32)
        self.gram = ps("gram", [128, 1024], F32)
        self.gslot = lambda h: (
            (self.pA, self.pB, self.pC, self.pD, self.pE, self.pF)[h][:, 0:128]
            if h < 6 else self.gram[:, 512 * (h - 6):512 * (h - 6) + 128])
        self.am = self.pF[0:1, 384:392]
        sem = lambda n: Ctr(es.enter_context(nc.semaphore(n)))
        self.ceps = sbt("ceps", [128, 2], F32)
        self.dL = sem("dL")
        self.dX = sem("dX")
        self.dY = sem("dY")
        self.dO = sem("dO")
        self.t0 = sem("t0")
        self.v0 = sem("v0")
        self.s0 = sem("s0")
        self.t1 = sem("t1")
        self.v1 = sem("v1")
        self.s1 = sem("s1")
        self.g1 = sem("g1")
        self.t2 = sem("t2")
        self.v2 = sem("v2")
        self.dp = [sem(f"dp{i}") for i in range(8)]
        self.sw = [sem(f"sw{i}") for i in range(13)]
        self.swbase = 0

    def _load_consts(self):
        nc = self.nc
        with nc.Block() as blk:
            @blk.vector
            def _(v):
                v.memset(self.ceps[:, 0:1], EPS_LN)
                v.memset(self.ceps[:, 1:2], EPS_RMS)
            @blk.sync
            def _(sy):
                for k, sb in self.c.items():
                    idx = tuple([slice(None)] * len(self.hc[k].shape))
                    sy.dma_start(sb[idx], self.cst[k]).then_inc(self.dX.sem, 16)
                    self.dX.n += 16
                sy.wait_ge(self.dX.sem, self.dX.n)

    def prologue(self):
        nc, c, es = self.nc, self.c, self.es
        self.wv_w = self.a4flat[:, 8192:11264].rearrange(
            "p (m h d) -> p m h d", m=3, h=8)
        self.wv_d = self.a4flat[:, 11264:14336].rearrange(
            "p (m h d) -> p m h d", m=3, h=8)
        with nc.Block() as blk:
            @blk.gpsimd
            def _(g):
                for m, ap_w in enumerate((self.aWq, self.aWk, self.aWv)):
                    g.dma_start(self.wv_w[:, m, :, :],
                                ap_w.rearrange("h v d -> v h d")).then_inc(self.dL.sem, 16)
                    self.dL.n += 16
                for m, ap_m in enumerate((self.aA, self.aB, self.aC)):
                    for h in range(NB):
                        sl = slice(128 * h, 128 * (h + 1))
                        g.dma_start(self.wv_d[:, m, h, :], ap_m[sl, sl]) \
                            .then_inc(self.dL.sem, 16)
                        self.dL.n += 16
                g.dma_start(self.atin[0:1, 0:8], self.bselv).then_inc(self.dL.sem, 16)
                self.dL.n += 16
                g.wait_ge(self.dL.sem, self.dL.n)

        msb = self.a4flat[:, 14336:16384].bitcast(F32).rearrange(
            "p (h d) -> p h d", h=8)
        bt = self.t0.n
        with nc.Block() as blk:
            @blk.tensor
            def _(t):
                for h in range(NB):
                    p = (self.pA, self.pB, self.pC, self.pD)[h // 2]
                    sl = slice(128 * (h % 2), 128 * (h % 2) + 128)
                    self.t0.inc(t.matmul(p[:, sl], self.wv_w[:, 0, h, :],
                                         self.wv_w[:, 1, h, :], start=True, stop=True))
            @blk.vector
            def _(v):
                for h in range(NB):
                    v.wait_ge(self.t0.sem, bt + h + 1)
                    p = (self.pA, self.pB, self.pC, self.pD)[h // 2]
                    sl = slice(128 * (h % 2), 128 * (h % 2) + 128)
                    v.tensor_scalar(msb[:, h, :], p[:, sl], 1.0 / float(S),
                                    None, ALU.mult)

        xbat1 = self.a2f[:, :, :].rearrange("p a (c b) -> p (a c) b", b=1024)
        xbat2 = self.a3f[:, :, :].rearrange("p a (c b) -> p (a c) b", b=1024)
        xb = lambda tt: xbat1[:, tt, :] if tt < 8 else xbat2[:, tt - 8, :]
        st1 = self.asml[:, 0:16]
        st2 = self.asml[:, 16:32]
        sta = self.asml[:, 32:48]
        stb = self.asml[:, 48:64]
        prcol = self.asml[:, 64:72]
        sqscr = self.a5f[:, 0, 0:1024]
        ytile = self.a5[:, 2, :].rearrange("p (a b) -> p a b", b=1024)

        def batch_stats_block():
            bv1, bs1 = self.v1.n, self.s1.n
            with nc.Block() as blk:
                @blk.vector
                def _(v):
                    v.wait_ge(self.v1.sem, self.v1.n)
                    self.v1.inc(v.tensor_scalar(st1[:, :], st1[:, :],
                                                1.0 / 1024.0, None, ALU.mult))
                    v.wait_ge(self.v1.sem, self.v1.n)
                    self.v1.inc(v.tensor_scalar(st2[:, :], st2[:, :],
                                                1.0 / 1024.0, None, ALU.mult))
                    v.wait_ge(self.v1.sem, self.v1.n)
                    self.v1.inc(v.tensor_mul(sta[:, :], st1[:, :], st1[:, :]))
                    v.wait_ge(self.v1.sem, self.v1.n)
                    self.v1.inc(v.tensor_sub(sta[:, :], st2[:, :], sta[:, :]))
                    v.wait_ge(self.s1.sem, bs1 + 2)
                    self.v1.inc(v.scalar_tensor_tensor(
                        stb[:, :], st1[:, :], -1.0, sta[:, :],
                        ALU.mult, ALU.mult))
                @blk.scalar
                def _(s):
                    s.wait_ge(self.v1.sem, bv1 + 4)
                    self.s1.inc(s.activation(sta[:, :], sta[:, :], AF.Ln,
                                             bias=self.ceps[:, 0:1]))
                    s.wait_ge(self.s1.sem, self.s1.n)
                    self.s1.inc(s.activation(sta[:, :], sta[:, :], AF.Exp,
                                             scale=-0.5))

        ycons = [None]

        def load_and_rowstats(src_ap):
            ldv = []
            bprev = ycons[0]
            with nc.Block() as blk:
                @blk.sync
                def _(sy):
                    for tt in range(NT):
                        dp = self.dp[2 + tt % 4]
                        if tt >= 4:
                            sy.wait_ge(dp.sem, ldv[tt - 4])
                        if bprev is not None:
                            sy.wait_ge(self.s0.sem, bprev + tt + 1)
                        sy.dma_start(xb(tt), src_ap(tt)).then_inc(dp.sem, 16)
                        dp.n += 16
                        ldv.append(dp.n)
                @blk.vector
                def _(v):
                    for tt in range(NT):
                        v.wait_ge(self.dp[2 + tt % 4].sem, ldv[tt])
                        v.reduce_sum(st1[:, tt:tt + 1], xb(tt), axis=AX.X)
                @blk.scalar
                def _(s):
                    for tt in range(NT):
                        s.wait_ge(self.dp[2 + tt % 4].sem, ldv[tt])
                        s.wait_ge(self.s1.sem, self.s1.n)
                        self.s1.inc(s.activation(sqscr, xb(tt), AF.Square,
                                                 accum_out=st2[:, tt:tt + 1]))


        V0B, S1B, V1B, S0B, T0B = (self.v0.n, self.s1.n, self.v1.n,
                                   self.s0.n, self.t0.n)
        V2B = self.v2.n
        ldv = []
        with nc.Block() as blk:
            @blk.sync
            def _(sy):
                for j in range(8):
                    for tt in range(NT):
                        idx = 16 * j + tt
                        dp = self.dp[tt % 8]
                        if idx >= 8:
                            sy.wait_ge(dp.sem, ldv[idx - 8])
                        if j >= 1:
                            sy.wait_ge(self.s0.sem, S0B + 16 * (j - 1) + tt + 1)
                            sy.wait_ge(self.v0.sem, V0B + 16 * (j - 1) + tt + 1)
                        sy.dma_start(xb(tt),
                                     self.xall[j, 128 * tt:128 * (tt + 1), :]) \
                            .then_inc(dp.sem, 16)
                        dp.n += 16
                        ldv.append(dp.n)
            @blk.vector
            def _(v):
                for j in range(8):
                    for tt in range(NT):
                        v.wait_ge(self.dp[tt % 8].sem, ldv[16 * j + tt])
                        self.v0.inc(v.reduce_sum(st1[:, tt:tt + 1], xb(tt),
                                                 axis=AX.X))
                    def vch(inst):
                        self.v1.inc(inst)
                        v.wait_ge(self.v1.sem, self.v1.n)
                    v.wait_ge(self.v0.sem, V0B + 16 * j + 16)
                    vch(v.tensor_scalar(st1[:, :], st1[:, :], 1.0 / 1024.0,
                                        None, ALU.mult))
                    v.wait_ge(self.s1.sem, S1B + 18 * j + 16)
                    vch(v.tensor_scalar(st2[:, :], st2[:, :], 1.0 / 1024.0,
                                        None, ALU.mult))
                    vch(v.tensor_mul(sta[:, :], st1[:, :], st1[:, :]))
                    vch(v.tensor_sub(sta[:, :], st2[:, :], sta[:, :]))
                    v.wait_ge(self.s1.sem, S1B + 18 * j + 18)
                    vch(v.scalar_tensor_tensor(stb[:, :], st1[:, :], -1.0,
                                               sta[:, :], ALU.mult, ALU.mult))
                    v.wait_ge(self.t0.sem, T0B + 16 * j + 16)
                    for h in range(NB):
                        vch(v.tensor_mul(self.ass[:, h % 2, 0:128],
                                         self.gslot(h), msb[:, h, :]))
                        vch(v.reduce_sum(prcol[:, h:h + 1],
                                         self.ass[:, h % 2, 0:128], axis=AX.X))
            @blk.scalar
            def _(s):
                for j in range(8):
                    for tt in range(NT):
                        s.wait_ge(self.dp[tt % 8].sem, ldv[16 * j + tt])
                        s.wait_ge(self.s1.sem, self.s1.n)
                        self.s1.inc(s.activation(sqscr, xb(tt), AF.Square,
                                                 accum_out=st2[:, tt:tt + 1]))
                    s.wait_ge(self.v1.sem, V1B + 21 * j + 4)
                    s.wait_ge(self.s1.sem, self.s1.n)
                    self.s1.inc(s.activation(sta[:, :], sta[:, :], AF.Ln,
                                             bias=self.ceps[:, 0:1]))
                    s.wait_ge(self.s1.sem, self.s1.n)
                    self.s1.inc(s.activation(sta[:, :], sta[:, :], AF.Exp,
                                             scale=-0.5))
                    s.wait_ge(self.v1.sem, V1B + 21 * j + 5)
                    for tt in range(NT):
                        if tt >= 2:
                            s.wait_ge(self.t0.sem, T0B + 16 * j + tt - 1)
                        elif j >= 1:
                            s.wait_ge(self.t0.sem, T0B + 16 * j)
                        self.s0.inc(s.activation(ytile[:, tt % 2, :], xb(tt),
                                                 AF.Identity,
                                                 bias=stb[:, tt:tt + 1],
                                                 scale=sta[:, tt:tt + 1]))
            @blk.tensor
            def _(t):
                for j in range(8):
                    if j >= 1:
                        t.wait_ge(self.v1.sem, V1B + 21 * j)
                    for tt in range(NT):
                        t.wait_ge(self.s0.sem, S0B + 16 * j + tt + 1)
                        last = None
                        for h in range(NB):
                            last = t.matmul(
                                self.gslot(h),
                                ytile[:, tt % 2, 128 * h:128 * (h + 1)],
                                ytile[:, tt % 2, 128 * h:128 * (h + 1)],
                                start=(tt == 0), stop=(tt == NT - 1))
                        self.t0.inc(last)
                    t.wait_ge(self.v1.sem, V1B + 21 * (j + 1))
                    for h in range(NB):
                        t.matmul(self.am[0:1, j:j + 1], c["ones_col_f"][:, 0:1],
                                 prcol[:, h:h + 1], start=(h == 0),
                                 stop=(h == NB - 1))
        ycons[0] = None

        A = self.atin

        def vstep(fn):
            with nc.Block() as blk:
                @blk.vector
                def _(v):
                    fn(v)
        with nc.Block() as blk:
            @blk.scalar
            def _(s):
                s.activation(A[0:1, 8:16], self.am[0:1, 0:8], AF.Exp, scale=-10.0 / LS)
        vstep(lambda v: v.tensor_scalar(A[0:1, 8:16], A[0:1, 8:16], 1.0, None, ALU.add))
        vstep(lambda v: v.reciprocal(A[0:1, 16:24], A[0:1, 8:16]))
        vstep(lambda v: (v.reduce_sum(A[0:1, 24:25], A[0:1, 16:24], axis=AX.X),
                         v.tensor_mul(A[0:1, 32:40], A[0:1, 16:24], A[0:1, 0:8])))
        vstep(lambda v: (v.tensor_scalar(A[0:1, 25:26], A[0:1, 24:25], 0.125,
                                         None, ALU.mult),
                         v.reduce_sum(A[0:1, 26:27], A[0:1, 32:40], axis=AX.X)))
        vstep(lambda v: (v.tensor_scalar(A[0:1, 27:28], A[0:1, 25:26], 0.99,
                                         None, ALU.is_gt),
                         v.tensor_scalar(A[0:1, 28:29], A[0:1, 25:26], 0.01,
                                         None, ALU.is_lt)))
        vstep(lambda v: (v.tensor_scalar(A[0:1, 29:30], A[0:1, 28:29], -1.0, 1.0,
                                         ALU.mult, ALU.add),
                         v.tensor_scalar(A[0:1, 31:32], A[0:1, 27:28], -1.0, 1.0,
                                         ALU.mult, ALU.add)))
        vstep(lambda v: v.tensor_mul(A[0:1, 30:31], A[0:1, 29:30], A[0:1, 26:27]))
        vstep(lambda v: v.tensor_mul(A[0:1, 30:31], A[0:1, 31:32], A[0:1, 30:31]))
        vstep(lambda v: v.tensor_add(A[0:1, 40:41], A[0:1, 30:31], A[0:1, 27:28]))
        vstep(lambda v: (v.tensor_scalar(A[0:1, 41:42], A[0:1, 40:41], -1.0, 1.0,
                                         ALU.mult, ALU.add),
                         v.tensor_scalar(A[0:1, 42:43], A[0:1, 40:41], 0.1,
                                         None, ALU.mult)))
        with nc.Block() as blk:
            @blk.tensor
            def _(t):
                t.matmul(self.pA[:, 0:1], c["ones_row_f"][0:1, :], A[0:1, 41:42],
                         start=True, stop=True)
                t.matmul(self.pA[:, 1:2], c["ones_row_f"][0:1, :], A[0:1, 42:43],
                         start=True, stop=True)
        vstep(lambda v: v.tensor_copy(self.asml[:, 128:130], self.pA[:, 0:2]))
        self.w1m = self.asml[:, 128:129]
        self.w01 = self.asml[:, 129:130]

        load_and_rowstats(lambda tt: self.xown[128 * tt:128 * (tt + 1), :])
        batch_stats_block()
        yown = self.a1[:, :, :].rearrange("p a (c b) -> p (a c) b", b=1024)
        with nc.Block() as blk:
            @blk.scalar
            def _(s):
                for tt in range(NT):
                    s.activation(yown[:, tt, :], xb(tt), AF.Identity,
                                 bias=stb[:, tt:tt + 1], scale=sta[:, tt:tt + 1])
        xrow = self.ag[:, :, :]
        bt, bv = self.t0.n, self.v0.n
        stv = []
        with nc.Block() as blk:
            @blk.sync
            def _(sy):
                for pt in range(NPT):
                    dp = self.dp[pt % 2]
                    if pt >= 2:
                        sy.wait_ge(dp.sem, stv[pt - 2])
                    sy.wait_ge(self.v0.sem, bv + (pt + 1) * NT)
                    sy.dma_start(self.hx[0][:, pt, :], xrow[:, pt % 2, :]) \
                        .then_inc(dp.sem, 16)
                    dp.n += 16
                    stv.append(dp.n)
            @blk.tensor
            def _(t):
                for pt in range(NPT):
                    for tt in range(NT):
                        k = pt * NT + tt
                        p = (self.pA, self.pB, self.pC, self.pD)[k % 4]
                        if k >= 4:
                            t.wait_ge(self.v0.sem, bv + k - 3)
                        self.t0.inc(t.transpose(p[:, 0:128],
                                                xb(tt)[:, 128 * pt:128 * (pt + 1)],
                                                c["ident_f"][:, :]))
            @blk.vector
            def _(v):
                for pt in range(NPT):
                    for tt in range(NT):
                        k = pt * NT + tt
                        v.wait_ge(self.t0.sem, bt + k + 1)
                        if tt == 0 and pt >= 2:
                            v.wait_ge(self.dp[pt % 2].sem, stv[pt - 2])
                        p = (self.pA, self.pB, self.pC, self.pD)[k % 4]
                        self.v0.inc(v.tensor_copy(
                            xrow[:, pt % 2, 128 * tt:128 * (tt + 1)], p[:, 0:128]))
        self.hx0_done = (self.dp[0].n, self.dp[1].n)
        self.yhat = self.a2
        pAb = self.pA[:, :].bitcast(BF16)
        pBb = self.pB[:, :].bitcast(BF16)
        pCb = self.pC[:, :].bitcast(BF16)
        pDb = self.pD[:, :].bitcast(BF16)
        bt, bv = self.t0.n, self.v0.n
        with nc.Block() as blk:
            @blk.tensor
            def _(t):
                for tt in range(NT):
                    for pt in range(NPT):
                        k = tt * NPT + pt
                        pb = (pAb, pBb, pCb, pDb)[k % 4]
                        if k >= 4:
                            t.wait_ge(self.v0.sem, bv + k - 3)
                        self.t0.inc(t.transpose(pb[:, 0:128],
                                                yown[:, tt, 128 * pt:128 * (pt + 1)],
                                                c["ident_b"][:, :]))
            @blk.vector
            def _(v):
                for tt in range(NT):
                    for pt in range(NPT):
                        k = tt * NPT + pt
                        v.wait_ge(self.t0.sem, bt + k + 1)
                        pb = (pAb, pBb, pCb, pDb)[k % 4]
                        self.v0.inc(v.tensor_copy(
                            self.yhat[:, pt, 128 * tt:128 * (tt + 1)], pb[:, 0:128]))

    def attn_a(self):
        nc, c = self.nc, self.c
        awv = self.aw[:, :, :].rearrange("p a b -> p (a b)")
        Gm = lambda m, h: awv[:, (m * 8 + h) * 128:(m * 8 + h) * 128 + 128]
        Gp = lambda h: awv[:, (24 + h) * 128:(24 + h) * 128 + 128]
        Rm = lambda m, h: awv[:, (32 + m * 8 + h) * 128:(32 + m * 8 + h) * 128 + 128]
        bt, bv = self.t0.n, self.v0.n
        with nc.Block() as blk:
            @blk.tensor
            def _(t):
                for h in range(NB):
                    if h >= 1:
                        t.wait_ge(self.v0.sem, bv + 7 * h)
                    last = None
                    for m in range(3):
                        p = (self.pA, self.pB, self.pC)[m]
                        t.matmul(p[:, 0:128], self.wv_w[:, m, h, :],
                                 self.wv_d[:, m, h, :], start=True, stop=True)
                        t.matmul(self.pD[:, 0:128], self.wv_w[:, m, h, :],
                                 self.wv_d[:, m, h, :], start=(m == 0),
                                 stop=(m == 2))
                        pe = self.pE if m < 2 else self.pF
                        last = t.matmul(
                            pe[:, 128 * (m % 2):128 * (m % 2) + 128],
                            self.wv_w[:, m, h, :], c["ones_b"][:, :],
                            start=True, stop=True)
                    self.t0.inc(last)
            @blk.vector
            def _(v):
                for h in range(NB):
                    v.wait_ge(self.t0.sem, bt + h + 1)
                    for m in range(3):
                        p = (self.pA, self.pB, self.pC)[m]
                        self.v0.inc(v.tensor_copy(Gm(m, h), p[:, 0:128]))
                        pe = self.pE if m < 2 else self.pF
                        self.v0.inc(v.tensor_scalar(
                            Rm(m, h), pe[:, 128 * (m % 2):128 * (m % 2) + 128],
                            self.w01[:, 0:1], None, ALU.mult))
                    self.v0.inc(v.scalar_tensor_tensor(
                        Gp(h), self.pD[:, 0:128], self.w1m[:, 0:1],
                        c["ident_f"][:, :], ALU.mult, ALU.add))

        ssm = lambda pt: (self.a3f[:, pt, :] if pt < 4 else self.a4f[:, pt - 4, :])
        yh = self.yhat
        ymt = self.a5[:, 0:3, :].rearrange("p a (cc b) -> p (a cc) b", b=512)
        NK = NB * NCH
        bt0, bt1, bv1, bs0 = self.t0.n, self.t1.n, self.v1.n, self.s0.n
        with nc.Block() as blk:
            @blk.tensor
            def _(t):
                for h in range(NB):
                    for ch in range(NCH):
                        k = h * NCH + ch
                        sl = slice(512 * ch, 512 * (ch + 1))
                        if k >= 1:
                            t.wait_ge(self.v1.sem, bv1 + k)
                        last = None
                        for m in range(3):
                            p = (self.pA, self.pB, self.pC)[m]
                            last = t.matmul(p[:, :], Rm(m, h), yh[:, h, sl],
                                            start=True, stop=True)
                        self.t0.inc(last)
                        t.wait_ge(self.v1.sem, bv1 + (k + 1))
                        if k >= 2:
                            t.wait_ge(self.s0.sem, bs0 + k - 1)
                        po = self.pD if k % 2 == 0 else self.pE
                        t.matmul(po[:, :], Gp(h), yh[:, h, sl], start=True, stop=False)
                        for m in range(3):
                            mm = t.matmul(po[:, :], Gm(m, h),
                                          ymt[:, m * 4 + (k % 4), :],
                                          start=False, stop=(m == 2))
                            if m == 2:
                                self.t1.inc(mm)
            @blk.vector
            def _(v):
                for h in range(NB):
                    for ch in range(NCH):
                        k = h * NCH + ch
                        sl = slice(512 * ch, 512 * (ch + 1))
                        v.wait_ge(self.t0.sem, bt0 + k + 1)
                        if k >= 4:
                            v.wait_ge(self.t1.sem, bt1 + k - 3)
                        last = None
                        for m in range(3):
                            p = (self.pA, self.pB, self.pC)[m]
                            last = v.tensor_mul(ymt[:, m * 4 + (k % 4), :],
                                                p[:, :], yh[:, h, sl])
                        self.v1.inc(last)
            @blk.scalar
            def _(s):
                for h in range(NB):
                    for ch in range(NCH):
                        k = h * NCH + ch
                        sl = slice(512 * ch, 512 * (ch + 1))
                        s.wait_ge(self.t1.sem, bt1 + k + 1)
                        po = self.pD if k % 2 == 0 else self.pE
                        self.s0.inc(s.copy(ssm(h)[:, sl], po[:, :]))

        wo = self.aw
        with nc.Block() as blk:
            @blk.gpsimd
            def _(g):
                g.dma_start(wo[:, :, :],
                            self.aWo.rearrange("(a p) o -> p a o", p=128)) \
                    .then_inc(self.dL.sem, 16)
                self.dL.n += 16

        self._ln_gamma_beta(ssm)
        wotot = self.dL.n

        normed = self.a1
        xch = self.ag[:, 0, :].rearrange("p (a b) -> p a b", b=512)
        och = self.ag[:, 1, :].rearrange("p (a b) -> p a b", b=512)
        bt, bv = self.t0.n, self.v0.n
        ldv = []
        stv = []
        with nc.Block() as blk:
            @blk.sync
            def _(sy):
                sy.wait_ge(self.dp[0].sem, self.hx0_done[0])
                sy.wait_ge(self.dp[1].sem, self.hx0_done[1])
                for k in range(32):
                    ot, ch = divmod(k, NCH)
                    dp = self.dp[2 + k % 4]
                    if k >= 4:
                        sy.wait_ge(dp.sem, ldv[k - 4])
                        sy.wait_ge(self.v0.sem, bv + k - 3)
                    sy.dma_start(xch[:, k % 4, :],
                                 self.hx[0][:, ot, 512 * ch:512 * (ch + 1)]) \
                        .then_inc(dp.sem, 16)
                    dp.n += 16
                    ldv.append(dp.n)
            @blk.scalar
            def _(sy):
                for k in range(32):
                    ot, ch = divmod(k, NCH)
                    dp = self.dp[6 + k % 2]
                    sy.wait_ge(self.v0.sem, bv + k + 1)
                    sy.dma_start(self.hx[1][:, ot, 512 * ch:512 * (ch + 1)],
                                 och[:, k % 2, :]).then_inc(dp.sem, 16)
                    dp.n += 16
                    stv.append(dp.n)
            @blk.tensor
            def _(t):
                t.wait_ge(self.dL.sem, wotot)
                for k in range(32):
                    ot, ch = divmod(k, NCH)
                    po = self.pA if k % 2 == 0 else self.pB
                    if k >= 2:
                        t.wait_ge(self.v0.sem, bv + k - 1)
                    last = None
                    for kt in range(NPT):
                        last = t.matmul(po[:, :], wo[:, kt, 128 * ot:128 * (ot + 1)],
                                        normed[:, kt, 512 * ch:512 * (ch + 1)],
                                        start=(kt == 0), stop=(kt == NPT - 1))
                    self.t0.inc(last)
            @blk.vector
            def _(v):
                for k in range(32):
                    po = self.pA if k % 2 == 0 else self.pB
                    v.wait_ge(self.t0.sem, bt + k + 1)
                    v.wait_ge(self.dp[2 + k % 4].sem, ldv[k])
                    if k >= 2:
                        v.wait_ge(self.dp[6 + k % 2].sem, stv[k - 2])
                    self.v0.inc(v.tensor_add(och[:, k % 2, :], xch[:, k % 4, :],
                                             po[:, :]))
        self.hx1_done = [(self.dp[6], self.dp[6].n), (self.dp[7], self.dp[7].n)]

    def _ln_gamma_beta(self, ssm):
        nc, c = self.nc, self.c
        gb = self.asml[:, 130:146]
        with nc.Block() as blk:
            @blk.gpsimd
            def _(g):
                g.dma_start(gb[:, 0:8],
                            self.aLg.rearrange("o (a p) -> p (o a)", p=128),
                            allow_slow_non_contiguous=True).then_inc(self.dL.sem, 16)
                g.dma_start(gb[:, 8:16],
                            self.aLb.rearrange("o (a p) -> p (o a)", p=128),
                            allow_slow_non_contiguous=True).then_inc(self.dL.sem, 16)
                self.dL.n += 32
                g.wait_ge(self.dL.sem, self.dL.n)
        xbfc = self.a5[:, 0:2, :].rearrange("p a (cc b) -> p (a cc) b", b=512)
        sqc = self.a5[:, 2:4, :].rearrange("p a (cc b) -> p (a cc) b", b=512)
        mus = self.af[:, 0, :]
        rst = self.af[:, 1, :]
        for ch in range(NCH):
            sl = slice(512 * ch, 512 * (ch + 1))
            with nc.Block() as blk:
                @blk.vector
                def _(v):
                    for pt in range(NPT):
                        v.tensor_copy(xbfc[:, pt, :], ssm(pt)[:, sl])
                @blk.scalar
                def _(s):
                    for pt in range(NPT):
                        s.activation(sqc[:, pt, :], ssm(pt)[:, sl], AF.Square)
            with nc.Block() as blk:
                @blk.tensor
                def _(t):
                    for pt in range(NPT):
                        t.matmul(self.pA[:, :], c["odiv"][:, :], xbfc[:, pt, :],
                                 start=(pt == 0), stop=(pt == NPT - 1))
                    for pt in range(NPT):
                        t.matmul(self.pB[:, :], c["odiv"][:, :], sqc[:, pt, :],
                                 start=(pt == 0), stop=(pt == NPT - 1))
            with nc.Block() as blk:
                @blk.vector
                def _(v):
                    v.tensor_copy(mus[:, sl], self.pA[:, :])
            with nc.Block() as blk:
                @blk.vector
                def _(v):
                    v.tensor_mul(self.ass[:, 0, :], mus[:, sl], mus[:, sl])
            with nc.Block() as blk:
                @blk.vector
                def _(v):
                    v.tensor_sub(rst[:, sl], self.pB[:, :], self.ass[:, 0, :])
        with nc.Block() as blk:
            @blk.scalar
            def _(s):
                s.activation(rst[:, :], rst[:, :], AF.Ln, bias=self.ceps[:, 0:1])
        with nc.Block() as blk:
            @blk.scalar
            def _(s):
                s.activation(rst[:, :], rst[:, :], AF.Exp, scale=-0.5)
        with nc.Block() as blk:
            @blk.vector
            def _(v):
                for pt in range(NPT):
                    for ch in range(NCH):
                        sl = slice(512 * ch, 512 * (ch + 1))
                        v.wait_ge(self.v1.sem, self.v1.n)
                        self.v1.inc(v.tensor_sub(self.ass[:, ch % 2, :],
                                                 ssm(pt)[:, sl], mus[:, sl]))
                        v.wait_ge(self.v1.sem, self.v1.n)
                        self.v1.inc(v.tensor_mul(self.ass[:, ch % 2, :],
                                                 self.ass[:, ch % 2, :], rst[:, sl]))
                        v.wait_ge(self.v1.sem, self.v1.n)
                        self.v1.inc(v.tensor_scalar(
                            self.a1[:, pt, sl], self.ass[:, ch % 2, :],
                            gb[:, pt:pt + 1], gb[:, 8 + pt:9 + pt],
                            ALU.mult, ALU.add))

    def _ln_stats_stream(self, src, done):
        nc, c = self.nc, self.c
        xc = self.ag[:, :, :].rearrange("p a (cc b) -> p (a cc) b", b=512)
        xbfc = self.a5[:, 0:2, :].rearrange("p a (cc b) -> p (a cc) b", b=512)
        mus = self.af[:, 0, :]
        rst = self.af[:, 1, :]
        for ch in range(NCH):
            sl = slice(512 * ch, 512 * (ch + 1))
            with nc.Block() as blk:
                @blk.sync
                def _(sy):
                    if ch == 0 and done:
                        for ctr, val in done:
                            sy.wait_ge(ctr.sem, val)
                    dp = self.dp[0]
                    sy.dma_start(xc[:, 0:8, :], src[:, :, sl]) \
                        .then_inc(dp.sem, 16)
                    dp.n += 16
                tot = self.dp[0].n
                @blk.vector
                def _(v):
                    v.wait_ge(self.dp[0].sem, tot)
                    for pt in range(NPT):
                        v.tensor_copy(xbfc[:, pt, :], xc[:, pt, :])
            with nc.Block() as blk:
                @blk.tensor
                def _(t):
                    for pt in range(NPT):
                        t.matmul(self.pA[:, :], c["odiv"][:, :], xbfc[:, pt, :],
                                 start=(pt == 0), stop=(pt == NPT - 1))
            with nc.Block() as blk:
                @blk.scalar
                def _(s):
                    for pt in range(NPT):
                        s.wait_ge(self.s1.sem, self.s1.n)
                        self.s1.inc(s.activation(xbfc[:, pt, :], xc[:, pt, :],
                                                 AF.Square))
            with nc.Block() as blk:
                @blk.tensor
                def _(t):
                    for pt in range(NPT):
                        t.matmul(self.pB[:, :], c["odiv"][:, :], xbfc[:, pt, :],
                                 start=(pt == 0), stop=(pt == NPT - 1))
            with nc.Block() as blk:
                @blk.vector
                def _(v):
                    v.tensor_copy(mus[:, sl], self.pA[:, :])
            with nc.Block() as blk:
                @blk.vector
                def _(v):
                    v.tensor_mul(self.ass[:, 0, :], mus[:, sl], mus[:, sl])
            with nc.Block() as blk:
                @blk.vector
                def _(v):
                    v.tensor_sub(rst[:, sl], self.pB[:, :], self.ass[:, 0, :])
        with nc.Block() as blk:
            @blk.scalar
            def _(s):
                s.activation(rst[:, :], rst[:, :], AF.Ln, bias=self.ceps[:, 0:1])
        with nc.Block() as blk:
            @blk.scalar
            def _(s):
                s.activation(rst[:, :], rst[:, :], AF.Exp, scale=-0.5)

    def _make_z_rms(self, src, w_dram, dst):
        nc, c = self.nc, self.c
        wcol = self.asml[:, 146:154]
        with nc.Block() as blk:
            @blk.gpsimd
            def _(g):
                g.dma_start(wcol[:, 0:8],
                            w_dram.rearrange("o (a p) -> p (o a)", p=128),
                            allow_slow_non_contiguous=True).then_inc(self.dL.sem, 16)
                self.dL.n += 16
                g.wait_ge(self.dL.sem, self.dL.n)
        xc = self.ag[:, :, :].rearrange("p a (cc b) -> p (a cc) b", b=512)
        a5s = self.a5[:, :, :].rearrange("p a (cc b) -> p (a cc) b", b=512)
        y2bf = lambda pt: a5s[:, pt, :]
        sqs = lambda r: a5s[:, 14 + r, :]
        rstd2 = self.a5f[:, 1, 1024:1536]
        mus = self.af[:, 0, :]
        rst = self.af[:, 1, :]
        for ch in range(NCH):
            sl = slice(512 * ch, 512 * (ch + 1))
            with nc.Block() as blk:
                @blk.sync
                def _(sy):
                    dp = self.dp[0]
                    sy.dma_start(xc[:, 0:8, :], src[:, :, sl]) \
                        .then_inc(dp.sem, 16)
                    dp.n += 16
                tot = self.dp[0].n
                @blk.vector
                def _(v):
                    v.wait_ge(self.dp[0].sem, tot)
                    for pt in range(NPT):
                        v.wait_ge(self.v1.sem, self.v1.n)
                        self.v1.inc(v.tensor_sub(y2bf(pt), xc[:, pt, :],
                                                 mus[:, sl]))
                        v.wait_ge(self.v1.sem, self.v1.n)
                        self.v1.inc(v.tensor_mul(y2bf(pt), y2bf(pt), rst[:, sl]))
            bs1, bt0 = self.s1.n, self.t0.n
            with nc.Block() as blk:
                @blk.scalar
                def _(s):
                    for pt in range(NPT):
                        if pt >= 2:
                            s.wait_ge(self.t0.sem, bt0 + pt - 1)
                        self.s1.inc(s.activation(sqs(pt % 2), y2bf(pt), AF.Square))
                @blk.tensor
                def _(t):
                    for pt in range(NPT):
                        t.wait_ge(self.s1.sem, bs1 + pt + 1)
                        self.t0.inc(t.matmul(self.pA[:, :], c["odiv"][:, :],
                                             sqs(pt % 2), start=(pt == 0),
                                             stop=(pt == NPT - 1)))
            with nc.Block() as blk:
                @blk.scalar
                def _(s):
                    s.activation(rstd2, self.pA[:, :], AF.Ln,
                                 bias=self.ceps[:, 1:2])
            with nc.Block() as blk:
                @blk.scalar
                def _(s):
                    s.activation(rstd2, rstd2, AF.Exp, scale=-0.5)
            with nc.Block() as blk:
                @blk.vector
                def _(v):
                    for pt in range(NPT):
                        v.wait_ge(self.v1.sem, self.v1.n)
                        self.v1.inc(v.tensor_mul(y2bf(pt), y2bf(pt), rstd2))
                        v.wait_ge(self.v1.sem, self.v1.n)
                        self.v1.inc(v.tensor_scalar(dst[:, pt, sl], y2bf(pt),
                                                    wcol[:, pt:pt + 1], None,
                                                    ALU.mult))

    def _front_fused(self, src, done, w_dram, dst, plain=False):
        nc, c = self.nc, self.c
        xcA = self.ag[:, :, :].rearrange("p a (cc b) -> p (a cc) b", b=512)
        a2f = self.a2[:, :, :].rearrange("p a b -> p (a b)").bitcast(F32) \
            .rearrange("p (a b) -> p a b", b=512)
        xcB = a2f[:, 0:8, :]
        xbuf = lambda ch: xcA if ch % 2 == 0 else xcB
        a5s = self.a5[:, :, :].rearrange("p a (cc b) -> p (a cc) b", b=512)
        xb = lambda pt: a5s[:, pt, :]
        y2s = lambda pt: a5s[:, 8 + pt, :]
        mucf = lambda ch: self.af[:, 0, 512 * ch:512 * (ch + 1)]
        rstcf = lambda ch: self.af[:, 1, 512 * ch:512 * (ch + 1)]
        mu2 = a2f[:, 8, :]
        rstd2 = a2f[:, 9, :]
        wcol = self.asml[:, 146:154]
        VS, SS, TS = (27, 10, 2) if plain else (35, 28, 3)
        V1B, S1B, T0B = self.v1.n, self.s1.n, self.t0.n
        v1 = lambda ch, off: V1B + VS * ch + off
        s1 = lambda ch, off: S1B + SS * ch + off
        t0 = lambda ch, off: T0B + TS * ch + off
        ldv = []
        if not plain:
            self.dL.n += 16
        wctot = self.dL.n

        def vop(v, inst):
            self.v1.inc(inst)
            v.wait_ge(self.v1.sem, self.v1.n)

        def sop(s, inst):
            self.s1.inc(inst)
            s.wait_ge(self.s1.sem, self.s1.n)
        with nc.Block() as blk:
            if not plain:
                @blk.gpsimd
                def _(g):
                    g.dma_start(wcol[:, 0:8],
                                w_dram.rearrange("o (a p) -> p (o a)", p=128),
                                allow_slow_non_contiguous=True) \
                        .then_inc(self.dL.sem, 16)
            @blk.sync
            def _(sy):
                for ch in range(NCH):
                    sl = slice(512 * ch, 512 * (ch + 1))
                    dp = self.dp[ch % 2]
                    if ch == 0 and done:
                        for ctr, val in done:
                            sy.wait_ge(ctr.sem, val)
                    if ch >= 2:
                        sy.wait_ge(dp.sem, ldv[ch - 2])
                        sy.wait_ge(self.v1.sem, v1(ch - 2, 26))
                    sy.dma_start(xbuf(ch)[:, 0:8, :], src[:, :, sl]) \
                        .then_inc(dp.sem, 16)
                    dp.n += 16
                    ldv.append(dp.n)
            @blk.vector
            def _(v):
                for ch in range(NCH):
                    sl = slice(512 * ch, 512 * (ch + 1))
                    xc = xbuf(ch)
                    v.wait_ge(self.dp[ch % 2].sem, ldv[ch])
                    if ch >= 1:
                        v.wait_ge(self.t0.sem, t0(ch - 1, TS))
                    for pt in range(NPT):
                        self.v1.inc(v.tensor_copy(xb(pt), xc[:, pt, :]))
                    v.wait_ge(self.t0.sem, t0(ch, 1))
                    vop(v, v.tensor_copy(mucf(ch), self.pA[:, :]))
                    vop(v, v.tensor_mul(mu2, mucf(ch), mucf(ch)))
                    v.wait_ge(self.t0.sem, t0(ch, 2))
                    vop(v, v.tensor_sub(rstcf(ch), self.pB[:, :], mu2))
                    v.wait_ge(self.s1.sem, s1(ch, 10))
                    if plain:
                        for pt in range(NPT):
                            vop(v, v.tensor_sub(y2s(pt), xc[:, pt, :],
                                                mucf(ch)))
                            vop(v, v.tensor_mul(dst[:, pt, sl], y2s(pt),
                                                rstcf(ch)))
                        continue
                    for pt in range(NPT):
                        vop(v, v.tensor_sub(y2s(pt), xc[:, pt, :], mucf(ch)))
                        vop(v, v.tensor_mul(y2s(pt), y2s(pt), rstcf(ch)))
                    v.wait_ge(self.s1.sem, s1(ch, 20))
                    for pt in range(NPT):
                        vop(v, v.tensor_mul(y2s(pt), y2s(pt), rstd2))
            @blk.tensor
            def _(t):
                for ch in range(NCH):
                    t.wait_ge(self.v1.sem, v1(ch, 8))
                    for pt in range(NPT):
                        last = t.matmul(self.pA[:, :], c["odiv"][:, :], xb(pt),
                                        start=(pt == 0), stop=(pt == NPT - 1))
                    self.t0.inc(last)
                    t.wait_ge(self.s1.sem, s1(ch, 8))
                    for pt in range(NPT):
                        last = t.matmul(self.pB[:, :], c["odiv"][:, :], xb(pt),
                                        start=(pt == 0), stop=(pt == NPT - 1))
                    self.t0.inc(last)
                    if plain:
                        continue
                    for pt in range(NPT):
                        t.wait_ge(self.s1.sem, s1(ch, 11 + pt))
                        last = t.matmul(self.pD[:, :], c["odiv"][:, :], xb(pt),
                                        start=(pt == 0), stop=(pt == NPT - 1))
                    self.t0.inc(last)
            @blk.scalar
            def _(s):
                for ch in range(NCH):
                    sl = slice(512 * ch, 512 * (ch + 1))
                    s.wait_ge(self.t0.sem, t0(ch, 1))
                    for pt in range(NPT):
                        s.wait_ge(self.v1.sem, v1(ch, pt + 1))
                        sop(s, s.activation(xb(pt), xb(pt), AF.Square))
                    s.wait_ge(self.v1.sem, v1(ch, 11))
                    sop(s, s.activation(rstcf(ch), rstcf(ch), AF.Ln,
                                        bias=self.ceps[:, 0:1]))
                    sop(s, s.activation(rstcf(ch), rstcf(ch), AF.Exp,
                                        scale=-0.5))
                    if plain:
                        continue
                    s.wait_ge(self.t0.sem, t0(ch, 2))
                    for pt in range(NPT):
                        s.wait_ge(self.v1.sem, v1(ch, 13 + 2 * pt))
                        sop(s, s.activation(xb(pt), y2s(pt), AF.Square))
                    s.wait_ge(self.t0.sem, t0(ch, 3))
                    if ch >= 1:
                        s.wait_ge(self.v1.sem, v1(ch - 1, 35))
                    sop(s, s.activation(rstd2, self.pD[:, :], AF.Ln,
                                        bias=self.ceps[:, 1:2]))
                    sop(s, s.activation(rstd2, rstd2, AF.Exp, scale=-0.5))
                    if ch == 0:
                        s.wait_ge(self.dL.sem, wctot)
                    for pt in range(NPT):
                        s.wait_ge(self.v1.sem, v1(ch, 28 + pt))
                        sop(s, s.activation(dst[:, pt, sl], y2s(pt),
                                            AF.Identity,
                                            scale=wcol[:, pt:pt + 1]))

    def ffn(self, xin, xin_done, xout):
        nc = self.nc
        wkbuf = self.a4flat[:, 0:8192].rearrange("p (a b) -> p a b", b=1024)
        with nc.Block() as blk:
            @blk.gpsimd
            def _(g):
                g.dma_start(self.aw[:, :, :],
                            self.fwq.rearrange("(a p) n -> p a n", p=128)) \
                    .then_inc(self.dL.sem, 16)
                self.dL.n += 16
                ck = self.sw[12]
                g.dma_start(wkbuf[:, :, :],
                            self.fwk.rearrange("(a p) n -> p a n", p=128)) \
                    .then_inc(ck.sem, 16)
                ck.n += 16
        self._front_fused(xin, xin_done, self.fAnw, self.a1)
        self._rope_tables()
        self._qkv_rope()
        self._attention()
        self._wo_h_xp(xin, xout)
        self._make_z2()
        return self._moe(xout)

    def _rope_tables(self):
        nc, c = self.nc, self.c
        fct = self.a5f[:, 0, :][0:64, :]
        ft = self.a5f[:, 1, 0:128].rearrange("p (a b) -> p a b", b=64)
        ldv = []
        bt, bv = self.t0.n, self.v0.n
        with nc.Block() as blk:
            @blk.sync
            def _(sy):
                for tt in range(NT):
                    dp = self.dp[1 + tt % 2]
                    if tt >= 2:
                        sy.wait_ge(dp.sem, ldv[tt - 2])
                        sy.wait_ge(self.t0.sem, bt + tt - 1)
                    sy.dma_start(ft[:, tt % 2, :],
                                 self.ffc[128 * tt:128 * (tt + 1), :]) \
                        .then_inc(dp.sem, 16)
                    dp.n += 16
                    ldv.append(dp.n)
            @blk.tensor
            def _(t):
                for tt in range(NT):
                    t.wait_ge(self.dp[1 + tt % 2].sem, ldv[tt])
                    if tt >= 2:
                        t.wait_ge(self.v0.sem, bv + tt - 1)
                    self.t0.inc(t.transpose(self.pA[0:64, 0:128] if tt % 2 == 0
                                            else self.pB[0:64, 0:128],
                                            ft[:, tt % 2, :], c["ident_f"][:, :]))
            @blk.vector
            def _(v):
                for tt in range(NT):
                    v.wait_ge(self.t0.sem, bt + tt + 1)
                    p = self.pA if tt % 2 == 0 else self.pB
                    self.v0.inc(v.tensor_copy(fct[:, 128 * tt:128 * (tt + 1)],
                                              p[0:64, 0:128]))
        bt, bv = self.t0.n, self.v0.n
        with nc.Block() as blk:
            @blk.tensor
            def _(t):
                for ch in range(NCH):
                    sl = slice(512 * ch, 512 * (ch + 1))
                    if ch >= 1:
                        t.wait_ge(self.v0.sem, bv + 2 * ch)
                    t.matmul(self.pC[:, :], c["rsel"][:, 0:128], fct[:, sl],
                             start=True, stop=True)
                    self.t0.inc(t.matmul(self.pD[:, :], c["rsel"][:, 128:256],
                                         fct[:, sl], start=True, stop=True))
            @blk.vector
            def _(v):
                for ch in range(NCH):
                    sl = slice(512 * ch, 512 * (ch + 1))
                    v.wait_ge(self.t0.sem, bt + ch + 1)
                    self.v0.inc(v.tensor_copy(self.ag[:, 0, sl], self.pC[:, :]))
                    self.v0.inc(v.tensor_copy(self.ag[:, 1, sl], self.pD[:, :]))

    def _qkv_rope(self):
        nc, c = self.nc, self.c
        a5s = self.a5[:, :, :].rearrange("p a (cc b) -> p (a cc) b", b=512)
        z1 = self.a1
        wkbuf = self.a4flat[:, 0:8192].rearrange("p (a b) -> p a b", b=1024)
        T0B, T1B, S1B, V1B = self.t0.n, self.t1.n, self.s1.n, self.v1.n
        T0 = {"q": T0B, "k": T0B + 32, "v": T0B + 64}
        T1 = {"q": T1B, "k": T1B + 32, "v": T1B + 64}
        S1 = {"q": S1B, "k": S1B + 32, "v": S1B + 64}
        V1 = {"q": V1B, "k": V1B + 96, "v": V1B + 192}
        vtot = self.dY.n + 16
        pAb = self.pA[:, :].bitcast(BF16)
        pBb = self.pB[:, :].bitcast(BF16)
        wq_tot, wk_tot = self.dL.n, self.sw[12].n
        self.dY.n += 16
        with nc.Block() as blk:
            @blk.gpsimd
            def _(g):
                g.wait_ge(self.t0.sem, T0B + 32)
                g.dma_start(self.aw[:, :, :],
                            self.fwv.rearrange("(a p) n -> p a n", p=128)) \
                    .then_inc(self.dY.sem, 16)

            @blk.tensor
            def _(t):
                for ph, wbuf in (("q", self.aw), ("k", wkbuf)):
                    if ph == "q":
                        t.wait_ge(self.dL.sem, wq_tot)
                    else:
                        t.wait_ge(self.sw[12].sem, wk_tot)
                    for k in range(34):
                        if k < 32:
                            nt, ch = divmod(k, NCH)
                            po = self.pA if k % 2 == 0 else self.pB
                            if k >= 2:
                                t.wait_ge(self.s1.sem, S1[ph] + k - 1)
                            elif ph == "k":
                                t.wait_ge(self.s1.sem, S1["q"] + 31 + k)
                            for kt in range(NPT):
                                last = t.matmul(
                                    po[:, :],
                                    wbuf[:, kt, 128 * nt:128 * (nt + 1)],
                                    z1[:, kt, 512 * ch:512 * (ch + 1)],
                                    start=(kt == 0), stop=(kt == NPT - 1))
                            self.t0.inc(last)
                        if k >= 2:
                            j = k - 2
                            t.wait_ge(self.s1.sem, S1[ph] + j + 1)
                            ps = self.pC if j % 2 == 0 else self.pD
                            if j >= 2:
                                t.wait_ge(self.v1.sem, V1[ph] + 3 * (j - 1))
                            elif ph == "k":
                                t.wait_ge(self.v1.sem, V1["q"] + 92 + 3 * j)
                            self.t1.inc(t.matmul(ps[:, :], c["pswap"][:, :],
                                                 a5s[:, j % 2, :],
                                                 start=True, stop=True))
                t.wait_ge(self.dY.sem, vtot)
                for nt in range(NPT):
                    for ch in range(NCH):
                        k = nt * NCH + ch
                        po = self.pC if k % 2 == 0 else self.pD
                        if k >= 2:
                            t.wait_ge(self.s1.sem, S1["v"] + k - 1)
                        else:
                            t.wait_ge(self.v1.sem, V1["k"] + 92 + 3 * k)
                        for kt in range(NPT):
                            last = t.matmul(
                                po[:, :],
                                self.aw[:, kt, 128 * nt:128 * (nt + 1)],
                                z1[:, kt, 512 * ch:512 * (ch + 1)],
                                start=(kt == 0), stop=(kt == NPT - 1))
                        self.t0.inc(last)
                    t.wait_ge(self.s1.sem, S1["v"] + (nt + 1) * NCH)
                    for tb in range(NT):
                        kk = nt * NT + tb
                        pb = pAb if kk % 2 == 0 else pBb
                        if kk >= 2:
                            t.wait_ge(self.v1.sem, V1["v"] + 2 * (kk - 1))
                        else:
                            t.wait_ge(self.s1.sem, S1["k"] + 31 + kk)
                        self.t1.inc(t.transpose(
                            pb[:, 0:128],
                            a5s[:, 8 + tb // 4,
                                128 * (tb % 4):128 * (tb % 4) + 128],
                            c["ident_b"][:, :]))

            @blk.scalar
            def _(s):
                for ph in ("q", "k"):
                    for k in range(32):
                        po = self.pA if k % 2 == 0 else self.pB
                        s.wait_ge(self.t0.sem, T0[ph] + k + 1)
                        if k >= 2:
                            s.wait_ge(self.v1.sem, V1[ph] + 3 * (k - 2) + 1)
                        elif ph == "k":
                            s.wait_ge(self.v1.sem, V1["q"] + 91 + 3 * k)
                        self.s1.inc(s.copy(a5s[:, k % 2, :], po[:, :]))
                for nt in range(NPT):
                    for ch in range(NCH):
                        k = nt * NCH + ch
                        po = self.pC if k % 2 == 0 else self.pD
                        s.wait_ge(self.t0.sem, T0["v"] + k + 1)
                        if nt >= 1:
                            s.wait_ge(self.t1.sem, T1["v"] + nt * NT)
                        elif k < 2:
                            s.wait_ge(self.v1.sem, V1["k"] + 91 + 3 * k)
                        self.s1.inc(s.copy(a5s[:, 8 + ch, :], po[:, :]))

            @blk.vector
            def _(v):
                for ph, dst in (("q", self.a2), ("k", self.a3)):
                    for k in range(32):
                        nt, ch = divmod(k, NCH)
                        sl = slice(512 * ch, 512 * (ch + 1))
                        ps = self.pC if k % 2 == 0 else self.pD
                        v.wait_ge(self.t1.sem, T1[ph] + k + 1)
                        v.wait_ge(self.v1.sem, self.v1.n)
                        self.v1.inc(v.tensor_mul(a5s[:, 2 + k % 2, :],
                                                 a5s[:, k % 2, :],
                                                 self.ag[:, 0, sl]))
                        v.wait_ge(self.v1.sem, self.v1.n)
                        self.v1.inc(v.tensor_mul(a5s[:, 4 + k % 2, :],
                                                 ps[:, :], self.ag[:, 1, sl]))
                        v.wait_ge(self.v1.sem, self.v1.n)
                        self.v1.inc(v.tensor_add(dst[:, nt, sl],
                                                 a5s[:, 2 + k % 2, :],
                                                 a5s[:, 4 + k % 2, :]))
                v.wait_ge(self.t0.sem, T0B + 64)
                for tt in range(NT):
                    v.memset(self.a4[:, tt, 64::65], 1.0)
                for nt in range(NPT):
                    for tb in range(NT):
                        kk = nt * NT + tb
                        v.wait_ge(self.t1.sem, T1["v"] + kk + 1)
                        pb = pAb if kk % 2 == 0 else pBb
                        hA, hB = 2 * nt, 2 * nt + 1
                        self.v1.inc(v.tensor_copy(
                            self.a4[:, tb, 65 * hA:65 * hA + 64], pb[:, 0:64]))
                        self.v1.inc(v.tensor_copy(
                            self.a4[:, tb, 65 * hB:65 * hB + 64],
                            pb[:, 64:128]))

    def _attention(self, retention=False):
        nc, c = self.nc, self.c
        a1v = self.a1[:, :, :].rearrange("p a (cc b) -> p (a cc) b", b=1024)
        a5s = self.a5[:, :, :].rearrange("p a (cc b) -> p (a cc) b", b=512)
        agb = self.ag[:, :, :].rearrange("p a b -> p (a b)").bitcast(BF16) \
            .rearrange("p (a b) -> p a b", b=512)
        if retention:
            bufB = self.af[:, :, :].rearrange("p a b -> p (a b)").bitcast(BF16) \
                .rearrange("p (a b) -> p a b", b=512)
        else:
            bufB = self.aw[:, :, :].rearrange("p a (c b) -> p (a c) b", b=512)

        def ebuf(u, hh, kt):
            if u % 2 == 0:
                return (a5s if hh == 0 else agb)[:, kt, :]
            return bufB[:, 8 * hh + kt, :]
        qT, kT = self.a2, self.a3
        bt0, bs1, bv1, bt1, bv0, bs0 = (self.t0.n, self.s1.n, self.v1.n,
                                        self.t1.n, self.v0.n, self.s0.n)
        avslot = lambda i: (self.pC, self.pD, self.pE, self.pF)[i % 4][:, 0:65]
        denr = self.asml[:, 72:76]
        NU = (NH // 2) * NCH
        uhst = []
        for stA, stB in ((3, 0), (2, 1)):
            for hp in range(NH // 2):
                uhst += [(hp, stA), (hp, stB)]
        on_S = lambda kt, hh: (not retention) or ((2 * kt + hh) % 2 == 0)
        s1v = {}
        v1e = {}
        v1m = {}
        t0v = {}
        s1c, v1c, t0c = bs1, bv1, bt0
        eidx = {}
        gi = 0
        for u, (hp, st) in enumerate(uhst):
            for kt in range(4 * st + 4):
                t0c += 1
                for hh in range(2):
                    t0v[(u, hh, kt)] = t0c
                    eidx[gi] = (u, hh, kt)
                    gi += 1
                    if on_S(kt, hh):
                        s1c += 1
                        s1v[(u, hh, kt)] = s1c
                    else:
                        v1c += 1
                        v1e[(u, hh, kt)] = v1c
            for hh in range(2):
                for j in range(4):
                    v1c += 1
                    v1m[(u, hh, j)] = v1c

        def emit_av(t, u):
            hp, st = uhst[u]
            smax = max([s1v[k] for k in s1v if k[0] == u], default=bs1)
            t.wait_ge(self.s1.sem, smax)
            t.wait_ge(self.v1.sem, v1m[(u, 1, 3)])
            nw = 64 if retention else 65
            for hh in range(2):
                h = 2 * hp + hh
                for j in range(4):
                    qt = 4 * st + j
                    gidx = 8 * u + 4 * hh + j + 1
                    if gidx >= 5:
                        if retention:
                            t.wait_ge(self.v0.sem, bv0 + gidx - 4)
                        else:
                            t.wait_ge(self.v0.sem, bv0 + 2 * (gidx - 4))
                    av = avslot(gidx)
                    for kt in range(qt + 1):
                        last = t.matmul(av[:, 0:nw],
                                        ebuf(u, hh, kt)[:, 128 * j:128 * (j + 1)],
                                        self.a4[:, kt, 65 * h:65 * h + nw],
                                        start=(kt == 0), stop=(kt == qt))
                    self.t1.inc(last)

        with nc.Block() as blk:
            @blk.tensor
            def _(t):
                n = 0
                for u in range(NU):
                    hp, st = uhst[u]
                    pt_h = hp
                    if u >= 2:
                        emit_av(t, u - 2)
                    for kt in range(4 * st + 4):
                        if n >= 2:
                            for hh in range(2):
                                up, hhp, kp = eidx[2 * (n - 2) + hh]
                                if (up, hhp, kp) in s1v:
                                    t.wait_ge(self.s1.sem, s1v[(up, hhp, kp)])
                                else:
                                    t.wait_ge(self.v1.sem, v1e[(up, hhp, kp)])
                        pe0 = self.pA if n % 2 == 0 else self.pB
                        pe1 = self.gram[:, 0:512] if n % 2 == 0 \
                            else self.gram[:, 512:1024]
                        t.matmul(pe0[:, :],
                                 kT[0:64, pt_h, 128 * kt:128 * (kt + 1)],
                                 qT[0:64, pt_h, 512 * st:512 * (st + 1)],
                                 start=True, stop=True)
                        self.t0.inc(t.matmul(
                            pe1[:, :],
                            kT[64:128, pt_h, 128 * kt:128 * (kt + 1)],
                            qT[64:128, pt_h, 512 * st:512 * (st + 1)],
                            start=True, stop=True))
                        n += 1
                emit_av(t, NU - 2)
                emit_av(t, NU - 1)

            @blk.scalar
            def _(s):
                n = 0
                for u in range(NU):
                    hp, st = uhst[u]
                    if u >= 2:
                        s.wait_ge(self.t1.sem, bt1 + 8 * (u - 1))
                    for kt in range(4 * st + 4):
                        for hh in range(2):
                            if on_S(kt, hh):
                                s.wait_ge(self.t0.sem, t0v[(u, hh, kt)])
                                psc = (self.pA if n % 2 == 0 else self.pB) \
                                    if hh == 0 else (
                                    self.gram[:, 0:512] if n % 2 == 0
                                    else self.gram[:, 512:1024])
                                if retention:
                                    self.s1.inc(s.copy(ebuf(u, hh, kt), psc[:, :]))
                                else:
                                    self.s1.inc(s.activation(
                                        ebuf(u, hh, kt), psc[:, :],
                                        AF.Exp, scale=0.125))
                        n += 1

            @blk.vector
            def _(v):
                def post(u):
                    hp, ss = uhst[u]
                    for hh in range(2):
                        h = 2 * hp + hh
                        for j in range(4):
                            gidx = 8 * u + 4 * hh + j + 1
                            qt = 4 * ss + j
                            v.wait_ge(self.t1.sem, bt1 + gidx)
                            av = avslot(gidx)
                            if retention:
                                self.v0.inc(v.tensor_scalar(
                                    a1v[:, qt, 64 * h:64 * (h + 1)], av[:, 0:64],
                                    c["gpos"][:, h * NT + qt:h * NT + qt + 1],
                                    None, ALU.mult))
                            else:
                                self.v0.inc(v.reciprocal(
                                    denr[:, gidx % 4:gidx % 4 + 1], av[:, 64:65]))
                                v.wait_ge(self.v0.sem, self.v0.n)
                                self.v0.inc(v.tensor_scalar(
                                    a1v[:, qt, 64 * h:64 * (h + 1)], av[:, 0:64],
                                    denr[:, gidx % 4:gidx % 4 + 1],
                                    None, ALU.mult))
                n = 0
                for u in range(NU):
                    hp, st = uhst[u]
                    if retention and u >= 2:
                        v.wait_ge(self.t1.sem, bt1 + 8 * (u - 1))
                    for kt in range(4 * st + 4):
                        for hh in range(2):
                            if not on_S(kt, hh):
                                v.wait_ge(self.t0.sem, t0v[(u, hh, kt)])
                                psc = (self.pA if n % 2 == 0 else self.pB) \
                                    if hh == 0 else (
                                    self.gram[:, 0:512] if n % 2 == 0
                                    else self.gram[:, 512:1024])
                                self.v1.inc(v.tensor_copy(ebuf(u, hh, kt),
                                                          psc[:, :]))
                        n += 1
                    for hh in range(2):
                        for j in range(4):
                            kt = 4 * st + j
                            if (u, hh, kt) in s1v:
                                v.wait_ge(self.s1.sem, s1v[(u, hh, kt)])
                            else:
                                v.wait_ge(self.v1.sem, self.v1.n)
                            self.v1.inc(v.tensor_mul(ebuf(u, hh, kt),
                                                     ebuf(u, hh, kt),
                                                     c["trimask"][:, j, :]))
                    if u >= 1:
                        post(u - 1)
                for u in (NU - 1,):
                    post(u)

    def _wo_h_xp(self, xin, xout):
        nc, c = self.nc, self.c
        a1v = self.a1[:, :, :].rearrange("p a (cc b) -> p (a cc) b", b=1024)
        pAb = self.pA[:, :].bitcast(BF16)
        pBb = self.pB[:, :].bitcast(BF16)
        pCb = self.pC[:, :].bitcast(BF16)
        pDb = self.pD[:, :].bitcast(BF16)
        bt, bv = self.t0.n, self.v0.n
        with nc.Block() as blk:
            @blk.tensor
            def _(t):
                for tt in range(NT):
                    for nt in range(NPT):
                        k = tt * NPT + nt
                        pb = (pAb, pBb, pCb, pDb)[k % 4]
                        if k >= 4:
                            t.wait_ge(self.v0.sem, bv + k - 3)
                        self.t0.inc(t.transpose(
                            pb[:, 0:128], a1v[:, tt, 128 * nt:128 * (nt + 1)],
                            c["ident_b"][:, :]))
            @blk.vector
            def _(v):
                for tt in range(NT):
                    for nt in range(NPT):
                        k = tt * NPT + nt
                        v.wait_ge(self.t0.sem, bt + k + 1)
                        pb = (pAb, pBb, pCb, pDb)[k % 4]
                        self.v0.inc(v.tensor_copy(
                            self.a2[:, nt, 128 * tt:128 * (tt + 1)], pb[:, 0:128]))
        with nc.Block() as blk:
            @blk.gpsimd
            def _(g):
                g.dma_start(self.aw[:, :, :],
                            self.fwo.rearrange("(a p) n -> p a n", p=128)) \
                    .then_inc(self.dL.sem, 16)
                self.dL.n += 16
                g.wait_ge(self.dL.sem, self.dL.n)
        xc = self.ag[:, 0, :].rearrange("p (a b) -> p a b", b=512)
        hsc = self.ag[:, 1, 0:512]
        xps = self.ag[:, 1, :].rearrange("p (a b) -> p a b", b=512)
        mus = self.af[:, 0, :]
        rst = self.af[:, 1, :]
        bt, bv = self.t0.n, self.v0.n
        bv1 = self.v1.n
        bs0w = self.s0.n
        ldv = []
        stv = []
        with nc.Block() as blk:
            @blk.sync
            def _(sy):
                for k in range(32):
                    ot, ch = divmod(k, NCH)
                    dp = self.dp[2 + k % 4]
                    if k >= 4:
                        sy.wait_ge(dp.sem, ldv[k - 4])
                        sy.wait_ge(self.v0.sem, bv + 4 * (k - 3))
                    sy.dma_start(xc[:, k % 4, :],
                                 xin[:, ot, 512 * ch:512 * (ch + 1)]) \
                        .then_inc(dp.sem, 16)
                    dp.n += 16
                    ldv.append(dp.n)
            @blk.scalar
            def _(sy):
                bs0 = self.s0.n
                for k in range(32):
                    ot, ch = divmod(k, NCH)
                    dp = self.dp[6 + k % 2]
                    sy.wait_ge(self.v0.sem, bv + 4 * (k + 1))
                    sy.dma_start(xout[:, ot, 512 * ch:512 * (ch + 1)],
                                 xps[:, 1 + k % 2, :]).then_inc(dp.sem, 16)
                    dp.n += 16
                    stv.append(dp.n)
                    sy.wait_ge(self.v1.sem, bv1 + 3 * k + 3)
                    self.s0.inc(sy.copy(self.a3[:, ot,
                                        512 * ch:512 * (ch + 1)], hsc))
            @blk.tensor
            def _(t):
                for k in range(32):
                    ot, ch = divmod(k, NCH)
                    po = self.pE if k % 2 == 0 else self.pF
                    if k >= 2:
                        t.wait_ge(self.v1.sem, bv1 + 3 * (k - 2) + 3)
                    for kt in range(NPT):
                        last = t.matmul(po[:, :],
                                        self.aw[:, kt, 128 * ot:128 * (ot + 1)],
                                        self.a2[:, kt, 512 * ch:512 * (ch + 1)],
                                        start=(kt == 0), stop=(kt == NPT - 1))
                    self.t0.inc(last)
            @blk.vector
            def _(v):
                for k in range(32):
                    ot, ch = divmod(k, NCH)
                    sl = slice(512 * ch, 512 * (ch + 1))
                    po = self.pE if k % 2 == 0 else self.pF
                    v.wait_ge(self.t0.sem, bt + k + 1)
                    v.wait_ge(self.dp[2 + k % 4].sem, ldv[k])
                    if k >= 2:
                        v.wait_ge(self.dp[6 + k % 2].sem, stv[k - 2])
                    if k >= 1:
                        v.wait_ge(self.v0.sem, bv + 4 * k)
                        v.wait_ge(self.s0.sem, bs0w + k)
                    self.v1.inc(v.tensor_sub(hsc, xc[:, k % 4, :], mus[:, sl]))
                    v.wait_ge(self.v1.sem, self.v1.n)
                    self.v1.inc(v.tensor_mul(hsc, hsc, rst[:, sl]))
                    v.wait_ge(self.v1.sem, self.v1.n)
                    self.v1.inc(v.tensor_add(hsc, hsc, po[:, :]))
                    v.wait_ge(self.v1.sem, self.v1.n)
                    last = v.tensor_add(xps[:, 1 + k % 2, :], xc[:, k % 4, :], hsc)
                    self.v0.inc(last, k=4)
        self.xp_done = [(self.dp[6], self.dp[6].n), (self.dp[7], self.dp[7].n)]

    def _make_z2(self):
        nc, c = self.nc, self.c
        wcol = self.asml[:, 146:154]
        with nc.Block() as blk:
            @blk.gpsimd
            def _(g):
                g.dma_start(wcol[:, 0:8],
                            self.fFnw.rearrange("o (a p) -> p (o a)", p=128),
                            allow_slow_non_contiguous=True).then_inc(self.dL.sem, 16)
                self.dL.n += 16
                g.wait_ge(self.dL.sem, self.dL.n)
        a5s = self.a5[:, :, :].rearrange("p a (cc b) -> p (a cc) b", b=512)
        rstd2 = self.af[:, 0, :]
        bs1, bt0, bv1 = self.s1.n, self.t0.n, self.v1.n
        with nc.Block() as blk:
            @blk.scalar
            def _(s):
                for ch in range(NCH):
                    sl = slice(512 * ch, 512 * (ch + 1))
                    for pt in range(NPT):
                        ki = ch * NPT + pt
                        if ki >= 2:
                            s.wait_ge(self.t0.sem, bt0 + ki - 1)
                        self.s1.inc(s.activation(a5s[:, 14 + pt % 2, :],
                                                 self.a3[:, pt, sl], AF.Square))
                    s.wait_ge(self.t0.sem, bt0 + 8 * (ch + 1))
                    self.s1.inc(s.activation(rstd2[:, sl],
                                             (self.pA, self.pB, self.pC,
                                              self.pD)[ch][:, :], AF.Ln,
                                             bias=self.ceps[:, 1:2]))
                    s.wait_ge(self.s1.sem, self.s1.n)
                    self.s1.inc(s.activation(rstd2[:, sl], rstd2[:, sl],
                                             AF.Exp, scale=-0.5))
            @blk.tensor
            def _(t):
                for ch in range(NCH):
                    for pt in range(NPT):
                        t.wait_ge(self.s1.sem, bs1 + ch * 10 + pt + 1)
                        self.t0.inc(t.matmul(
                            (self.pA, self.pB, self.pC, self.pD)[ch][:, :],
                            c["odiv"][:, :], a5s[:, 14 + pt % 2, :],
                            start=(pt == 0), stop=(pt == NPT - 1)))
            @blk.vector
            def _(v):
                for ch in range(NCH):
                    sl = slice(512 * ch, 512 * (ch + 1))
                    v.wait_ge(self.s1.sem, bs1 + ch * 10 + 10)
                    for pt in range(NPT):
                        v.wait_ge(self.v1.sem, self.v1.n)
                        self.v1.inc(v.tensor_mul(a5s[:, pt % 2, :],
                                                 self.a3[:, pt, sl],
                                                 rstd2[:, sl]))
                        v.wait_ge(self.v1.sem, self.v1.n)
                        self.v1.inc(v.tensor_scalar(self.a1[:, pt, sl],
                                                    a5s[:, pt % 2, :],
                                                    wcol[:, pt:pt + 1], None,
                                                    ALU.mult))

    def _moe(self, xout):
        nc, c = self.nc, self.c
        awf = self.aw[:, :, :].rearrange("p a b -> p (a b)").bitcast(F32)
        combT = awf[0:8, 0:2048]
        logit = awf[0:8, 2048:4096]
        probs = self.asml[:, 64:192].rearrange("p (a b) -> p a b", b=8)
        pmax = self.asml[:, 32:48]
        psum_c = self.asml[:, 48:64]
        prec = self.asml[:, 0:16]
        max1 = self.asml[:, 16:32]
        agf = self.ag[:, :, :].rearrange("p a b -> p (a b)")
        eq1 = agf[:, 0:128].rearrange("p (a b) -> p a b", b=8)
        pr2 = agf[:, 128:256].rearrange("p (a b) -> p a b", b=8)
        eq2 = agf[:, 256:384].rearrange("p (a b) -> p a b", b=8)
        tmpq = agf[:, 384:512].rearrange("p (a b) -> p a b", b=8)
        comb = agf[:, 512:640].rearrange("p (a b) -> p a b", b=8)
        max2 = self.asml[:, 154:170]
        ssm_ = self.asml[:, 170:186]
        with nc.Block() as blk:
            @blk.gpsimd
            def _(g):
                g.dma_start(self.aw[:, :, 0:8],
                            self.frw.rearrange("(a p) n -> p a n", p=128)) \
                    .then_inc(self.dL.sem, 16)
                self.dL.n += 16
                g.wait_ge(self.dL.sem, self.dL.n)
        bt = self.t0.n
        with nc.Block() as blk:
            @blk.tensor
            def _(t):
                for ch in range(NCH):
                    p = (self.pA, self.pB, self.pC, self.pD)[ch]
                    for kt in range(NPT):
                        last = t.matmul(p[0:8, :], self.aw[:, kt, 0:8],
                                        self.a1[:, kt, 512 * ch:512 * (ch + 1)],
                                        start=(kt == 0), stop=(kt == NPT - 1))
                    self.t0.inc(last)
            @blk.scalar
            def _(s):
                for ch in range(NCH):
                    s.wait_ge(self.t0.sem, bt + NCH)
                    p = (self.pA, self.pB, self.pC, self.pD)[ch]
                    s.copy(logit[:, 512 * ch:512 * (ch + 1)], p[0:8, :])
        bt, bv = self.t0.n, self.v0.n
        with nc.Block() as blk:
            @blk.tensor
            def _(t):
                for tt in range(NT):
                    p = self.pE if tt % 2 == 0 else self.pF
                    if tt >= 2:
                        t.wait_ge(self.v0.sem, bv + tt - 1)
                    self.t0.inc(t.transpose(p[:, 0:8],
                                            logit[:, 128 * tt:128 * (tt + 1)],
                                            c["ident_f"][0:8, 0:8]))
            @blk.vector
            def _(v):
                for tt in range(NT):
                    v.wait_ge(self.t0.sem, bt + tt + 1)
                    p = self.pE if tt % 2 == 0 else self.pF
                    self.v0.inc(v.tensor_copy(probs[:, tt, :], p[:, 0:8]))
        prb = probs[:, :, :]
        bP = lambda t16: t16.rearrange("p (a b) -> p a b", b=1).broadcast_to(
            (128, 16, 8))
        bv1, bs1 = self.v1.n, self.s1.n

        def vch(v, inst):
            self.v1.inc(inst)
            v.wait_ge(self.v1.sem, self.v1.n)
        with nc.Block() as blk:
            @blk.vector
            def _(v):
                vch(v, v.reduce_max(pmax[:, 0:16].rearrange("p (a b) -> p a b", b=1),
                                    prb, axis=AX.X))
                vch(v, v.tensor_sub(eq1[:, :, :], prb, bP(pmax[:, 0:16])))
                v.wait_ge(self.s1.sem, bs1 + 1)
                vch(v, v.reduce_sum(psum_c[:, 0:16].rearrange("p (a b) -> p a b", b=1),
                                    prb, axis=AX.X))
                vch(v, v.reciprocal(prec[:, 0:16], psum_c[:, 0:16]))
                vch(v, v.tensor_mul(prb, prb, bP(prec[:, 0:16])))
                vch(v, v.reduce_max(max1[:, 0:16].rearrange("p (a b) -> p a b", b=1),
                                    prb, axis=AX.X))
                vch(v, v.tensor_tensor(eq1[:, :, :], prb, bP(max1[:, 0:16]),
                                       ALU.is_equal))
                vch(v, v.scalar_tensor_tensor(pr2[:, :, :], eq1[:, :, :], -1.0,
                                              prb, ALU.mult, ALU.mult))
                vch(v, v.tensor_add(pr2[:, :, :], pr2[:, :, :], prb))
                vch(v, v.reduce_max(max2[:, 0:16].rearrange("p (a b) -> p a b", b=1),
                                    pr2, axis=AX.X))
                vch(v, v.tensor_tensor(eq2[:, :, :], pr2, bP(max2[:, 0:16]),
                                       ALU.is_equal))
                vch(v, v.tensor_add(ssm_[:, 0:16], max1[:, 0:16], max2[:, 0:16]))
                vch(v, v.reciprocal(ssm_[:, 0:16], ssm_[:, 0:16]))
                vch(v, v.tensor_mul(max1[:, 0:16], max1[:, 0:16], ssm_[:, 0:16]))
                vch(v, v.tensor_mul(max2[:, 0:16], max2[:, 0:16], ssm_[:, 0:16]))
                vch(v, v.tensor_mul(comb[:, :, :], eq1[:, :, :], bP(max1[:, 0:16])))
                vch(v, v.tensor_mul(eq2[:, :, :], eq2[:, :, :], bP(max2[:, 0:16])))
                vch(v, v.tensor_add(comb[:, :, :], comb[:, :, :], eq2[:, :, :]))
            @blk.scalar
            def _(s):
                s.wait_ge(self.v1.sem, bv1 + 2)
                self.s1.inc(s.activation(prb, eq1[:, :, :], AF.Exp))
        bt, bv = self.t0.n, self.v0.n
        with nc.Block() as blk:
            @blk.tensor
            def _(t):
                for tt in range(NT):
                    p = self.pE if tt % 2 == 0 else self.pF
                    if tt >= 2:
                        t.wait_ge(self.v0.sem, bv + tt - 1)
                    self.t0.inc(t.transpose(p[0:8, 0:128], comb[:, tt, :],
                                            c["ident_f"][:, :]))
            @blk.vector
            def _(v):
                for tt in range(NT):
                    v.wait_ge(self.t0.sem, bt + tt + 1)
                    p = self.pE if tt % 2 == 0 else self.pF
                    self.v0.inc(v.tensor_copy(combT[:, 128 * tt:128 * (tt + 1)],
                                              p[0:8, 0:128]))
        pG = self.gram[:, 0:512]
        pH = self.gram[:, 512:1024]
        a4flat = self.a4flat
        glo = self.a5[:, :, :].rearrange("p a (cc b) -> p (a cc) b", b=1024)
        ghi = a4flat[:, 0:8192].rearrange("p (a b) -> p a b", b=1024)
        g_of = lambda it: glo[:, it, :] if it < 8 else ghi[:, it - 8, :]
        ze = a4flat[:, 8192:16384].rearrange("p (a b) -> p a b", b=1024)
        affl = self.af[:, :, :].rearrange("p a b -> p (a b)")
        combB = lambda c2: self.af[:, 0, 1024 * c2:1024 * (c2 + 1)]
        silu_s = affl[:, 2048:3072].bitcast(BF16).rearrange(
            "p (a b) -> p a b", b=512)
        yev = self.af[:, 1, :].rearrange("p (a b) -> p a b", b=512)
        w2s = agf[:, 1024:4096].bitcast(BF16).rearrange(
            "p (a b) -> p a b", b=2048)
        w2v = lambda r: w2s[:, r, :].rearrange("p (a b) -> p a b", b=128)
        ldw = []
        with nc.Block() as blk:
            @blk.gpsimd
            def _(g):
                for dst, wsrc in ((self.a2, self.fw1), (self.a3, self.fw3)):
                    g.dma_start(dst[:, :, :],
                                wsrc[0].rearrange("(a p) i -> p a i", p=128)) \
                        .then_inc(self.dL.sem, 16)
                    self.dL.n += 16
                ldw.append(self.dL.n)
        accv = []
        w2ldv = []
        w2t0 = []
        w1tv = {}
        w3tv = {}
        NE = 8
        T0B, T1B, T2B = self.t0.n, self.t1.n, self.t2.n
        S1B, V1B, V2B = self.s1.n, self.v1.n, self.v2.n
        t2_start = lambda i: T2B + 2 + 18 * i
        t2_ye = lambda i, ka: t2_start(i) + (
            ka + 1 if (ka < 2 or i == 15) else ka + 3)
        t2_cb = lambda j: t2_start(j - 1) + 4
        sw_acc = (self.sw[self.swbase + 3], self.sw[self.swbase + 4])
        wtp = ([self.sw[5 + k] for k in range(4)] if self.swbase == 0
               else [self.sw[k] for k in range(4)])
        wtsem = lambda it, j: wtp[(2 * it + j) % 4]
        with nc.Block() as blk:
            @blk.gpsimd
            def _(g):
                for e in range(NE):
                    for c2 in range(2):
                        i = e * 2 + c2
                        w2base = len(w2ldv)
                        if i == 0:
                            for ctr, val in self.xp_done:
                                g.wait_ge(ctr.sem, val)

                        def issue_load(ot, e=e, i=i, w2base=w2base):
                            idx = w2base + ot
                            dpw = self.sw[self.swbase + idx % 3]
                            if idx >= 3:
                                g.wait_ge(dpw.sem, w2ldv[idx - 3])
                                g.wait_ge(self.t2.sem, w2t0[idx - 3])
                            g.dma_start(
                                w2s[:, idx % 3, :].rearrange(
                                    "p (a b) -> p a b", b=128),
                                self.fw2[e].rearrange("(a p) o -> p a o", p=128)
                                [:, :, 128 * ot:128 * (ot + 1)])                                 .then_inc(dpw.sem, 16)
                            dpw.n += 16
                            w2ldv.append(dpw.n)
                            w2t0.append(t2_ye(i, 2 * ot + 1))

                        for ot in range(3):
                            issue_load(ot)
                        if c2 == 1 and e < NE - 1:
                            for it in range(16):
                                g.wait_ge(self.t0.sem, T0B + 32 * i + 2 * it + 2)
                                g.wait_ge(self.t1.sem, T1B + 32 * i + 2 * it + 2)
                                for j, (dst, wsrc, rec) in enumerate((
                                        (self.a2, self.fw1, w1tv),
                                        (self.a3, self.fw3, w3tv))):
                                    ctr = wtsem(it, j)
                                    g.wait_ge(ctr.sem, ctr.n)
                                    g.dma_start(
                                        dst[:, :, 128 * it:128 * (it + 1)],
                                        wsrc[e + 1].rearrange(
                                            "(a p) i -> p a i", p=128)
                                        [:, :, 128 * it:128 * (it + 1)]) \
                                        .then_inc(ctr.sem, 16)
                                    ctr.n += 16
                                    rec[(e + 1, it)] = ctr.n
                        for ot in range(NPT):
                            for hf in range(2):
                                ka = ot * 2 + hf
                                dpa = sw_acc[ka % 2]
                                gi = 16 * i + ka
                                if gi >= 2:
                                    g.wait_ge(dpa.sem, accv[gi - 2])
                                g.wait_ge(self.s1.sem,
                                          S1B + 48 * i + 32 + ka + 1)
                                g.dma_start(
                                    xout[:, ot, 1024 * c2 + 512 * hf:
                                         1024 * c2 + 512 * (hf + 1)],
                                    yev[:, 2 + ka % 2, :],
                                    accum_op=ALU.add).then_inc(dpa.sem, 16)
                                dpa.n += 16
                                accv.append(dpa.n)
                            if ot + 3 < NPT:
                                issue_load(ot + 3)

            @blk.tensor
            def _(t):
                for e in range(NE):
                    for c2 in range(2):
                        i = e * 2 + c2
                        w2base = 8 * i
                        if i == 0:
                            t.wait_ge(self.dL.sem, ldw[0])
                            for hf in range(2):
                                self.t2.inc(t.matmul(
                                    (self.pC if hf == 0 else self.pD)[:, :],
                                    c["esel"][:, 0:128],
                                    combT[:, 512 * hf:512 * (hf + 1)],
                                    start=True, stop=True))
                        for it in range(16):
                            for hf in range(2):
                                ki = it * 2 + hf
                                p1 = (self.pA, self.pB, self.pC, self.pD)[ki % 4]
                                p3 = (self.pE, self.pF, pG, pH)[ki % 4]
                                if c2 == 0 and e > 0 and hf == 0:
                                    t.wait_ge(wtsem(it, 0).sem, w1tv[(e, it)])
                                if ki == 0 and i >= 1:
                                    t.wait_ge(self.s1.sem, S1B + 48 * i - 1)
                                if ki == 1 and i >= 1:
                                    t.wait_ge(self.s1.sem, S1B + 48 * i)
                                if ki == 2:
                                    t.wait_ge(self.v2.sem, V2B + 10 * i + 2)
                                if ki >= 4:
                                    t.wait_ge(self.s1.sem,
                                              S1B + 48 * i + ki - 3)
                                for kt in range(NPT):
                                    lastA = t.matmul(
                                        p1[:, :],
                                        self.a2[:, kt, 128 * it:128 * (it + 1)],
                                        self.a1[:, kt, 1024 * c2 + 512 * hf:
                                                1024 * c2 + 512 * (hf + 1)],
                                        start=(kt == 0), stop=(kt == NPT - 1))
                                self.t0.inc(lastA)
                                if c2 == 0 and e > 0 and hf == 0:
                                    t.wait_ge(wtsem(it, 1).sem, w3tv[(e, it)])
                                if ki == 0:
                                    t.wait_ge(self.v2.sem, V2B + 10 * (i + 1))
                                if ki >= 4 or i >= 1:
                                    t.wait_ge(self.v1.sem,
                                              max(0, V1B + 32 * i + ki - 3))
                                for kt in range(NPT):
                                    lastB = t.matmul(
                                        p3[:, :],
                                        self.a3[:, kt, 128 * it:128 * (it + 1)],
                                        ze[:, kt, 512 * hf:512 * (hf + 1)],
                                        start=(kt == 0), stop=(kt == NPT - 1))
                                self.t1.inc(lastB)
                        t.wait_ge(self.s1.sem, S1B + 48 * i + 32)
                        t.wait_ge(self.v1.sem, V1B + 32 * i + 32)
                        for ot in range(NPT):
                            t.wait_ge(self.sw[self.swbase + (w2base + ot) % 3].sem,
                                      w2ldv[w2base + ot])
                            for hf in range(2):
                                ka = ot * 2 + hf
                                po = self.pA if ka % 2 == 0 else self.pB
                                if ka >= 2:
                                    t.wait_ge(self.s1.sem,
                                              S1B + 48 * i + 32 + ka - 1)
                                for it in range(16):
                                    last = t.matmul(
                                        po[:, :],
                                        w2v((w2base + ot) % 3)[:, it, :],
                                        g_of(it)[:, 512 * hf:512 * (hf + 1)],
                                        start=(it == 0), stop=(it == 15))
                                self.t2.inc(last)
                            if ot == 0 and i < 15:
                                c2n = (i + 1) % 2
                                en = (i + 1) // 2
                                for hf in range(2):
                                    self.t2.inc(t.matmul(
                                        (self.pC if hf == 0 else self.pD)[:, :],
                                        c["esel"][:, 128 * en:128 * (en + 1)],
                                        combT[:, 1024 * c2n + 512 * hf:
                                              1024 * c2n + 512 * (hf + 1)],
                                        start=True, stop=True))

            @blk.scalar
            def _(s):
                for e in range(NE):
                    for c2 in range(2):
                        i = e * 2 + c2
                        for it in range(16):
                            for hf in range(2):
                                ki = it * 2 + hf
                                p1 = (self.pA, self.pB, self.pC, self.pD)[ki % 4]
                                s.wait_ge(self.t0.sem, T0B + 32 * i + ki + 1)
                                s.wait_ge(self.v1.sem,
                                          max(0, V1B + 32 * i + ki - 3))
                                self.s1.inc(s.activation(silu_s[:, ki % 4, :],
                                                         p1[:, :], AF.Silu))
                        for ka in range(16):
                            po = self.pA if ka % 2 == 0 else self.pB
                            s.wait_ge(self.t2.sem, t2_ye(i, ka))
                            gi = 16 * i + ka
                            if gi >= 2:
                                s.wait_ge(sw_acc[ka % 2].sem, accv[gi - 2])
                            self.s1.inc(s.copy(yev[:, 2 + ka % 2, :], po[:, :]))

            @blk.vector
            def _(v):
                def prep(j):
                    c2j = j % 2
                    v.wait_ge(self.t2.sem, T2B + 2 if j == 0 else t2_cb(j))
                    self.v2.inc(v.tensor_copy(combB(c2j)[:, 0:512],
                                              self.pC[:, :]))
                    self.v2.inc(v.tensor_copy(combB(c2j)[:, 512:1024],
                                              self.pD[:, :]))
                    v.wait_ge(self.v2.sem, V2B + 10 * j + 2)
                    for kt in range(NPT):
                        self.v2.inc(v.tensor_mul(
                            ze[:, kt, :],
                            self.a1[:, kt, 1024 * c2j:1024 * (c2j + 1)],
                            combB(c2j)))
                prep(0)
                for e in range(NE):
                    for c2 in range(2):
                        i = e * 2 + c2
                        if i >= 1:
                            v.wait_ge(self.t2.sem, t2_start(i))
                        for it in range(16):
                            for hf in range(2):
                                ki = it * 2 + hf
                                p3 = (self.pE, self.pF, pG, pH)[ki % 4]
                                v.wait_ge(self.t1.sem, T1B + 32 * i + ki + 1)
                                v.wait_ge(self.s1.sem, S1B + 48 * i + ki + 1)
                                self.v1.inc(v.tensor_mul(
                                    g_of(it)[:, 512 * hf:512 * (hf + 1)],
                                    silu_s[:, ki % 4, :], p3[:, :]))
                        if i < 15:
                            prep(i + 1)
        ret = [(self.sw[self.swbase + 3], self.sw[self.swbase + 3].n),
               (self.sw[self.swbase + 4], self.sw[self.swbase + 4].n)]
        self.swbase += 5
        return ret

    def _ln_plain_to(self, src, dst):
        nc = self.nc
        xc = self.ag[:, :, :].rearrange("p a (cc b) -> p (a cc) b", b=512)
        a5s = self.a5[:, :, :].rearrange("p a (cc b) -> p (a cc) b", b=512)
        mus = self.af[:, 0, :]
        rst = self.af[:, 1, :]
        for ch in range(NCH):
            sl = slice(512 * ch, 512 * (ch + 1))
            with nc.Block() as blk:
                @blk.sync
                def _(sy):
                    dp = self.dp[0]
                    sy.dma_start(xc[:, 0:8, :], src[:, :, sl]) \
                        .then_inc(dp.sem, 16)
                    dp.n += 16
                tot = self.dp[0].n
                @blk.vector
                def _(v):
                    v.wait_ge(self.dp[0].sem, tot)
                    for pt in range(NPT):
                        v.wait_ge(self.v1.sem, self.v1.n)
                        self.v1.inc(v.tensor_sub(a5s[:, pt, :], xc[:, pt, :],
                                                 mus[:, sl]))
                        v.wait_ge(self.v1.sem, self.v1.n)
                        self.v1.inc(v.tensor_mul(dst[:, pt, sl], a5s[:, pt, :],
                                                 rst[:, sl]))

    def attn_b(self, xin, xin_done, xout):
        nc, c = self.nc, self.c
        a4flat = self.a4flat
        wldA = a4flat[:, 0:8192].rearrange("p (a b) -> p a b", b=1024)
        wldB = a4flat[:, 8192:16384].rearrange("p (a b) -> p a b", b=1024)
        wldC = self.af[:, :, :].rearrange("p a b -> p (a b)").bitcast(BF16) \
            .rearrange("p (a b) -> p a b", b=1024)
        ldrec = {}
        wsem = {"wq": self.sw[10], "wk": self.sw[11],
                "wv": self.sw[12], "wo": self.sw[11]}
        with nc.Block() as blk:
            @blk.gpsimd
            def _(g):
                for nm, wsrc, buf in (("wq", self.bWq, wldA),
                                      ("wk", self.bWk, wldB)):
                    ctr = wsem[nm]
                    g.dma_start(buf[:, :, :],
                                wsrc.rearrange("(a p) d -> p a d", p=128)) \
                        .then_inc(ctr.sem, 16)
                    ctr.n += 16
                    ldrec[nm] = (ctr, ctr.n)
        self._front_fused(xin, xin_done, None, self.a1, plain=True)

        def transpose_w(buf, extra_load=None):
            pAb = self.pA[:, :].bitcast(BF16)
            pBb = self.pB[:, :].bitcast(BF16)
            bt, bv = self.t0.n, self.v0.n
            with nc.Block() as blk:
                if extra_load is not None:
                    nm, wsrc, dbuf = extra_load
                    @blk.gpsimd
                    def _(g):
                        ctr = wsem[nm]
                        g.dma_start(dbuf[:, :, :],
                                    wsrc.rearrange("(a p) d -> p a d", p=128)) \
                            .then_inc(ctr.sem, 16)
                        ctr.n += 16
                        ldrec[nm] = (ctr, ctr.n)
                @blk.tensor
                def _(t):
                    for k in range(64):
                        a, b = divmod(k, NPT)
                        pb = pAb if k % 2 == 0 else pBb
                        if k >= 2:
                            t.wait_ge(self.v0.sem, bv + k - 1)
                        self.t0.inc(t.transpose(
                            pb[:, 0:128], buf[:, a, 128 * b:128 * (b + 1)],
                            c["ident_b"][:, :]))
                @blk.vector
                def _(v):
                    for k in range(64):
                        a, b = divmod(k, NPT)
                        v.wait_ge(self.t0.sem, bt + k + 1)
                        pb = pAb if k % 2 == 0 else pBb
                        self.v0.inc(v.tensor_copy(
                            self.aw[:, b, 128 * a:128 * (a + 1)], pb[:, 0:128]))

        def wait_w(nm):
            with nc.Block() as blk:
                @blk.gpsimd
                def _(g):
                    ctr, val = ldrec[nm]
                    g.wait_ge(ctr.sem, val)

        wait_w("wq")
        transpose_w(wldA)
        self._proj_plain(self.a1, self.a2, scale_stream=None)
        with nc.Block() as blk:
            @blk.gpsimd
            def _(g):
                ctr = wsem["wv"]
                g.dma_start(wldA[:, :, :],
                            self.bWv.rearrange("(a p) d -> p a d", p=128)) \
                    .then_inc(ctr.sem, 16)
                ctr.n += 16
                ldrec["wv"] = (ctr, ctr.n)
        wait_w("wk")
        transpose_w(wldB, extra_load=("wo", self.bWo, wldC))
        self._proj_plain(self.a1, self.a3, scale_stream=self.cst["gneg"])
        wait_w("wv")
        transpose_w(wldA)
        self._v_proj_tm(self.a1)
        wait_w("wo")
        transpose_w(wldC)
        self._attention(retention=True)
        a1v = self.a1[:, :, :].rearrange("p a (cc b) -> p (a cc) b", b=1024)
        pAb = self.pA[:, :].bitcast(BF16)
        pBb = self.pB[:, :].bitcast(BF16)
        pCb = self.pC[:, :].bitcast(BF16)
        pDb = self.pD[:, :].bitcast(BF16)
        bt, bv = self.t0.n, self.v0.n
        with nc.Block() as blk:
            @blk.tensor
            def _(t):
                for tt in range(NT):
                    for nt in range(NPT):
                        k = tt * NPT + nt
                        pb = (pAb, pBb, pCb, pDb)[k % 4]
                        if k >= 4:
                            t.wait_ge(self.v0.sem, bv + k - 3)
                        self.t0.inc(t.transpose(
                            pb[:, 0:128], a1v[:, tt, 128 * nt:128 * (nt + 1)],
                            c["ident_b"][:, :]))
            @blk.vector
            def _(v):
                for tt in range(NT):
                    for nt in range(NPT):
                        k = tt * NPT + nt
                        v.wait_ge(self.t0.sem, bt + k + 1)
                        pb = (pAb, pBb, pCb, pDb)[k % 4]
                        self.v0.inc(v.tensor_copy(
                            self.a2[:, nt, 128 * tt:128 * (tt + 1)], pb[:, 0:128]))
        s1c = self.asml[:, 0:8]
        s2c = self.asml[:, 8:16]
        A2 = self.atin
        gwb = self.asml[:, 32:48]
        bt, bv, bs = self.t0.n, self.v1.n, self.s1.n
        self.dL.n += 32
        gdone = self.dL.n
        with nc.Block() as blk:
            @blk.gpsimd
            def _(g):
                g.dma_start(gwb[:, 0:8],
                            self.bGw.rearrange("o (a p) -> p (o a)", p=128),
                            allow_slow_non_contiguous=True).then_inc(self.dL.sem, 16)
                g.dma_start(gwb[:, 8:16],
                            self.bGb.rearrange("o (a p) -> p (o a)", p=128),
                            allow_slow_non_contiguous=True).then_inc(self.dL.sem, 16)
            @blk.vector
            def _(v):
                for pt in range(NPT):
                    self.v1.inc(v.reduce_sum(s1c[:, pt:pt + 1],
                                             self.a2[:, pt, :], axis=AX.X))
                v.wait_ge(self.t0.sem, bt + 1)
                self.v1.inc(v.tensor_scalar(A2[0:2, 0:16], self.pA[0:2, 0:16],
                                            1.0 / (64.0 * float(S)), None,
                                            ALU.mult))
                v.wait_ge(self.v1.sem, bv + 9)
                self.v1.inc(v.tensor_mul(A2[0:2, 16:24], A2[0:2, 0:8],
                                         A2[0:2, 0:8]))
                v.wait_ge(self.v1.sem, bv + 10)
                self.v1.inc(v.tensor_sub(A2[0:2, 16:24], A2[0:2, 8:16],
                                         A2[0:2, 16:24]))
                v.wait_ge(self.v1.sem, bv + 11)
                v.wait_ge(self.s1.sem, bs + 10)
                self.v1.inc(v.scalar_tensor_tensor(A2[0:2, 24:32], A2[0:2, 0:8],
                                                   -1.0, A2[0:2, 16:24],
                                                   ALU.mult, ALU.mult))
                v.wait_ge(self.t0.sem, bt + 2)
                self.v1.inc(v.tensor_copy(self.asml[:, 16:32], self.pB[:, 0:16]))
                v.wait_ge(self.v1.sem, bv + 13)
                v.wait_ge(self.dL.sem, gdone)
                self.v1.inc(v.tensor_mul(self.asml[:, 48:56],
                                         self.asml[:, 16:24], gwb[:, 0:8]))
                v.wait_ge(self.v1.sem, bv + 14)
                self.v1.inc(v.scalar_tensor_tensor(self.asml[:, 56:64],
                                                   self.asml[:, 24:32], 1.0,
                                                   gwb[:, 0:8], ALU.mult,
                                                   ALU.mult))
                v.wait_ge(self.v1.sem, bv + 15)
                self.v1.inc(v.tensor_add(self.asml[:, 56:64],
                                         self.asml[:, 56:64], gwb[:, 8:16]))
            @blk.scalar
            def _(s):
                for pt in range(NPT):
                    s.wait_ge(self.s1.sem, self.s1.n)
                    self.s1.inc(s.activation(self.a5[:, 0, 0:S],
                                             self.a2[:, pt, :], AF.Square,
                                             accum_out=s2c[:, pt:pt + 1]))
                s.wait_ge(self.v1.sem, bv + 11)
                s.wait_ge(self.s1.sem, bs + 8)
                self.s1.inc(s.activation(A2[0:2, 16:24], A2[0:2, 16:24], AF.Ln,
                                         bias=self.ceps[0:2, 0:1]))
                s.wait_ge(self.s1.sem, bs + 9)
                self.s1.inc(s.activation(A2[0:2, 16:24], A2[0:2, 16:24], AF.Exp,
                                         scale=-0.5))
                s.wait_ge(self.v1.sem, bv + 16)
                for pt in range(NPT):
                    for ch in range(NCH):
                        sl = slice(512 * ch, 512 * (ch + 1))
                        s.wait_ge(self.s1.sem, self.s1.n)
                        self.s1.inc(s.activation(
                            self.a1[:, pt, sl], self.a2[:, pt, sl], AF.Identity,
                            scale=self.asml[:, 48 + pt:49 + pt],
                            bias=self.asml[:, 56 + pt:57 + pt]))
            @blk.tensor
            def _(t):
                t.wait_ge(self.v1.sem, bv + 8)
                t.wait_ge(self.s1.sem, bs + 8)
                for pt in range(NPT):
                    t.matmul(self.pA[0:2, pt:pt + 1], c["gsel"][:, :],
                             s1c[:, pt:pt + 1], start=True, stop=True)
                    last = t.matmul(self.pA[0:2, 8 + pt:9 + pt], c["gsel"][:, :],
                                    s2c[:, pt:pt + 1], start=True, stop=True)
                self.t0.inc(last)
                t.wait_ge(self.s1.sem, bs + 10)
                t.wait_ge(self.v1.sem, bv + 12)
                for pt in range(NPT):
                    t.matmul(self.pB[:, pt:pt + 1], c["grepl"][0:2, :],
                             A2[0:2, 16 + pt:17 + pt], start=True, stop=True)
                    last = t.matmul(self.pB[:, 8 + pt:9 + pt], c["grepl"][0:2, :],
                                    A2[0:2, 24 + pt:25 + pt], start=True, stop=True)
                self.t0.inc(last)
        xc = self.ag[:, 0, :].rearrange("p (a b) -> p a b", b=512)
        och = self.ag[:, 1, :].rearrange("p (a b) -> p a b", b=512)
        bt, bv = self.t0.n, self.v0.n
        ldv = []
        stv = []
        with nc.Block() as blk:
            @blk.sync
            def _(sy):
                for k in range(32):
                    ot, ch = divmod(k, NCH)
                    dp = self.dp[2 + k % 4]
                    if k >= 4:
                        sy.wait_ge(dp.sem, ldv[k - 4])
                        sy.wait_ge(self.v0.sem, bv + k - 3)
                    sy.dma_start(xc[:, k % 4, :],
                                 xin[:, ot, 512 * ch:512 * (ch + 1)]) \
                        .then_inc(dp.sem, 16)
                    dp.n += 16
                    ldv.append(dp.n)
            @blk.scalar
            def _(sy):
                for k in range(32):
                    ot, ch = divmod(k, NCH)
                    dp = self.dp[6 + k % 2]
                    sy.wait_ge(self.v0.sem, bv + k + 1)
                    sy.dma_start(xout[:, ot, 512 * ch:512 * (ch + 1)],
                                 och[:, k % 2, :]).then_inc(dp.sem, 16)
                    dp.n += 16
                    stv.append(dp.n)
            @blk.tensor
            def _(t):
                for k in range(32):
                    ot, ch = divmod(k, NCH)
                    po = self.pE if k % 2 == 0 else self.pF
                    if k >= 2:
                        t.wait_ge(self.v0.sem, bv + k - 1)
                    for kt in range(NPT):
                        last = t.matmul(po[:, :],
                                        self.aw[:, kt, 128 * ot:128 * (ot + 1)],
                                        self.a1[:, kt, 512 * ch:512 * (ch + 1)],
                                        start=(kt == 0), stop=(kt == NPT - 1))
                    self.t0.inc(last)
            @blk.vector
            def _(v):
                for k in range(32):
                    po = self.pE if k % 2 == 0 else self.pF
                    v.wait_ge(self.t0.sem, bt + k + 1)
                    v.wait_ge(self.dp[2 + k % 4].sem, ldv[k])
                    if k >= 2:
                        v.wait_ge(self.dp[6 + k % 2].sem, stv[k - 2])
                    self.v0.inc(v.tensor_add(och[:, k % 2, :], xc[:, k % 4, :],
                                             po[:, :]))
        return [(self.dp[6], self.dp[6].n), (self.dp[7], self.dp[7].n)]

    def _proj_plain(self, zin, dst, scale_stream):
        nc, c = self.nc, self.c
        gst = self.ag[:, :, :]
        bt0, bs1, bv1 = self.t0.n, self.s1.n, self.v1.n
        ldv = []
        with nc.Block() as blk:
            if scale_stream is not None:
                @blk.sync
                def _(sy):
                    for nt in range(NPT):
                        dp = self.dp[0 if nt % 2 == 0 else 1]
                        if nt >= 2:
                            sy.wait_ge(dp.sem, ldv[nt - 2])
                            sy.wait_ge(self.v1.sem, bv1 + 4 * (nt - 1))
                        sy.dma_start(gst[:, nt % 2, :],
                                     scale_stream[128 * nt:128 * (nt + 1), :]) \
                            .then_inc(dp.sem, 16)
                        dp.n += 16
                        ldv.append(dp.n)
            @blk.tensor
            def _(t):
                for k in range(32):
                    nt, ch = divmod(k, NCH)
                    po = self.pA if k % 2 == 0 else self.pB
                    if k >= 2:
                        if scale_stream is not None:
                            t.wait_ge(self.v1.sem, bv1 + k - 1)
                        else:
                            t.wait_ge(self.s1.sem, bs1 + k - 1)
                    for kt in range(NPT):
                        last = t.matmul(po[:, :],
                                        self.aw[:, kt, 128 * nt:128 * (nt + 1)],
                                        zin[:, kt, 512 * ch:512 * (ch + 1)],
                                        start=(kt == 0), stop=(kt == NPT - 1))
                    self.t0.inc(last)
            if scale_stream is not None:
                @blk.vector
                def _(v):
                    for k in range(32):
                        nt, ch = divmod(k, NCH)
                        sl = slice(512 * ch, 512 * (ch + 1))
                        po = self.pA if k % 2 == 0 else self.pB
                        v.wait_ge(self.t0.sem, bt0 + k + 1)
                        if ch == 0:
                            v.wait_ge(self.dp[0 if nt % 2 == 0 else 1].sem,
                                      ldv[nt])
                        self.v1.inc(v.tensor_mul(dst[:, nt, sl], po[:, :],
                                                 gst[:, nt % 2, sl]))
            else:
                @blk.scalar
                def _(s):
                    for k in range(32):
                        nt, ch = divmod(k, NCH)
                        sl = slice(512 * ch, 512 * (ch + 1))
                        po = self.pA if k % 2 == 0 else self.pB
                        s.wait_ge(self.t0.sem, bt0 + k + 1)
                        self.s1.inc(s.copy(dst[:, nt, sl], po[:, :]))

    def _v_proj_tm(self, zin):
        nc, c = self.nc, self.c
        a5s = self.a5[:, :, :].rearrange("p a (cc b) -> p (a cc) b", b=512)
        bt0, bs1, bt1, bv1 = self.t0.n, self.s1.n, self.t1.n, self.v1.n
        pAb = self.pA[:, :].bitcast(BF16)
        pBb = self.pB[:, :].bitcast(BF16)
        with nc.Block() as blk:
            @blk.tensor
            def _(t):
                for nt in range(NPT):
                    for ch in range(NCH):
                        k = nt * NCH + ch
                        po = self.pC if k % 2 == 0 else self.pD
                        if k >= 2:
                            t.wait_ge(self.s1.sem, bs1 + k - 1)
                        for kt in range(NPT):
                            last = t.matmul(
                                po[:, :], self.aw[:, kt, 128 * nt:128 * (nt + 1)],
                                zin[:, kt, 512 * ch:512 * (ch + 1)],
                                start=(kt == 0), stop=(kt == NPT - 1))
                        self.t0.inc(last)
                    t.wait_ge(self.s1.sem, bs1 + (nt + 1) * NCH)
                    for tb in range(NT):
                        kk = nt * NT + tb
                        pb = pAb if kk % 2 == 0 else pBb
                        if kk >= 2:
                            t.wait_ge(self.v1.sem, bv1 + 2 * (kk - 1))
                        self.t1.inc(t.transpose(
                            pb[:, 0:128],
                            a5s[:, 8 + tb // 4, 128 * (tb % 4):128 * (tb % 4) + 128],
                            c["ident_b"][:, :]))
            @blk.scalar
            def _(s):
                for nt in range(NPT):
                    for ch in range(NCH):
                        k = nt * NCH + ch
                        po = self.pC if k % 2 == 0 else self.pD
                        s.wait_ge(self.t0.sem, bt0 + k + 1)
                        if nt >= 1:
                            s.wait_ge(self.t1.sem, bt1 + nt * NT)
                        self.s1.inc(s.copy(a5s[:, 8 + ch, :], po[:, :]))
            @blk.vector
            def _(v):
                for tt in range(NT):
                    v.memset(self.a4[:, tt, 64::65], 1.0)
                for nt in range(NPT):
                    for tb in range(NT):
                        kk = nt * NT + tb
                        v.wait_ge(self.t1.sem, bt1 + kk + 1)
                        pb = pAb if kk % 2 == 0 else pBb
                        hA, hB = 2 * nt, 2 * nt + 1
                        self.v1.inc(v.tensor_copy(
                            self.a4[:, tb, 65 * hA:65 * hA + 64], pb[:, 0:64]))
                        self.v1.inc(v.tensor_copy(
                            self.a4[:, tb, 65 * hB:65 * hB + 64], pb[:, 64:128]))

    def finalize_out_from_hx(self, idx, done=None):
        nc, c = self.nc, self.c
        hsrc = self.hxp if idx == 4 else self.hx[idx]
        xch = self.ag[:, 0, :].rearrange("p (a b) -> p a b", b=1024)
        och = self.ag[:, 1, :].rearrange("p (a b) -> p a b", b=1024)
        bt, bv = self.t0.n, self.v0.n
        ldv = []
        stv = []
        with nc.Block() as blk:
            @blk.sync
            def _(sy):
                if done is not None:
                    for ctr, val in done:
                        sy.wait_ge(ctr.sem, val)
                for k in range(16):
                    pt, hh = divmod(k, 2)
                    dp = self.dp[2 + k % 2]
                    if k >= 2:
                        sy.wait_ge(dp.sem, ldv[k - 2])
                        sy.wait_ge(self.t0.sem, bt + 8 * (k - 1))
                    sy.dma_start(xch[:, k % 2, :],
                                 hsrc[:, pt, 1024 * hh:1024 * (hh + 1)]) \
                        .then_inc(dp.sem, 16)
                    dp.n += 16
                    ldv.append(dp.n)
            @blk.scalar
            def _(sy):
                for k in range(16):
                    pt, hh = divmod(k, 2)
                    dp = self.dp[4 + k % 2]
                    sy.wait_ge(self.v0.sem, bv + (k + 1) * 8)
                    sy.dma_start(
                        self.out[1024 * hh:1024 * (hh + 1),
                                 128 * pt:128 * (pt + 1)].rearrange(
                                     "(q p) d -> p q d", p=128),
                        och[:, k % 2, :].rearrange("p (q d) -> p q d", d=128)) \
                        .then_inc(dp.sem, 16)
                    dp.n += 16
                    stv.append(dp.n)
            @blk.tensor
            def _(t):
                for k in range(16):
                    t.wait_ge(self.dp[2 + k % 2].sem, ldv[k])
                    for q in range(8):
                        kk = k * 8 + q
                        p = (self.pA, self.pB, self.pC, self.pD)[kk % 4]
                        if kk >= 4:
                            t.wait_ge(self.v0.sem, bv + kk - 3)
                        self.t0.inc(t.transpose(p[:, 0:128],
                                                xch[:, k % 2, 128 * q:128 * (q + 1)],
                                                c["ident_f"][:, :]))
            @blk.vector
            def _(v):
                for k in range(16):
                    for q in range(8):
                        kk = k * 8 + q
                        v.wait_ge(self.t0.sem, bt + kk + 1)
                        if q == 0 and k >= 2:
                            v.wait_ge(self.dp[4 + k % 2].sem, stv[k - 2])
                        p = (self.pA, self.pB, self.pC, self.pD)[kk % 4]
                        self.v0.inc(v.tensor_copy(
                            och[:, k % 2, 128 * q:128 * (q + 1)], p[:, 0:128]))
        with nc.Block() as blk:
            @blk.sync
            def _(sy):
                sy.wait_ge(self.dp[4].sem, self.dp[4].n)
                sy.wait_ge(self.dp[5].sem, self.dp[5].n)

    def build(self):
        self._load_consts()
        self.prologue()
        self.attn_a()
        if NSUB == 1:
            self.finalize_out_from_hx(1, done=self.hx1_done)
        else:
            done2 = self.ffn(self.hx[1], self.hx1_done, self.hx[2])
            if NSUB == 2:
                self.finalize_out_from_hx(2, done=done2)
            else:
                done3 = self.attn_b(self.hx[2], done2, self.hx[3])
                if NSUB == 3:
                    self.finalize_out_from_hx(3, done=done3)
                else:
                    done4 = self.ffn(self.hx[3], done3, self.hxp)
                    self.finalize_out_from_hx(4, done=done4)
        self.es.close()
        return self.nc


_CACHE = {}


def _get_nc():
    if "nc" not in _CACHE:
        _CACHE["nc"] = MK().build()
    return _CACHE["nc"]


def kernel(**inputs):
    nc = _get_nc()
    hc = _host_consts()
    x = np.ascontiguousarray(np.asarray(inputs["L0_attn_a_x"], dtype=np.float32))
    base = {
        "xall": x,
        "aWq": inputs["L0_attn_a_W_q"], "aWk": inputs["L0_attn_a_W_k"],
        "aWv": inputs["L0_attn_a_W_v"], "aWo": inputs["L0_attn_a_W_o"],
        "aA": inputs["L0_attn_a_SSM_A"], "aB": inputs["L0_attn_a_SSM_B"],
        "aC": inputs["L0_attn_a_SSM_C"],
        "aLg": np.asarray(inputs["L0_attn_a_ln_gamma"]).reshape(1, D),
        "aLb": np.asarray(inputs["L0_attn_a_ln_beta"]).reshape(1, D),
        "bWq": inputs["L0_attn_b_Wq"], "bWk": inputs["L0_attn_b_Wk"],
        "bWv": inputs["L0_attn_b_Wv"], "bWo": inputs["L0_attn_b_Wo"],
        "bGw": np.asarray(inputs["L0_attn_b_gn_w"]).reshape(1, D),
        "bGb": np.asarray(inputs["L0_attn_b_gn_b"]).reshape(1, D),
        "fAnw": np.asarray(inputs["L0_ffn_attn_norm_w"]).reshape(1, D),
        "fFnw": np.asarray(inputs["L0_ffn_ffn_norm_w"]).reshape(1, D),
        "fwq": inputs["L0_ffn_wq"], "fwk": inputs["L0_ffn_wk"],
        "fwv": inputs["L0_ffn_wv"], "fwo": inputs["L0_ffn_wo"],
        "frw": inputs["L0_ffn_router_w"],
        "fw1": inputs["L0_ffn_w1"], "fw3": inputs["L0_ffn_w3"],
        "fw2": inputs["L0_ffn_w2"],
        "ffc": inputs["L0_ffn_freqs_cis"],
    }
    base = {k: np.ascontiguousarray(np.asarray(v, dtype=np.float32))
            for k, v in base.items()}
    for k, v in hc.items():
        base["c_" + k] = v
    in_maps = []
    for b in range(8):
        m = dict(base)
        m["xown"] = base["xall"][b]
        sel = np.zeros((1, 8), dtype=np.float32)
        sel[0, b] = 1.0
        m["bselv"] = sel
        in_maps.append(m)
    res = run_bass_kernel_spmd(nc, in_maps, core_ids=list(range(8)))
    return np.stack([np.asarray(r["out"]) for r in res.results], axis=0)
